# version 01
# baseline (speedup 1.0000x reference)
import numpy as np
import ml_dtypes
import concourse.bass as bass
import concourse.mybir as mybir
from concourse.bass_utils import run_bass_kernel_spmd
from concourse.alu_op_type import AluOpType as ALU
from contextlib import ExitStack

F32 = mybir.dt.float32
BF16 = mybir.dt.bfloat16
AF = mybir.ActivationFunctionType
AXX = mybir.AxisListType.X

D = 2048
S = 4096
NH = 8
NKV = 2
DH = 128
HFF = 5632
NCHK = D // 128
T1 = 256
NT1 = S // T1
HALF = S // 2
LAM = float(1.0 / np.sqrt(128.0))
EPS_LN = 1e-5
EPS_Q = 1e-8
BOUND_INFLATE = 1.02
NHC_FF = HFF // 128

ENGS = ("sync", "gp", "pe", "act", "dve")
QSEM = {"sync": "ld", "gp": "gp"}
ESEM = {"pe": "pe", "act": "act", "dve": "dve"}


class Ctx:
    def __init__(self, nc, debug=False):
        self.nc = nc
        self.debug = debug
        self.streams = {e: [] for e in ENGS}
        self.es = ExitStack()
        self.sem = {}
        self.cnt = {}
        for name in ("pe", "act", "dve", "cc"):
            self._mksem(name)

    def _mksem(self, name):
        if name not in self.sem:
            self.sem[name] = self.es.enter_context(self.nc.semaphore(name))
            self.cnt[name] = 0
        return self.sem[name]

    def E(self, eng, fn):
        self.streams[eng].append(fn)

    def wait(self, eng, sem, val):
        if val is None or val <= 0:
            return
        h = self.sem[sem]
        self.E(eng, lambda e, h=h, v=val: e.wait_ge(h, v))

    def dma(self, qeng, sem, out, in_):
        h = self._mksem(sem)
        self.cnt[sem] += 16
        tgt = self.cnt[sem]
        self.E(qeng, lambda e, o=out, i=in_, h=h: e.dma_start(out=o, in_=i).then_inc(h, 16))
        return tgt

    def op(self, eng, fn, inc=True):
        semname = ESEM[eng]
        if eng in ("act", "dve") and self.cnt[semname] > 0:
            h0 = self.sem[semname]
            v0 = self.cnt[semname]
            self.E(eng, lambda e, h=h0, v=v0: e.wait_ge(h, v))
        if inc:
            self.cnt[semname] += 1
            tgt = self.cnt[semname]
            h = self.sem[semname]
            self.E(eng, lambda e, fn=fn, h=h: fn(e).then_inc(h, 1))
            return tgt
        self.E(eng, lambda e, fn=fn: fn(e))
        return self.cnt[semname]

    def barrier(self):
        counts = dict(self.cnt)
        own_engine_sem = {"pe": "pe", "act": "act", "dve": "dve"}
        for eng in ENGS:
            skip = own_engine_sem.get(eng)
            for s, v in counts.items():
                if s == skip or s == "cc":
                    continue
                self.wait(eng, s, v)

    def emit_block(self):
        nc = self.nc
        streams = self.streams
        with nc.Block() as block:
            @block.sync
            def _(sync):
                for fn in streams["sync"]:
                    fn(sync)

            @block.gpsimd
            def _(gpsimd):
                for fn in streams["gp"]:
                    fn(gpsimd)

            @block.tensor
            def _(tensor):
                for fn in streams["pe"]:
                    fn(tensor)

            @block.scalar
            def _(scalar):
                for fn in streams["act"]:
                    fn(scalar)

            @block.vector
            def _(vector):
                for fn in streams["dve"]:
                    fn(vector)
        self.streams = {e: [] for e in ENGS}


def rearr_cpt(ap2d):
    return ap2d.rearrange("(c p) t -> p c t", p=128)


def split_dma(ctx, qeng, sem, dst3, src3, nsplit=4):
    C = dst3.shape[1]
    step = C // nsplit
    tgt = None
    for i in range(nsplit):
        tgt = ctx.dma(qeng, sem, dst3[:, i * step:(i + 1) * step, :],
                      src3[:, i * step:(i + 1) * step, :])
    return tgt


def build(debug=False, upto=5):
    nc = bass.Bass("TRN2", target_bir_lowering=False)

    xT = nc.dram_tensor("xT", [D, S], F32, kind="ExternalInput").ap()
    xTown = nc.dram_tensor("xTown", [D, HALF], F32, kind="ExternalInput").ap()
    wqT = nc.dram_tensor("wqT", [D, NH * DH], F32, kind="ExternalInput").ap()
    wkT = nc.dram_tensor("wkT", [D, NKV * DH], F32, kind="ExternalInput").ap()
    wvT = nc.dram_tensor("wvT", [D, NKV * DH], F32, kind="ExternalInput").ap()
    woT = nc.dram_tensor("woT", [D, D], F32, kind="ExternalInput").ap()
    w1T = nc.dram_tensor("w1T", [D, HFF], F32, kind="ExternalInput").ap()
    w2T = nc.dram_tensor("w2T", [D, HFF], F32, kind="ExternalInput").ap()
    w3T = nc.dram_tensor("w3T", [HFF, D], F32, kind="ExternalInput").ap()
    costab = nc.dram_tensor("costab", [64, S], F32, kind="ExternalInput").ap()
    sintab = nc.dram_tensor("sintab", [64, S], F32, kind="ExternalInput").ap()
    tri = nc.dram_tensor("tri", [128, 128], BF16, kind="ExternalInput").ap()
    out = nc.dram_tensor("out", [D, HALF], F32, kind="ExternalOutput").ap()

    q_sc = nc.dram_tensor("q_sc", [NH, 128, S], BF16, kind="Internal").ap()
    q1_sc = nc.dram_tensor("q1_sc", [NCHK, 128, HFF], BF16, kind="Internal").ap()
    q2_sc = nc.dram_tensor("q2_sc", [NCHK, 128, HFF], BF16, kind="Internal").ap()
    q3_sc = nc.dram_tensor("q3_sc", [NHC_FF, 128, D], BF16, kind="Internal").ap()
    cin = nc.dram_tensor("cin", [NH * DH, S], BF16, kind="Internal").ap()
    cout_t = nc.dram_tensor("cout", [2 * NH * DH, S], BF16, kind="Internal")
    cout = cout_t.ap()
    y_sc = nc.dram_tensor("y_sc", [NCHK, 128, HALF], F32, kind="Internal").ap()
    xn2_sc = nc.dram_tensor("xn2_sc", [NCHK, 128, HALF], BF16, kind="Internal").ap()

    dbg = {}
    if debug:
        dbg["k_dbg"] = nc.dram_tensor("k_dbg", [NKV * 128, S], BF16, kind="ExternalOutput").ap()
        dbg["v_dbg"] = nc.dram_tensor("v_dbg", [S // 128, 128, NKV * DH], BF16, kind="ExternalOutput").ap()
        dbg["q_dbg"] = nc.dram_tensor("q_dbg", [NH, 128, S], BF16, kind="ExternalOutput").ap()
        dbg["negm_dbg"] = nc.dram_tensor("negm_dbg", [128, 1], F32, kind="ExternalOutput").ap()
        dbg["thr_dbg"] = nc.dram_tensor("thr_dbg", [128, 3], F32, kind="ExternalOutput").ap()
        dbg["ao_dbg"] = nc.dram_tensor("ao_dbg", [2 * NH * DH, S], BF16, kind="ExternalOutput").ap()
        dbg["y_dbg"] = nc.dram_tensor("y_dbg", [NCHK, 128, HALF], F32, kind="ExternalOutput").ap()
        dbg["xn2_dbg"] = nc.dram_tensor("xn2_dbg", [NCHK, 128, HALF], BF16, kind="ExternalOutput").ap()

    ctx = Ctx(nc, debug)
    es = ctx.es

    ones_col_bf = es.enter_context(nc.sbuf_tensor("ones_col_bf", [128, 1], BF16))
    ones_col_f = es.enter_context(nc.sbuf_tensor("ones_col_f", [128, 1], F32))
    ones_row_f = es.enter_context(nc.sbuf_tensor("ones_row_f", [1, 128], F32))
    negm_sb = es.enter_context(nc.sbuf_tensor("negm_sb", [128, 1], F32))
    maxbuf = es.enter_context(nc.sbuf_tensor("maxbuf", [1, 192], F32))
    p0s = ExitStack()
    wt_sb = [p0s.enter_context(nc.sbuf_tensor(f"wt_sb{i}", [128, 512], F32)) for i in range(3)]
    dump_sb = p0s.enter_context(nc.sbuf_tensor("dump_sb", [128, 512], BF16))
    gacc = p0s.enter_context(nc.sbuf_tensor("gacc", [128, 176], F32))
    gcol = p0s.enter_context(nc.sbuf_tensor("gcol", [128, 3], F32))
    thr_sb = p0s.enter_context(nc.sbuf_tensor("thr_sb", [128, 3], F32))
    nthr_sb = p0s.enter_context(nc.sbuf_tensor("nthr_sb", [128, 3], F32))
    bq_sb = p0s.enter_context(nc.sbuf_tensor("bq_sb", [128, 512], BF16))
    qq_sb = [p0s.enter_context(nc.sbuf_tensor(f"qq_sb{i}", [128, 512], BF16)) for i in range(2)]
    kvs = ExitStack()
    k_sb = [kvs.enter_context(nc.sbuf_tensor(f"k_sb{i}", [128, S], BF16)) for i in range(NKV)]
    v_sb = [kvs.enter_context(nc.sbuf_tensor(f"v_sb{i}", [128, NKV * DH], BF16)) for i in range(S // 128)]
    tri_sb = kvs.enter_context(nc.sbuf_tensor("tri_sb", [128, 128], BF16))

    ctx.op("dve", lambda e: e.memset(ones_col_bf[:], 1.0))
    ctx.op("dve", lambda e: e.memset(ones_col_f[:], 1.0))
    ctx.op("dve", lambda e: e.memset(ones_row_f[:], 1.0))
    ctx.op("dve", lambda e: e.memset(maxbuf[:], 0.0))
    ctx.dma("sync", "ldonce", tri_sb[:], tri)

    WMATS = [(w1T, D, HFF, q1_sc), (w2T, D, HFF, q2_sc), (w3T, HFF, D, q3_sc)]
    wt_guard = {}
    qq_guard = {}

    def p0_gamma_one(widx):
        w, R, C, _ = WMATS[widx]
        numel = R * C
        nrt, nct = R // 128, C // 512
        ntile = nrt * nct
        last = 0
        ti = 0
        ctx.wait("act", "dve", ctx.cnt["dve"])
        for r in range(nrt):
            for c in range(nct):
                buf = wt_sb[ti % 3]
                if (ti % 3) in wt_guard:
                    ctx.wait("sync", *wt_guard[ti % 3])
                ld = ctx.dma("sync", f"p0w{ti % 3}", buf[:], w[r * 128:(r + 1) * 128, c * 512:(c + 1) * 512])
                ctx.wait("act", f"p0w{ti % 3}", ld)
                last = ctx.op("act", lambda e, b=buf, col=ti: e.activation(
                    out=dump_sb[:], in_=b[:], func=AF.Abs,
                    accum_out=gacc[:, col:col + 1]))
                wt_guard[ti % 3] = ("act", last)
                ti += 1
        ctx.wait("dve", "act", last)
        rt = ctx.op("dve", lambda e, n=ntile, wi=widx: e.tensor_reduce(
            out=gcol[:, wi:wi + 1], in_=gacc[:, 0:n], axis=AXX, op=ALU.add))
        ctx.wait("pe", "dve", rt)
        with nc.psum_tensor(f"g_ps{widx}", [128, 16], F32) as g_ps:
            pm = ctx.op("pe", lambda e, wi=widx: e.matmul(
                g_ps[0:1, 0:1], ones_col_f[:], gcol[:, wi:wi + 1], start=True, stop=True))
            ctx.wait("dve", "pe", pm)
            t1 = ctx.op("dve", lambda e, wi=widx, n=numel: e.tensor_scalar(
                out=thr_sb[0:1, wi:wi + 1], in0=g_ps[0:1, 0:1],
                scalar1=0.5 / n, scalar2=0.5 * EPS_Q, op0=ALU.mult, op1=ALU.add))
            ctx.wait("pe", "dve", t1)
            pb = ctx.op("pe", lambda e, wi=widx: e.matmul(
                g_ps[:, 1:2], ones_row_f[:], thr_sb[0:1, wi:wi + 1], start=True, stop=True))
            ctx.wait("dve", "pe", pb)
            t2 = ctx.op("dve", lambda e, wi=widx: e.tensor_copy(
                out=thr_sb[:, wi:wi + 1], in_=g_ps[:, 1:2]))
            t3 = ctx.op("dve", lambda e, wi=widx: e.tensor_scalar(
                out=nthr_sb[:, wi:wi + 1], in0=g_ps[:, 1:2],
                scalar1=-1.0, scalar2=0.0, op0=ALU.mult, op1=ALU.bypass))
            ctx.wait("pe", "dve", t3)

    def p0_quant_one(widx):
        w, R, C, tgt = WMATS[widx]
        nrt, nct = R // 128, C // 512
        dve_done = {}
        st_done = {}
        ti = 0
        for r in range(nrt):
            for c in range(nct):
                buf = wt_sb[ti % 3]
                if (ti % 3) in wt_guard:
                    ctx.wait("sync", *wt_guard[ti % 3])
                ld = ctx.dma("sync", f"p0w{ti % 3}", buf[:], w[r * 128:(r + 1) * 128, c * 512:(c + 1) * 512])
                ctx.wait("dve", f"p0w{ti % 3}", ld)
                qb = qq_sb[ti % 2]
                if (ti % 2) in qq_guard:
                    ctx.wait("dve", f"p0s{ti % 2}", qq_guard[ti % 2])
                ctx.op("dve", lambda e, b=buf, wi=widx: e.tensor_scalar(
                    out=bq_sb[:], in0=b[:], scalar1=nthr_sb[:, wi:wi + 1], scalar2=-1.0,
                    op0=ALU.is_gt, op1=ALU.add))
                dve_done[ti] = ctx.op("dve", lambda e, b=buf, wi=widx, qb=qb: e.scalar_tensor_tensor(
                    out=qb[:], in0=b[:], scalar=thr_sb[:, wi:wi + 1], in1=bq_sb[:],
                    op0=ALU.is_ge, op1=ALU.add))
                wt_guard[ti % 3] = ("dve", dve_done[ti])
                ctx.wait("gp", "dve", dve_done[ti])
                qq_guard[ti % 2] = ctx.dma("gp", f"p0s{ti % 2}", tgt[r, :, c * 512:(c + 1) * 512], qb[:])
                ti += 1

    for widx in range(3):
        p0_gamma_one(widx)
    for widx in range(3):
        p0_quant_one(widx)
    if debug:
        ctx.wait("gp", "dve", ctx.cnt["dve"])
        ctx.dma("gp", "dbg", dbg["thr_dbg"], thr_sb[:, 0:3])
    ctx.barrier()
    ctx.emit_block()
    if upto < 1:
        kvs.close(); p0s.close(); ctx.es.close()
        return nc

    with ExitStack() as p1s:
        x_sb = [p1s.enter_context(nc.sbuf_tensor(f"x_sb{i}", [128, NCHK * T1], F32)) for i in range(2)]
        xb_sb = [p1s.enter_context(nc.sbuf_tensor(f"xb_sb{i}", [128, NCHK * T1], BF16)) for i in range(2)]
        xn1_sb = p1s.enter_context(nc.sbuf_tensor("xn1_sb", [128, NCHK * T1], BF16))
        sq_sb = [p1s.enter_context(nc.sbuf_tensor(f"sq_sb{i}", [128, T1], BF16)) for i in range(2)]
        wq_sl = [p1s.enter_context(nc.sbuf_tensor(f"wq_sl{i}", [128, NCHK * 128], BF16)) for i in range(NH)]
        wk_sl = [p1s.enter_context(nc.sbuf_tensor(f"wk_sl{i}", [128, NCHK * 128], BF16)) for i in range(NKV)]
        wv_sl = p1s.enter_context(nc.sbuf_tensor("wv_sl", [128, NCHK * NKV * DH], BF16))
        mrow = p1s.enter_context(nc.sbuf_tensor("mrow", [1, T1], F32))
        vrow = p1s.enter_context(nc.sbuf_tensor("vrow", [1, T1], F32))
        rrow = p1s.enter_context(nc.sbuf_tensor("rrow", [1, T1], F32))
        mb_sb = p1s.enter_context(nc.sbuf_tensor("mb_sb", [128, T1], F32))
        rb_sb = p1s.enter_context(nc.sbuf_tensor("rb_sb", [128, T1], F32))
        tmp_f = p1s.enter_context(nc.sbuf_tensor("tmp_f", [128, T1], F32))
        cs_sb = [p1s.enter_context(nc.sbuf_tensor(f"cs_sb{i}", [64, 2 * T1], F32)) for i in range(2)]
        rtm_sb = [p1s.enter_context(nc.sbuf_tensor(f"rtm_sb{i}", [64, T1], F32)) for i in range(4)]
        qrot_sb = [p1s.enter_context(nc.sbuf_tensor(f"qrot_sb{i}", [128, T1], BF16)) for i in range(2)]
        nsq_sb = p1s.enter_context(nc.sbuf_tensor("nsq_sb", [128, T1], BF16))
        ps = [p1s.enter_context(nc.psum_tensor(f"p1ps{i}", [128, 512], F32)) for i in range(8)]
        SUM, SUMSQ, MB, RB, K0, K1, QA, QB = range(8)

        for f in range(NH):
            split_dma(ctx, "gp", "p1w", wq_sl[f].ap().rearrange("p (c f) -> p c f", c=NCHK),
                      rearr_cpt(wqT[:, f * 128:(f + 1) * 128]))
        for kv in range(NKV):
            split_dma(ctx, "gp", "p1w", wk_sl[kv].ap().rearrange("p (c f) -> p c f", c=NCHK),
                      rearr_cpt(wkT[:, kv * 128:(kv + 1) * 128]))
        wld = split_dma(ctx, "gp", "p1w", wv_sl.ap().rearrange("p (c f) -> p c f", c=NCHK),
                        rearr_cpt(wvT[:, :]))

        prev = {}
        knorm_i = [0]

        _P1STAGE = 9

        def tile_p1(tau):
            t0 = tau * T1
            x_ = x_sb[tau % 2]
            xb_ = xb_sb[tau % 2]
            cs = cs_sb[tau % 2]
            if tau >= 2:
                ctx.wait("sync", "dve", prev[("xdone", tau - 2)])
                ctx.wait("sync", "dve", prev[("csdone", tau - 2)])
                ctx.wait("gp", "pe", prev[("xb_pe", tau - 2)])
                ctx.wait("gp", "act", prev[("xb_act", tau - 2)])
            if ("nmred",) in prev:
                ctx.wait("pe", "dve", prev[("nmred",)])
            xl = split_dma(ctx, "sync", f"p1x{tau % 2}", x_.ap().rearrange("p (c t) -> p c t", c=NCHK),
                           rearr_cpt(xT[:, t0:t0 + T1]))
            cl1 = ctx.dma("sync", f"p1c{tau % 2}", cs[:, 0:T1], costab[:, t0:t0 + T1])
            cl2 = ctx.dma("sync", f"p1c{tau % 2}", cs[:, T1:2 * T1], sintab[:, t0:t0 + T1])
            xbl = split_dma(ctx, "gp", f"p1b{tau % 2}", xb_.ap().rearrange("p (c t) -> p c t", c=NCHK),
                            rearr_cpt(xT[:, t0:t0 + T1]))

            ctx.wait("pe", f"p1b{tau % 2}", xbl)
            for c in range(NCHK):
                ctx.op("pe", lambda e, c=c: e.matmul(
                    ps[SUM][0:1, 0:T1], ones_col_bf[:], xb_[:, c * T1:(c + 1) * T1],
                    start=(c == 0), stop=(c == NCHK - 1)), inc=(c == NCHK - 1))
            ctx.wait("act", f"p1b{tau % 2}", xbl)
            sqmm = {}
            for c in range(NCHK):
                sq = sq_sb[c % 2]
                if c >= 2:
                    ctx.wait("act", "pe", sqmm[c - 2])
                sa = ctx.op("act", lambda e, c=c, sq=sq: e.activation(
                    out=sq[:], in_=xb_[:, c * T1:(c + 1) * T1], func=AF.Square))
                ctx.wait("pe", "act", sa)
                sqmm[c] = ctx.op("pe", lambda e, c=c, sq=sq: e.matmul(
                    ps[SUMSQ][0:1, 0:T1], ones_col_bf[:], sq[:],
                    start=(c == 0), stop=(c == NCHK - 1)))
            stats_pe = ctx.cnt["pe"]
            prev[("xb_pe", tau)] = stats_pe
            prev[("xb_act", tau)] = ctx.cnt["act"]
            if _P1STAGE <= 1:
                prev[("xdone", tau)] = ctx.cnt["dve"]
                prev[("csdone", tau)] = ctx.cnt["dve"]
                return

            ctx.wait("dve", "pe", stats_pe)
            ctx.op("dve", lambda e: e.tensor_scalar(
                out=mrow[:], in0=ps[SUM][0:1, 0:T1], scalar1=1.0 / D, scalar2=0.0,
                op0=ALU.mult, op1=ALU.bypass))
            ctx.op("dve", lambda e: e.tensor_scalar(
                out=vrow[:], in0=ps[SUMSQ][0:1, 0:T1], scalar1=1.0 / D, scalar2=EPS_LN,
                op0=ALU.mult, op1=ALU.add))
            ctx.op("dve", lambda e: e.tensor_tensor(
                out=rrow[:], in0=mrow[:], in1=mrow[:], op=ALU.mult))
            dv = ctx.op("dve", lambda e: e.tensor_tensor(
                out=vrow[:], in0=vrow[:], in1=rrow[:], op=ALU.subtract))
            ctx.wait("act", "dve", dv)
            ctx.op("act", lambda e: e.activation(
                out=rrow[:], in_=vrow[:], func=AF.Ln))
            ar = ctx.op("act", lambda e: e.activation(
                out=rrow[:], in_=rrow[:], func=AF.Exp, scale=-0.5))
            ctx.wait("pe", "act", ar)
            bm = ctx.op("pe", lambda e: e.matmul(ps[MB][:, 0:T1], ones_row_f[:], mrow[:],
                                                 start=True, stop=True))
            br = ctx.op("pe", lambda e: e.matmul(ps[RB][:, 0:T1], ones_row_f[:], rrow[:],
                                                 start=True, stop=True))
            ctx.wait("act", "pe", br)
            ctx.op("act", lambda e: e.activation(out=mb_sb[:], in_=ps[MB][:, 0:T1], func=AF.Copy))
            ab = ctx.op("act", lambda e: e.activation(out=rb_sb[:], in_=ps[RB][:, 0:T1], func=AF.Copy))
            ctx.wait("dve", "act", ab)
            ctx.wait("dve", f"p1x{tau % 2}", xl)
            for c in range(NCHK):
                ctx.op("dve", lambda e, c=c: e.tensor_tensor(
                    out=tmp_f[:], in0=x_[:, c * T1:(c + 1) * T1], in1=mb_sb[:], op=ALU.subtract))
                ctx.op("dve", lambda e, c=c: e.tensor_tensor(
                    out=xn1_sb[:, c * T1:(c + 1) * T1], in0=tmp_f[:], in1=rb_sb[:], op=ALU.mult))
            xn_done = ctx.cnt["dve"]
            prev[("xdone", tau)] = xn_done
            if _P1STAGE <= 2:
                prev[("csdone", tau)] = ctx.cnt["dve"]
                return

            def rope_emit(psrc, dst_even, dst_odd):
                cosA = cs[:, 0:T1]
                sinA = cs[:, T1:2 * T1]
                ctx.op("dve", lambda e: e.tensor_tensor(out=rtm_sb[0][:], in0=psrc[0:64, 0:T1], in1=cosA, op=ALU.mult))
                ctx.op("dve", lambda e: e.tensor_tensor(out=rtm_sb[1][:], in0=psrc[64:128, 0:T1], in1=sinA, op=ALU.mult))
                ctx.op("dve", lambda e: e.tensor_tensor(out=dst_even, in0=rtm_sb[0][:], in1=rtm_sb[1][:], op=ALU.subtract))
                ctx.op("dve", lambda e: e.tensor_tensor(out=rtm_sb[2][:], in0=psrc[0:64, 0:T1], in1=sinA, op=ALU.mult))
                ctx.op("dve", lambda e: e.tensor_tensor(out=rtm_sb[3][:], in0=psrc[64:128, 0:T1], in1=cosA, op=ALU.mult))
                return ctx.op("dve", lambda e: e.tensor_tensor(out=dst_odd, in0=rtm_sb[2][:], in1=rtm_sb[3][:], op=ALU.add))

            ctx.wait("pe", "dve", xn_done)
            if tau == 0:
                ctx.wait("pe", "p1w", wld)
            kmm = {}
            for kv in range(NKV):
                for c in range(NCHK):
                    ctx.op("pe", lambda e, kv=kv, c=c: e.matmul(
                        ps[K0 + kv][:, 0:T1], wk_sl[kv][:, c * 128:(c + 1) * 128],
                        xn1_sb[:, c * T1:(c + 1) * T1],
                        start=(c == 0), stop=(c == NCHK - 1)), inc=(c == NCHK - 1))
                kmm[kv] = ctx.cnt["pe"]
            ctx.wait("dve", f"p1c{tau % 2}", cl2)
            last_nm_red = prev.get(("nmred",), None)
            _sub = "all"
            for kv in range(NKV):
                ctx.wait("dve", "pe", kmm[kv])
                if _sub in ("all", "rope"):
                    rp = rope_emit(ps[K0 + kv], k_sb[kv][0:64, t0:t0 + T1], k_sb[kv][64:128, t0:t0 + T1])
                else:
                    rp = ctx.op("dve", lambda e, kv=kv: e.tensor_copy(
                        out=k_sb[kv][:, t0:t0 + T1], in_=ps[K0 + kv][:, 0:T1]))
                if _sub in ("all", "norm", "normsq", "normmm"):
                    ctx.wait("act", "dve", rp)
                    sa = ctx.op("act", lambda e, kv=kv, t0=t0: e.activation(
                        out=nsq_sb[:], in_=k_sb[kv][:, t0:t0 + T1], func=AF.Square))
                    if _sub != "normsq":
                        ctx.wait("pe", "act", sa)
                        if last_nm_red is not None:
                            ctx.wait("pe", "dve", last_nm_red)
                        nm = ctx.op("pe", lambda e: e.matmul(ps[SUM][0:1, 0:T1], ones_col_bf[:], nsq_sb[:],
                                                             start=True, stop=True))
                        if _sub != "normmm":
                            ctx.wait("dve", "pe", nm)
                            idx = 128 + knorm_i[0]
                            knorm_i[0] += 1
                            last_nm_red = ctx.op("dve", lambda e, idx=idx: e.reduce_max(
                                out=maxbuf[0:1, idx:idx + 1], in_=ps[SUM][0:1, 0:T1], axis=AXX))
                        ctx.wait("act", "pe", nm)
            prev[("nmred",)] = last_nm_red
            if _P1STAGE <= 3:
                prev[("csdone", tau)] = ctx.cnt["dve"]
                return

            for ts in range(T1 // 128):
                vp = ps[MB + ts]
                for c in range(NCHK):
                    ctx.op("pe", lambda e, ts=ts, c=c, vp=vp: e.matmul(
                        vp[:, 0:NKV * DH],
                        xn1_sb[:, c * T1 + ts * 128: c * T1 + ts * 128 + 128],
                        wv_sl[:, c * NKV * DH:(c + 1) * NKV * DH],
                        start=(c == 0), stop=(c == NCHK - 1)), inc=(c == NCHK - 1))
                vm = ctx.cnt["pe"]
                ctx.wait("act", "pe", vm)
                ctx.op("act", lambda e, ts=ts, vp=vp: e.activation(
                    out=v_sb[(t0 + ts * 128) // 128][:], in_=vp[:, 0:NKV * DH], func=AF.Copy))

            if _P1STAGE <= 4:
                prev[("csdone", tau)] = ctx.cnt["dve"]
                return
            for f in range(NH):
                qi = tau * NH + f
                qp = ps[QA + (f % 2)]
                if qi >= 2:
                    ctx.wait("pe", "dve", prev[("qrope", qi - 2)])
                    ctx.wait("pe", "act", prev[("qsq", qi - 2)])
                for c in range(NCHK):
                    ctx.op("pe", lambda e, f=f, c=c, qp=qp: e.matmul(
                        qp[:, 0:T1], wq_sl[f][:, c * 128:(c + 1) * 128],
                        xn1_sb[:, c * T1:(c + 1) * T1],
                        start=(c == 0), stop=(c == NCHK - 1)), inc=(c == NCHK - 1))
                qmm = ctx.cnt["pe"]
                qd = qrot_sb[f % 2]
                if qi >= 2:
                    ctx.wait("dve", f"p1q{qi % 2}", prev[("qst", qi - 2)])
                ctx.wait("dve", "pe", qmm)
                prev[("qrope", qi)] = rope_emit(qp, qd[0:64, :], qd[64:128, :])
                ctx.wait("act", "dve", prev[("qrope", qi)])
                sa = ctx.op("act", lambda e, qd=qd: e.activation(
                    out=nsq_sb[:], in_=qd[:], func=AF.Square))
                prev[("qsq", qi)] = sa
                ctx.wait("pe", "act", sa)
                ctx.wait("pe", "dve", last_nm_red)
                nm = ctx.op("pe", lambda e: e.matmul(ps[SUM][0:1, 0:T1], ones_col_bf[:], nsq_sb[:],
                                                     start=True, stop=True))
                ctx.wait("dve", "pe", nm)
                last_nm_red = ctx.op("dve", lambda e, qi=qi: e.reduce_max(
                    out=maxbuf[0:1, qi:qi + 1], in_=ps[SUM][0:1, 0:T1], axis=AXX))
                ctx.wait("act", "pe", nm)
                ctx.wait("gp", "dve", ctx.cnt["dve"])
                prev[("qst", qi)] = ctx.dma("gp", f"p1q{qi % 2}", q_sc[f, :, t0:t0 + T1], qd[:])
            prev[("nmred",)] = last_nm_red
            prev[("csdone", tau)] = ctx.cnt["dve"]

        for tau in range(NT1):
            tile_p1(tau)

        ctx.op("dve", lambda e: e.reduce_max(out=maxbuf[0:1, 190:191], in_=maxbuf[0:1, 0:128], axis=AXX))
        ctx.op("dve", lambda e: e.reduce_max(out=maxbuf[0:1, 191:192], in_=maxbuf[0:1, 128:160], axis=AXX))
        mm2 = ctx.op("dve", lambda e: e.tensor_tensor(out=maxbuf[0:1, 189:190], in0=maxbuf[0:1, 190:191],
                                                      in1=maxbuf[0:1, 191:192], op=ALU.mult))
        ctx.wait("act", "dve", mm2)
        sq2 = ctx.op("act", lambda e: e.activation(out=maxbuf[0:1, 188:189], in_=maxbuf[0:1, 189:190],
                                                   func=AF.Sqrt))
        ctx.wait("dve", "act", sq2)
        ng = ctx.op("dve", lambda e: e.tensor_scalar(
            out=maxbuf[0:1, 187:188], in0=maxbuf[0:1, 188:189],
            scalar1=-BOUND_INFLATE * LAM, scalar2=0.0, op0=ALU.mult, op1=ALU.bypass))
        ctx.wait("pe", "dve", ng)
        bm2 = ctx.op("pe", lambda e: e.matmul(ps[MB][:, 0:1], ones_row_f[:], maxbuf[0:1, 187:188],
                                              start=True, stop=True))
        ctx.wait("act", "pe", bm2)
        ctx.op("act", lambda e: e.activation(out=negm_sb[:], in_=ps[MB][:, 0:1], func=AF.Copy))

        if debug:
            ctx.wait("gp", "act", ctx.cnt["act"])
            ctx.wait("gp", "dve", ctx.cnt["dve"])
            ctx.wait("gp", "p1q0", ctx.cnt.get("p1q0", 0))
            ctx.wait("gp", "p1q1", ctx.cnt.get("p1q1", 0))
            ctx.dma("gp", "dbg", dbg["negm_dbg"], negm_sb[:])
            for kv in range(NKV):
                ctx.dma("gp", "dbg", dbg["k_dbg"][kv * 128:(kv + 1) * 128, :], k_sb[kv][:])
            for i in range(S // 128):
                ctx.dma("gp", "dbg", dbg["v_dbg"][i], v_sb[i][:])
            for f in range(NH):
                ctx.dma("gp", "dbg", dbg["q_dbg"][f], q_sc[f])
        ctx.barrier()
        ctx.emit_block()
    if upto < 3:
        kvs.close(); p0s.close(); ctx.es.close()
        return nc

    with ExitStack() as p3s:
        q_tl = [p3s.enter_context(nc.sbuf_tensor(f"q_tl{i}", [128, 512], BF16)) for i in range(2)]
        p_tl = [p3s.enter_context(nc.sbuf_tensor(f"p_tl{i}", [128, 512], BF16)) for i in range(3)]
        inv_row = p3s.enter_context(nc.sbuf_tensor("inv_row", [1, 512], F32))
        invb_sb = [p3s.enter_context(nc.sbuf_tensor(f"invb_sb{i}", [128, 512], F32)) for i in range(2)]
        ao_sb = [p3s.enter_context(nc.sbuf_tensor(f"ao_sb{i}", [128, 512], BF16)) for i in range(2)]
        sps = [p3s.enter_context(nc.psum_tensor(f"sps{i}", [128, 512], F32)) for i in range(2)]
        ops_ = [p3s.enter_context(nc.psum_tensor(f"ops{i}", [128, 512], F32)) for i in range(2)]
        rps = [p3s.enter_context(nc.psum_tensor(f"rps{i}", [1, 512], F32)) for i in range(2)]
        ibps = [p3s.enter_context(nc.psum_tensor(f"ibps{i}", [128, 512], F32)) for i in range(2)]

        pm = {}
        ui = 0
        for h in range(NH):
            kv = h // 4
            for j in range(8):
                KC = 4 * j + 4
                qt = q_tl[ui % 2]
                if ui >= 2:
                    ctx.wait("sync", "pe", pm[("qtile_pe", ui - 2)])
                qld = ctx.dma("sync", f"p3q{ui % 2}", qt[:], q_sc[h, :, j * 512:(j + 1) * 512])
                op_ = ops_[ui % 2]
                rp_ = rps[ui % 2]
                if ui >= 2:
                    ctx.wait("pe", "dve", pm[("oev", ui - 2)])
                last_score = None
                exp_done = {}
                for kc in range(KC):
                    di = kc - 4 * j
                    if di < 0:
                        qof, N = 0, 512
                    else:
                        qof, N = 128 * di, 512 - 128 * di
                    sp = sps[kc % 2]
                    if kc == 0:
                        ctx.wait("pe", f"p3q{ui % 2}", qld)
                    if kc >= 2:
                        ctx.wait("pe", "act", exp_done[kc - 2])
                    ctx.op("pe", lambda e, sp=sp, kv=kv, kc=kc, qt=qt, qof=qof, N=N: e.matmul(
                        sp[:, 0:N], k_sb[kv][:, kc * 128:(kc + 1) * 128], qt[:, qof:qof + N],
                        start=True, stop=True), inc=True)
                    smm = ctx.cnt["pe"]
                    pt = p_tl[kc % 3]
                    if kc >= 3:
                        ctx.wait("act", "pe", pm[("pcons", ui, kc - 3)])
                    ctx.wait("act", "pe", smm)
                    ed = ctx.op("act", lambda e, sp=sp, pt=pt, N=N: e.activation(
                        out=pt[:, 0:N], in_=sp[:, 0:N], func=AF.Exp,
                        bias=negm_sb[:], scale=LAM))
                    exp_done[kc] = ed
                    pe_src_ready = ("act", ed)
                    if di >= 0:
                        ctx.wait("dve", "act", ed)
                        md = ctx.op("dve", lambda e, pt=pt: e.tensor_tensor(
                            out=pt[:, 0:128], in0=pt[:, 0:128], in1=tri_sb[:], op=ALU.mult))
                        pe_src_ready = ("dve", md)
                    ctx.wait("pe", pe_src_ready[0], pe_src_ready[1])
                    ctx.op("pe", lambda e, rp_=rp_, pt=pt, qof=qof, N=N, kc=kc, KC=KC: e.matmul(
                        rp_[0:1, qof:qof + N], ones_col_bf[:], pt[:, 0:N],
                        start=(kc == 0), stop=(kc == KC - 1)), inc=False)
                    ctx.op("pe", lambda e, op_=op_, pt=pt, kc=kc, h=h, qof=qof, N=N, KC=KC: e.matmul(
                        op_[:, qof:qof + N],
                        v_sb[kc][:, (h // 4) * DH:(h // 4 + 1) * DH], pt[:, 0:N],
                        start=(kc == 0), stop=(kc == KC - 1)), inc=True)
                    pm[("pcons", ui, kc)] = ctx.cnt["pe"]
                    last_score = ctx.cnt["pe"]
                pm[("qtile_pe", ui)] = last_score
                ctx.wait("dve", "pe", last_score)
                rc = ctx.op("dve", lambda e, rp_=rp_: e.reciprocal(out=inv_row[:], in_=rp_[0:1, 0:512]))
                ib = ibps[ui % 2]
                ctx.wait("pe", "dve", rc)
                if ui >= 2:
                    ctx.wait("pe", "dve", pm[("ibcons", ui - 2)])
                bc = ctx.op("pe", lambda e, ib=ib: e.matmul(ib[:, 0:512], ones_row_f[:], inv_row[:],
                                                            start=True, stop=True))
                ctx.wait("dve", "pe", bc)
                iv = invb_sb[ui % 2]
                ctx.op("dve", lambda e, ib=ib, iv=iv: e.tensor_copy(out=iv[:], in_=ib[:, 0:512]))
                pm[("ibcons", ui)] = ctx.cnt["dve"]
                ao = ao_sb[ui % 2]
                if ui >= 2:
                    ctx.wait("dve", f"p3s{ui % 2}", pm[("aost", ui - 2)])
                ev = ctx.op("dve", lambda e, op_=op_, iv=iv, ao=ao: e.tensor_tensor(
                    out=ao[:], in0=op_[:, 0:512], in1=iv[:], op=ALU.mult))
                pm[("oev", ui)] = ev
                ctx.wait("gp", "dve", ev)
                pm[("aost", ui)] = ctx.dma("gp", f"p3s{ui % 2}", cin[h * DH:(h + 1) * DH, j * 512:(j + 1) * 512], ao[:])
                ui += 1

        cc = ctx.sem["cc"]
        ctx.wait("gp", "p3s0", ctx.cnt.get("p3s0", 0))
        ctx.wait("gp", "p3s1", ctx.cnt.get("p3s1", 0))
        ctx.cnt["cc"] += 1
        cctgt = ctx.cnt["cc"]
        ctx.E("gp", lambda e: e.collective_compute(
            "AllGather", ALU.bypass, ins=[cin.opt()], outs=[cout.opt()],
            replica_groups=[[0, 1], [2, 3], [4, 5], [6, 7]]).then_inc(cc, 1))
        ctx.wait("gp", "cc", cctgt)
        if debug:
            ctx.dma("gp", "dbg", dbg["ao_dbg"], cout)
        ctx.barrier()
        ctx.emit_block()
    kvs.close()
    if upto < 4:
        p0s.close(); ctx.es.close()
        return nc

    ao_my = nc.dram_tensor("ao_my", [2 * NH * DH, HALF], BF16, kind="Internal").ap()

    ctx._mksem("p4g")

    def _gather_my_half(e):
        pid = nc.partition_id([mybir.EngineType.Pool])
        off = (pid % 2) * HALF
        h = ctx.sem["p4g"]
        for i in range(4):
            e.dma_start(out=ao_my[i * 512:(i + 1) * 512, :],
                        in_=cout[i * 512:(i + 1) * 512, bass.ds(off, HALF)]).then_inc(h, 16)
    ctx.cnt["p4g"] += 4 * 16
    aomyld = ctx.cnt["p4g"]
    ctx.E("gp", _gather_my_half)
    ctx.wait("gp", "p4g", aomyld)

    with ExitStack() as p4s:
        wo_sl = [p4s.enter_context(nc.sbuf_tensor(f"wo_sl{i}", [128, NCHK * 128], BF16)) for i in range(NCHK)]
        ao4_sb = p4s.enter_context(nc.sbuf_tensor("ao4_sb", [128, 16 * 512], BF16))
        x4_sb = p4s.enter_context(nc.sbuf_tensor("x4_sb", [128, 16 * 512], F32))
        y4_sb = p4s.enter_context(nc.sbuf_tensor("y4_sb", [128, 16 * 512], F32))
        yb_t = p4s.enter_context(nc.sbuf_tensor("yb_t", [128, 512], BF16))
        sq4_t = p4s.enter_context(nc.sbuf_tensor("sq4_t", [128, 512], BF16))
        m4 = p4s.enter_context(nc.sbuf_tensor("m4", [1, 512], F32))
        v4 = p4s.enter_context(nc.sbuf_tensor("v4", [1, 512], F32))
        r4 = p4s.enter_context(nc.sbuf_tensor("r4", [1, 512], F32))
        mb4 = p4s.enter_context(nc.sbuf_tensor("mb4", [128, 512], F32))
        rb4 = p4s.enter_context(nc.sbuf_tensor("rb4", [128, 512], F32))
        t4 = p4s.enter_context(nc.sbuf_tensor("t4", [128, 512], F32))
        xn2_t = p4s.enter_context(nc.sbuf_tensor("xn2_t", [128, 512], BF16))
        yps = [p4s.enter_context(nc.psum_tensor(f"yps{i}", [128, 512], F32)) for i in range(2)]
        s4ps = p4s.enter_context(nc.psum_tensor("s4ps", [1, 512], F32))
        q4ps = p4s.enter_context(nc.psum_tensor("q4ps", [1, 512], F32))
        b4ps = [p4s.enter_context(nc.psum_tensor(f"b4ps{i}", [128, 512], F32)) for i in range(2)]

        for do in range(NCHK):
            wol = split_dma(ctx, "gp", "p4w", wo_sl[do].ap().rearrange("p (c f) -> p c f", c=NCHK),
                            rearr_cpt(woT[:, do * 128:(do + 1) * 128]))

        for b in range(4):
            bc0 = b * 512
            if b >= 1:
                ctx.wait("gp", "pe", pm.get(("aocons", b - 1), 0))
                ctx.wait("sync", "dve", pm.get(("xcons", b - 1), 0))
            aol = None
            for f in range(16):
                aol = ctx.dma("gp", "p4a", ao4_sb[:, f * 512:(f + 1) * 512],
                              ao_my[f * 128:(f + 1) * 128, bc0:bc0 + 512])
            xl4 = split_dma(ctx, "sync", "p4x", x4_sb.ap().rearrange("p (c t) -> p c t", c=NCHK),
                            rearr_cpt(xTown[:, bc0:bc0 + 512]))
            ctx.wait("pe", "p4a", aol)
            if b == 0:
                ctx.wait("pe", "p4w", wol)
            ctx.wait("dve", "p4x", xl4)
            if b >= 1:
                ctx.wait("dve", "p4y", ctx.cnt.get("p4y", 0))
            for do in range(NCHK):
                yp = yps[do % 2]
                if do >= 2:
                    ctx.wait("pe", "dve", pm[("yev", b, do - 2)])
                for f in range(16):
                    ctx.op("pe", lambda e, yp=yp, do=do, f=f: e.matmul(
                        yp[:, 0:512], wo_sl[do][:, f * 128:(f + 1) * 128],
                        ao4_sb[:, f * 512:(f + 1) * 512],
                        start=(f == 0), stop=(f == 15)), inc=(f == 15))
                ym = ctx.cnt["pe"]
                ctx.wait("dve", "pe", ym)
                ev = ctx.op("dve", lambda e, yp=yp, do=do: e.tensor_tensor(
                    out=y4_sb[:, do * 512:(do + 1) * 512], in0=yp[:, 0:512],
                    in1=x4_sb[:, do * 512:(do + 1) * 512], op=ALU.add))
                pm[("yev", b, do)] = ev
                ctx.wait("gp", "dve", ev)
                ctx.dma("gp", "p4y", y_sc[do, :, bc0:bc0 + 512], y4_sb[:, do * 512:(do + 1) * 512])
                ctx.wait("act", "dve", ev)
                if do >= 1:
                    ctx.wait("act", "pe", pm[("stmm", b, do - 1)])
                ya = ctx.op("act", lambda e, do=do: e.activation(
                    out=yb_t[:], in_=y4_sb[:, do * 512:(do + 1) * 512], func=AF.Copy))
                sa = ctx.op("act", lambda e, do=do: e.activation(
                    out=sq4_t[:], in_=y4_sb[:, do * 512:(do + 1) * 512], func=AF.Square))
                ctx.wait("pe", "act", sa)
                ctx.op("pe", lambda e, do=do: e.matmul(
                    s4ps[0:1, 0:512], ones_col_bf[:], yb_t[:],
                    start=(do == 0), stop=(do == 15)), inc=False)
                ctx.op("pe", lambda e, do=do: e.matmul(
                    q4ps[0:1, 0:512], ones_col_bf[:], sq4_t[:],
                    start=(do == 0), stop=(do == 15)), inc=True)
                pm[("stmm", b, do)] = ctx.cnt["pe"]
            pm[("aocons", b)] = ctx.cnt["pe"]
            st4 = pm[("stmm", b, 15)]
            ctx.wait("dve", "pe", st4)
            ctx.op("dve", lambda e: e.tensor_scalar(out=m4[:], in0=s4ps[0:1, 0:512],
                                                    scalar1=1.0 / D, scalar2=0.0, op0=ALU.mult, op1=ALU.bypass))
            ctx.op("dve", lambda e: e.tensor_scalar(out=v4[:], in0=q4ps[0:1, 0:512],
                                                    scalar1=1.0 / D, scalar2=EPS_LN, op0=ALU.mult, op1=ALU.add))
            ctx.op("dve", lambda e: e.tensor_tensor(out=r4[:], in0=m4[:], in1=m4[:], op=ALU.mult))
            dv4 = ctx.op("dve", lambda e: e.tensor_tensor(out=v4[:], in0=v4[:], in1=r4[:], op=ALU.subtract))
            ctx.wait("act", "dve", dv4)
            ctx.op("act", lambda e: e.activation(out=r4[:], in_=v4[:], func=AF.Ln))
            ar4 = ctx.op("act", lambda e: e.activation(out=r4[:], in_=r4[:], func=AF.Exp, scale=-0.5))
            ctx.wait("pe", "act", ar4)
            bm4 = ctx.op("pe", lambda e: e.matmul(b4ps[0][:, 0:512], ones_row_f[:], m4[:], start=True, stop=True))
            br4 = ctx.op("pe", lambda e: e.matmul(b4ps[1][:, 0:512], ones_row_f[:], r4[:], start=True, stop=True))
            ctx.wait("act", "pe", br4)
            ctx.op("act", lambda e: e.activation(out=mb4[:], in_=b4ps[0][:, 0:512], func=AF.Copy))
            ab4 = ctx.op("act", lambda e: e.activation(out=rb4[:], in_=b4ps[1][:, 0:512], func=AF.Copy))
            ctx.wait("dve", "act", ab4)
            for do in range(NCHK):
                ctx.op("dve", lambda e, do=do: e.tensor_tensor(
                    out=t4[:], in0=y4_sb[:, do * 512:(do + 1) * 512], in1=mb4[:], op=ALU.subtract))
                xd = ctx.op("dve", lambda e, do=do: e.tensor_tensor(
                    out=xn2_t[:], in0=t4[:], in1=rb4[:], op=ALU.mult))
                ctx.wait("gp", "dve", xd)
                ctx.dma("gp", "p4n", xn2_sc[do, :, bc0:bc0 + 512], xn2_t[:])
                ctx.wait("dve", "p4n", ctx.cnt["p4n"])
            pm[("xcons", b)] = ctx.cnt["dve"]
        if debug:
            ctx.wait("gp", "dve", ctx.cnt["dve"])
            ctx.wait("gp", "p4y", ctx.cnt.get("p4y", 0))
            ctx.wait("gp", "p4n", ctx.cnt.get("p4n", 0))
            for doo in range(NCHK):
                ctx.dma("gp", "dbg", dbg["y_dbg"][doo], y_sc[doo])
                ctx.dma("gp", "dbg", dbg["xn2_dbg"][doo], xn2_sc[doo])
        ctx.barrier()
        ctx.emit_block()
    p0s.close()
    if upto < 5:
        ctx.es.close()
        return nc

    with ExitStack() as p5s:
        xn5_sb = p5s.enter_context(nc.sbuf_tensor("xn5_sb", [128, NCHK * 1024], BF16))
        hid_sb = p5s.enter_context(nc.sbuf_tensor("hid_sb", [128, NHC_FF * 1024], BF16))
        w15 = [p5s.enter_context(nc.sbuf_tensor(f"w15_{i}", [128, NCHK * 128], BF16)) for i in range(2)]
        w25 = [p5s.enter_context(nc.sbuf_tensor(f"w25_{i}", [128, NCHK * 128], BF16)) for i in range(2)]
        w35 = [p5s.enter_context(nc.sbuf_tensor(f"w35_{i}", [128, NHC_FF * 128], BF16)) for i in range(2)]
        sig_t = [p5s.enter_context(nc.sbuf_tensor(f"sig_t{i}", [128, 512], BF16)) for i in range(2)]
        y5_t = [p5s.enter_context(nc.sbuf_tensor(f"y5_t{i}", [128, 512], F32)) for i in range(2)]
        o5_t = [p5s.enter_context(nc.sbuf_tensor(f"o5_t{i}", [128, 512], F32)) for i in range(2)]
        x1ps = [p5s.enter_context(nc.psum_tensor(f"x1ps{i}", [128, 512], F32)) for i in range(2)]
        x2ps = [p5s.enter_context(nc.psum_tensor(f"x2ps{i}", [128, 512], F32)) for i in range(2)]
        yops = [p5s.enter_context(nc.psum_tensor(f"yops{i}", [128, 512], F32)) for i in range(2)]

        for sti in range(2):
            sc0 = sti * 1024
            if sti >= 1:
                ctx.wait("sync", "pe", pm[("xncons", sti - 1)])
            for c in range(NCHK):
                xnl = ctx.dma("sync", "p5x", xn5_sb[:, c * 1024:(c + 1) * 1024],
                              xn2_sc[c, :, sc0:sc0 + 1024])
            for hc in range(NHC_FF):
                gi = sti * NHC_FF + hc
                w1b, w2b = w15[hc % 2], w25[hc % 2]
                if gi >= 2:
                    ctx.wait("sync", "pe", pm[("wAcons", gi - 2)])
                split_dma(ctx, "sync", f"p5w{hc % 2}", w1b.ap().rearrange("p (c h) -> p c h", c=NCHK),
                          q1_sc[:, :, hc * 128:(hc + 1) * 128].rearrange("c p h -> p c h"))
                wl = split_dma(ctx, "sync", f"p5w{hc % 2}", w2b.ap().rearrange("p (c h) -> p c h", c=NCHK),
                               q2_sc[:, :, hc * 128:(hc + 1) * 128].rearrange("c p h -> p c h"))
                for th in range(2):
                    x1p, x2p = x1ps[th], x2ps[th]
                    if gi >= 1:
                        ctx.wait("pe", "act", pm[("sig", gi - 1, th)])
                        ctx.wait("pe", "dve", pm[("hid", gi - 1, th)])
                    if th == 0:
                        ctx.wait("pe", f"p5w{hc % 2}", wl)
                        if hc == 0:
                            ctx.wait("pe", "p5x", xnl)
                    for c in range(NCHK):
                        ctx.op("pe", lambda e, c=c, th=th, x1p=x1p, w1b=w1b: e.matmul(
                            x1p[:, 0:512], w1b[:, c * 128:(c + 1) * 128],
                            xn5_sb[:, c * 1024 + th * 512: c * 1024 + th * 512 + 512],
                            start=(c == 0), stop=(c == NCHK - 1)), inc=(c == NCHK - 1))
                    x1m = ctx.cnt["pe"]
                    for c in range(NCHK):
                        ctx.op("pe", lambda e, c=c, th=th, x2p=x2p, w2b=w2b: e.matmul(
                            x2p[:, 0:512], w2b[:, c * 128:(c + 1) * 128],
                            xn5_sb[:, c * 1024 + th * 512: c * 1024 + th * 512 + 512],
                            start=(c == 0), stop=(c == NCHK - 1)), inc=(c == NCHK - 1))
                    x2m = ctx.cnt["pe"]
                    sg = sig_t[th]
                    ctx.wait("act", "pe", x1m)
                    if gi >= 1:
                        ctx.wait("act", "dve", pm[("hid", gi - 1, th)])
                    pm[("sig", gi, th)] = ctx.op("act", lambda e, x1p=x1p, sg=sg: e.activation(
                        out=sg[:], in_=x1p[:, 0:512], func=AF.Sigmoid))
                    ctx.wait("dve", "act", pm[("sig", gi, th)])
                    ctx.wait("dve", "pe", x2m)
                    pm[("hid", gi, th)] = ctx.op("dve", lambda e, x2p=x2p, sg=sg, hc=hc, th=th: e.tensor_tensor(
                        out=hid_sb[:, hc * 1024 + th * 512: hc * 1024 + th * 512 + 512],
                        in0=sg[:], in1=x2p[:, 0:512], op=ALU.mult))
                pm[("wAcons", gi)] = ctx.cnt["pe"]
            pm[("xncons", sti)] = ctx.cnt["pe"]
            hid_done = pm[("hid", sti * NHC_FF + NHC_FF - 1, 1)]

            for c in range(NCHK):
                gci = sti * NCHK + c
                w3b = w35[c % 2]
                if gci >= 2:
                    ctx.wait("sync", "pe", pm[("wBcons", gci - 2)])
                w3l = split_dma(ctx, "sync", f"p5v{c % 2}", w3b.ap().rearrange("p (c d) -> p c d", c=NHC_FF),
                                q3_sc[:, :, c * 128:(c + 1) * 128].rearrange("c p d -> p c d"))
                for th in range(2):
                    if gci >= 1:
                        ctx.wait("sync", "dve", pm[("oev5", gci - 1, th)])
                    yl = ctx.dma("sync", f"p5y{th}", y5_t[th][:], y_sc[c, :, sc0 + th * 512: sc0 + th * 512 + 512])
                    yop = yops[th]
                    if gci >= 1:
                        ctx.wait("pe", "dve", pm[("oev5", gci - 1, th)])
                    if th == 0:
                        ctx.wait("pe", f"p5v{c % 2}", w3l)
                        if c == 0:
                            ctx.wait("pe", "dve", hid_done)
                    for hcb in range(NHC_FF):
                        ctx.op("pe", lambda e, hcb=hcb, th=th, yop=yop, w3b=w3b: e.matmul(
                            yop[:, 0:512], w3b[:, hcb * 128:(hcb + 1) * 128],
                            hid_sb[:, hcb * 1024 + th * 512: hcb * 1024 + th * 512 + 512],
                            start=(hcb == 0), stop=(hcb == NHC_FF - 1)), inc=(hcb == NHC_FF - 1))
                    ym5 = ctx.cnt["pe"]
                    ctx.wait("dve", "pe", ym5)
                    ctx.wait("dve", f"p5y{th}", yl)
                    if gci >= 1:
                        ctx.wait("dve", f"p5o{th}", pm[("ost5", gci - 1, th)])
                    ev5 = ctx.op("dve", lambda e, yop=yop, th=th: e.tensor_tensor(
                        out=o5_t[th][:], in0=yop[:, 0:512], in1=y5_t[th][:], op=ALU.add))
                    pm[("oev5", gci, th)] = ev5
                    ctx.wait("gp", "dve", ev5)
                    pm[("ost5", gci, th)] = ctx.dma(
                        "gp", f"p5o{th}", out[c * 128:(c + 1) * 128, sc0 + th * 512: sc0 + th * 512 + 512],
                        o5_t[th][:])
                pm[("wBcons", gci)] = ctx.cnt["pe"]

        ctx.barrier()
        ctx.emit_block()

    ctx.es.close()
    return nc


_PERM128 = np.concatenate([np.arange(0, 128, 2), np.arange(1, 128, 2)])


def _permute_heads_rows(W):
    n = W.shape[0] // 128
    idx = np.concatenate([h * 128 + _PERM128 for h in range(n)])
    return W[idx]


def _rope_tables():
    freqs = (1.0 / (10000.0 ** (np.arange(0, DH, 2, dtype=np.float32) /
                                np.float32(DH)))).astype(np.float32)
    ang = np.arange(S, dtype=np.float32)[None, :] * freqs[:, None]
    return np.cos(ang).astype(np.float32), np.sin(ang).astype(np.float32)


_NC_CACHE = {}


def kernel(L0_attn_x, L0_attn_Wq, L0_attn_Wk, L0_attn_Wv, L0_attn_Wo,
           L0_ffn_w1, L0_ffn_w2, L0_ffn_w3, _debug=False, _results=None):
    x = np.asarray(L0_attn_x, dtype=np.float32)
    Wq = np.asarray(L0_attn_Wq, dtype=np.float32)
    Wk = np.asarray(L0_attn_Wk, dtype=np.float32)
    Wv = np.asarray(L0_attn_Wv, dtype=np.float32)
    Wo = np.asarray(L0_attn_Wo, dtype=np.float32)
    w1 = np.asarray(L0_ffn_w1, dtype=np.float32)
    w2 = np.asarray(L0_ffn_w2, dtype=np.float32)
    w3 = np.asarray(L0_ffn_w3, dtype=np.float32)

    WqP = _permute_heads_rows(Wq)
    WkP = _permute_heads_rows(Wk)
    cost, sint = _rope_tables()
    trim = np.triu(np.ones((128, 128), dtype=np.float32)).astype(ml_dtypes.bfloat16)
    woT = np.ascontiguousarray(Wo.T)
    w1T = np.ascontiguousarray(w1.T)
    w2T = np.ascontiguousarray(w2.T)
    w3T = np.ascontiguousarray(w3.T)

    in_maps = []
    for c in range(8):
        b, g = c // 2, c % 2
        xTb = np.ascontiguousarray(x[b].T)
        in_maps.append({
            "xT": xTb,
            "xTown": np.ascontiguousarray(xTb[:, g * HALF:(g + 1) * HALF]),
            "wqT": np.ascontiguousarray(WqP[g * NH * DH:(g + 1) * NH * DH, :].T),
            "wkT": np.ascontiguousarray(WkP[g * NKV * DH:(g + 1) * NKV * DH, :].T),
            "wvT": np.ascontiguousarray(Wv[g * NKV * DH:(g + 1) * NKV * DH, :].T),
            "woT": woT,
            "w1T": w1T,
            "w2T": w2T,
            "w3T": w3T,
            "costab": cost,
            "sintab": sint,
            "tri": trim,
        })

    upto = 5
    key = ("prog", bool(_debug), upto)
    if key not in _NC_CACHE:
        _NC_CACHE[key] = build(debug=_debug, upto=upto)
    nc = _NC_CACHE[key]
    res = run_bass_kernel_spmd(nc, in_maps, core_ids=list(range(8)))
    if _results is not None:
        _results.extend(res.results)
    outp = np.empty((4, S, D), dtype=np.float32)
    for c in range(8):
        b, g = c // 2, c % 2
        outp[b, g * HALF:(g + 1) * HALF, :] = res.results[c]["out"].T
    return outp


# version 02
# speedup vs baseline: 1.1932x; 1.1932x over previous
import numpy as np
import ml_dtypes
import concourse.bass as bass
import concourse.mybir as mybir
from concourse.bass_utils import run_bass_kernel_spmd
from concourse.alu_op_type import AluOpType as ALU
from contextlib import ExitStack

F32 = mybir.dt.float32
BF16 = mybir.dt.bfloat16
AF = mybir.ActivationFunctionType
AXX = mybir.AxisListType.X

D = 2048
S = 4096
NH = 8
NKV = 2
DH = 128
HFF = 5632
NCHK = D // 128
T1 = 256
NT1 = S // T1
HALF = S // 2
LAM = float(1.0 / np.sqrt(128.0))
EPS_LN = 1e-5
EPS_Q = 1e-8
BOUND_INFLATE = 1.02
NHC_FF = HFF // 128

ENGS = ("sync", "gp", "pe", "act", "dve")
QSEM = {"sync": "ld", "gp": "gp"}
ESEM = {"pe": "pe", "act": "act", "dve": "dve"}


class Ctx:
    def __init__(self, nc, debug=False):
        self.nc = nc
        self.debug = debug
        self.streams = {e: [] for e in ENGS}
        self.es = ExitStack()
        self.sem = {}
        self.cnt = {}
        for name in ("pe", "act", "dve", "cc"):
            self._mksem(name)

    def _mksem(self, name):
        if name not in self.sem:
            self.sem[name] = self.es.enter_context(self.nc.semaphore(name))
            self.cnt[name] = 0
        return self.sem[name]

    def E(self, eng, fn):
        self.streams[eng].append(fn)

    def wait(self, eng, sem, val):
        if val is None or val <= 0:
            return
        h = self.sem[sem]
        self.E(eng, lambda e, h=h, v=val: e.wait_ge(h, v))

    def dma(self, qeng, sem, out, in_):
        h = self._mksem(sem)
        self.cnt[sem] += 16
        tgt = self.cnt[sem]
        self.E(qeng, lambda e, o=out, i=in_, h=h: e.dma_start(out=o, in_=i).then_inc(h, 16))
        return tgt

    def op(self, eng, fn, inc=True):
        semname = ESEM[eng]
        if eng in ("act", "dve") and self.cnt[semname] > 0:
            h0 = self.sem[semname]
            v0 = self.cnt[semname]
            self.E(eng, lambda e, h=h0, v=v0: e.wait_ge(h, v))
        if inc:
            self.cnt[semname] += 1
            tgt = self.cnt[semname]
            h = self.sem[semname]
            self.E(eng, lambda e, fn=fn, h=h: fn(e).then_inc(h, 1))
            return tgt
        self.E(eng, lambda e, fn=fn: fn(e))
        return self.cnt[semname]

    def barrier(self):
        counts = dict(self.cnt)
        own_engine_sem = {"pe": "pe", "act": "act", "dve": "dve"}
        for eng in ENGS:
            skip = own_engine_sem.get(eng)
            for s, v in counts.items():
                if s == skip or s == "cc":
                    continue
                self.wait(eng, s, v)

    def emit_block(self, name="phase"):
        nc = self.nc
        streams = self.streams
        with nc.named_scope(name), nc.Block() as block:
            @block.sync
            def _(sync):
                for fn in streams["sync"]:
                    fn(sync)

            @block.gpsimd
            def _(gpsimd):
                for fn in streams["gp"]:
                    fn(gpsimd)

            @block.tensor
            def _(tensor):
                for fn in streams["pe"]:
                    fn(tensor)

            @block.scalar
            def _(scalar):
                for fn in streams["act"]:
                    fn(scalar)

            @block.vector
            def _(vector):
                for fn in streams["dve"]:
                    fn(vector)
        self.streams = {e: [] for e in ENGS}


def rearr_cpt(ap2d):
    return ap2d.rearrange("(c p) t -> p c t", p=128)


def split_dma(ctx, qeng, sem, dst3, src3, nsplit=4):
    C = dst3.shape[1]
    step = C // nsplit
    tgt = None
    for i in range(nsplit):
        tgt = ctx.dma(qeng, sem, dst3[:, i * step:(i + 1) * step, :],
                      src3[:, i * step:(i + 1) * step, :])
    return tgt


def build(debug=False, upto=5):
    nc = bass.Bass("TRN2", target_bir_lowering=False)

    xT = nc.dram_tensor("xT", [D, S], F32, kind="ExternalInput").ap()
    xTown = nc.dram_tensor("xTown", [D, HALF], F32, kind="ExternalInput").ap()
    wqT = nc.dram_tensor("wqT", [D, NH * DH], F32, kind="ExternalInput").ap()
    wkT = nc.dram_tensor("wkT", [D, NKV * DH], F32, kind="ExternalInput").ap()
    wvT = nc.dram_tensor("wvT", [D, NKV * DH], F32, kind="ExternalInput").ap()
    woT = nc.dram_tensor("woT", [D, D], F32, kind="ExternalInput").ap()
    w1T = nc.dram_tensor("w1T", [D, HFF], F32, kind="ExternalInput").ap()
    w2T = nc.dram_tensor("w2T", [D, HFF], F32, kind="ExternalInput").ap()
    w3T = nc.dram_tensor("w3T", [HFF, D], F32, kind="ExternalInput").ap()
    costab = nc.dram_tensor("costab", [64, S], F32, kind="ExternalInput").ap()
    sintab = nc.dram_tensor("sintab", [64, S], F32, kind="ExternalInput").ap()
    tri = nc.dram_tensor("tri", [128, 128], BF16, kind="ExternalInput").ap()
    out = nc.dram_tensor("out", [D, HALF], F32, kind="ExternalOutput").ap()

    q_sc = nc.dram_tensor("q_sc", [NH, 128, S], BF16, kind="Internal").ap()
    q1_sc = nc.dram_tensor("q1_sc", [NCHK, 128, HFF], BF16, kind="Internal").ap()
    q2_sc = nc.dram_tensor("q2_sc", [NCHK, 128, HFF], BF16, kind="Internal").ap()
    q3_sc = nc.dram_tensor("q3_sc", [NHC_FF, 128, D], BF16, kind="Internal").ap()
    cin = nc.dram_tensor("cin", [NH * DH, S], BF16, kind="Internal").ap()
    cout_t = nc.dram_tensor("cout", [2 * NH * DH, S], BF16, kind="Internal")
    cout = cout_t.ap()
    y_sc = nc.dram_tensor("y_sc", [NCHK, 128, HALF], F32, kind="Internal").ap()
    xn2_sc = nc.dram_tensor("xn2_sc", [NCHK, 128, HALF], BF16, kind="Internal").ap()

    dbg = {}
    if debug:
        dbg["k_dbg"] = nc.dram_tensor("k_dbg", [NKV * 128, S], BF16, kind="ExternalOutput").ap()
        dbg["v_dbg"] = nc.dram_tensor("v_dbg", [S // 128, 128, NKV * DH], BF16, kind="ExternalOutput").ap()
        dbg["q_dbg"] = nc.dram_tensor("q_dbg", [NH, 128, S], BF16, kind="ExternalOutput").ap()
        dbg["negm_dbg"] = nc.dram_tensor("negm_dbg", [128, 1], F32, kind="ExternalOutput").ap()
        dbg["thr_dbg"] = nc.dram_tensor("thr_dbg", [128, 3], F32, kind="ExternalOutput").ap()
        dbg["ao_dbg"] = nc.dram_tensor("ao_dbg", [2 * NH * DH, S], BF16, kind="ExternalOutput").ap()
        dbg["y_dbg"] = nc.dram_tensor("y_dbg", [NCHK, 128, HALF], F32, kind="ExternalOutput").ap()
        dbg["xn2_dbg"] = nc.dram_tensor("xn2_dbg", [NCHK, 128, HALF], BF16, kind="ExternalOutput").ap()

    ctx = Ctx(nc, debug)
    es = ctx.es

    ones_col_bf = es.enter_context(nc.sbuf_tensor("ones_col_bf", [128, 1], BF16))
    ones_col_f = es.enter_context(nc.sbuf_tensor("ones_col_f", [128, 1], F32))
    ones_row_f = es.enter_context(nc.sbuf_tensor("ones_row_f", [1, 128], F32))
    negm_sb = es.enter_context(nc.sbuf_tensor("negm_sb", [128, 1], F32))
    maxbuf = es.enter_context(nc.sbuf_tensor("maxbuf", [1, 192], F32))
    p0s = ExitStack()
    wt_sb = [p0s.enter_context(nc.sbuf_tensor(f"wt_sb{i}", [128, 2048], F32)) for i in range(3)]
    dump_sb = p0s.enter_context(nc.sbuf_tensor("dump_sb", [128, 2048], BF16))
    gacc = p0s.enter_context(nc.sbuf_tensor("gacc", [128, 64], F32))
    gcol = p0s.enter_context(nc.sbuf_tensor("gcol", [128, 3], F32))
    thr_sb = p0s.enter_context(nc.sbuf_tensor("thr_sb", [128, 3], F32))
    nthr_sb = p0s.enter_context(nc.sbuf_tensor("nthr_sb", [128, 3], F32))
    bq_sb = p0s.enter_context(nc.sbuf_tensor("bq_sb", [128, 2048], BF16))
    qq_sb = [p0s.enter_context(nc.sbuf_tensor(f"qq_sb{i}", [128, 2048], BF16)) for i in range(2)]
    kvs = ExitStack()
    k_sb = [kvs.enter_context(nc.sbuf_tensor(f"k_sb{i}", [128, S], BF16)) for i in range(NKV)]
    v_sb = [kvs.enter_context(nc.sbuf_tensor(f"v_sb{i}", [128, NKV * DH], BF16)) for i in range(S // 128)]
    tri_sb = kvs.enter_context(nc.sbuf_tensor("tri_sb", [128, 128], BF16))

    ctx.op("dve", lambda e: e.memset(ones_col_bf[:], 1.0))
    ctx.op("dve", lambda e: e.memset(ones_col_f[:], 1.0))
    ctx.op("dve", lambda e: e.memset(ones_row_f[:], 1.0))
    ctx.op("dve", lambda e: e.memset(maxbuf[:], 0.0))
    ctx.dma("sync", "ldonce", tri_sb[:], tri)

    WMATS = [(w1T, D, HFF, q1_sc, 1408), (w2T, D, HFF, q2_sc, 1408), (w3T, HFF, D, q3_sc, 2048)]
    wt_guard = {}
    qq_guard = {}

    def p0_gamma_one(widx):
        w, R, C, _, TW = WMATS[widx]
        numel = R * C
        nrt, nct = R // 128, C // TW
        ntile = nrt * nct
        last = 0
        ti = 0
        ctx.wait("act", "dve", ctx.cnt["dve"])
        for r in range(nrt):
            for c in range(nct):
                buf = wt_sb[ti % 3]
                if (ti % 3) in wt_guard:
                    ctx.wait("sync", *wt_guard[ti % 3])
                ld = ctx.dma("sync", f"p0w{ti % 3}", buf[:, 0:TW], w[r * 128:(r + 1) * 128, c * TW:(c + 1) * TW])
                ctx.wait("act", f"p0w{ti % 3}", ld)
                last = ctx.op("act", lambda e, b=buf, col=ti, TW=TW: e.activation(
                    out=dump_sb[:, 0:TW], in_=b[:, 0:TW], func=AF.Abs,
                    accum_out=gacc[:, col:col + 1]))
                wt_guard[ti % 3] = ("act", last)
                ti += 1
        ctx.wait("dve", "act", last)
        rt = ctx.op("dve", lambda e, n=ntile, wi=widx: e.tensor_reduce(
            out=gcol[:, wi:wi + 1], in_=gacc[:, 0:n], axis=AXX, op=ALU.add))
        ctx.wait("pe", "dve", rt)
        with nc.psum_tensor(f"g_ps{widx}", [128, 16], F32) as g_ps:
            pm = ctx.op("pe", lambda e, wi=widx: e.matmul(
                g_ps[0:1, 0:1], ones_col_f[:], gcol[:, wi:wi + 1], start=True, stop=True))
            ctx.wait("dve", "pe", pm)
            t1 = ctx.op("dve", lambda e, wi=widx, n=numel: e.tensor_scalar(
                out=thr_sb[0:1, wi:wi + 1], in0=g_ps[0:1, 0:1],
                scalar1=0.5 / n, scalar2=0.5 * EPS_Q, op0=ALU.mult, op1=ALU.add))
            ctx.wait("pe", "dve", t1)
            pb = ctx.op("pe", lambda e, wi=widx: e.matmul(
                g_ps[:, 1:2], ones_row_f[:], thr_sb[0:1, wi:wi + 1], start=True, stop=True))
            ctx.wait("dve", "pe", pb)
            t2 = ctx.op("dve", lambda e, wi=widx: e.tensor_copy(
                out=thr_sb[:, wi:wi + 1], in_=g_ps[:, 1:2]))
            t3 = ctx.op("dve", lambda e, wi=widx: e.tensor_scalar(
                out=nthr_sb[:, wi:wi + 1], in0=g_ps[:, 1:2],
                scalar1=-1.0, scalar2=0.0, op0=ALU.mult, op1=ALU.bypass))
            ctx.wait("pe", "dve", t3)

    def p0_quant_one(widx):
        w, R, C, tgt, TW = WMATS[widx]
        nrt, nct = R // 128, C // TW
        dve_done = {}
        st_done = {}
        ti = 0
        for r in range(nrt):
            for c in range(nct):
                buf = wt_sb[ti % 3]
                if (ti % 3) in wt_guard:
                    ctx.wait("sync", *wt_guard[ti % 3])
                ld = ctx.dma("sync", f"p0w{ti % 3}", buf[:, 0:TW], w[r * 128:(r + 1) * 128, c * TW:(c + 1) * TW])
                ctx.wait("dve", f"p0w{ti % 3}", ld)
                qb = qq_sb[ti % 2]
                if (ti % 2) in qq_guard:
                    ctx.wait("dve", f"p0s{ti % 2}", qq_guard[ti % 2])
                ctx.op("dve", lambda e, b=buf, wi=widx, TW=TW: e.tensor_scalar(
                    out=bq_sb[:, 0:TW], in0=b[:, 0:TW], scalar1=nthr_sb[:, wi:wi + 1], scalar2=-1.0,
                    op0=ALU.is_gt, op1=ALU.add))
                dve_done[ti] = ctx.op("dve", lambda e, b=buf, wi=widx, qb=qb, TW=TW: e.scalar_tensor_tensor(
                    out=qb[:, 0:TW], in0=b[:, 0:TW], scalar=thr_sb[:, wi:wi + 1], in1=bq_sb[:, 0:TW],
                    op0=ALU.is_ge, op1=ALU.add))
                wt_guard[ti % 3] = ("dve", dve_done[ti])
                ctx.wait("gp", "dve", dve_done[ti])
                qq_guard[ti % 2] = ctx.dma("gp", f"p0s{ti % 2}", tgt[r, :, c * TW:(c + 1) * TW], qb[:, 0:TW])
                ti += 1

    for widx in range(3):
        p0_gamma_one(widx)
    for widx in range(3):
        p0_quant_one(widx)
    if debug:
        ctx.wait("gp", "dve", ctx.cnt["dve"])
        ctx.dma("gp", "dbg", dbg["thr_dbg"], thr_sb[:, 0:3])
    ctx.barrier()
    ctx.emit_block("P0")
    if upto < 1:
        kvs.close(); p0s.close(); ctx.es.close()
        return nc

    with ExitStack() as p1s:
        x_sb = [p1s.enter_context(nc.sbuf_tensor(f"x_sb{i}", [128, NCHK * T1], F32)) for i in range(2)]
        xb_sb = [p1s.enter_context(nc.sbuf_tensor(f"xb_sb{i}", [128, NCHK * T1], BF16)) for i in range(2)]
        xn1_sb = p1s.enter_context(nc.sbuf_tensor("xn1_sb", [128, NCHK * T1], BF16))
        sq_sb = [p1s.enter_context(nc.sbuf_tensor(f"sq_sb{i}", [128, T1], BF16)) for i in range(2)]
        wq_sl = [p1s.enter_context(nc.sbuf_tensor(f"wq_sl{i}", [128, NCHK * 128], BF16)) for i in range(NH)]
        wk_sl = [p1s.enter_context(nc.sbuf_tensor(f"wk_sl{i}", [128, NCHK * 128], BF16)) for i in range(NKV)]
        wv_sl = p1s.enter_context(nc.sbuf_tensor("wv_sl", [128, NCHK * NKV * DH], BF16))
        mrow = p1s.enter_context(nc.sbuf_tensor("mrow", [1, T1], F32))
        vrow = p1s.enter_context(nc.sbuf_tensor("vrow", [1, T1], F32))
        rrow = p1s.enter_context(nc.sbuf_tensor("rrow", [1, T1], F32))
        mb_sb = p1s.enter_context(nc.sbuf_tensor("mb_sb", [128, T1], F32))
        rb_sb = p1s.enter_context(nc.sbuf_tensor("rb_sb", [128, T1], F32))
        tmp_f = p1s.enter_context(nc.sbuf_tensor("tmp_f", [128, T1], F32))
        cs_sb = [p1s.enter_context(nc.sbuf_tensor(f"cs_sb{i}", [64, 2 * T1], F32)) for i in range(2)]
        rtm_sb = [p1s.enter_context(nc.sbuf_tensor(f"rtm_sb{i}", [64, T1], F32)) for i in range(4)]
        qrot_sb = [p1s.enter_context(nc.sbuf_tensor(f"qrot_sb{i}", [128, T1], BF16)) for i in range(2)]
        nsq_sb = p1s.enter_context(nc.sbuf_tensor("nsq_sb", [128, T1], BF16))
        ps = [p1s.enter_context(nc.psum_tensor(f"p1ps{i}", [128, 512], F32)) for i in range(8)]
        SUM, SUMSQ, MB, RB, K0, K1, QA, QB = range(8)

        for f in range(NH):
            split_dma(ctx, "gp", "p1w", wq_sl[f].ap().rearrange("p (c f) -> p c f", c=NCHK),
                      rearr_cpt(wqT[:, f * 128:(f + 1) * 128]))
        for kv in range(NKV):
            split_dma(ctx, "gp", "p1w", wk_sl[kv].ap().rearrange("p (c f) -> p c f", c=NCHK),
                      rearr_cpt(wkT[:, kv * 128:(kv + 1) * 128]))
        wld = split_dma(ctx, "gp", "p1w", wv_sl.ap().rearrange("p (c f) -> p c f", c=NCHK),
                        rearr_cpt(wvT[:, :]))

        prev = {}
        knorm_i = [0]

        import os as _os2
        _P1STAGE = int(_os2.environ.get("KERN_P1STAGE", "9"))

        def tile_p1(tau):
            t0 = tau * T1
            x_ = x_sb[tau % 2]
            xb_ = xb_sb[tau % 2]
            cs = cs_sb[tau % 2]
            if tau >= 2:
                ctx.wait("sync", "dve", prev[("xdone", tau - 2)])
                ctx.wait("sync", "dve", prev[("csdone", tau - 2)])
                ctx.wait("gp", "pe", prev[("xb_pe", tau - 2)])
                ctx.wait("gp", "act", prev[("xb_act", tau - 2)])
            if ("nmred",) in prev:
                ctx.wait("pe", "dve", prev[("nmred",)])
            xl = split_dma(ctx, "sync", f"p1x{tau % 2}", x_.ap().rearrange("p (c t) -> p c t", c=NCHK),
                           rearr_cpt(xT[:, t0:t0 + T1]))
            cl1 = ctx.dma("sync", f"p1c{tau % 2}", cs[:, 0:T1], costab[:, t0:t0 + T1])
            cl2 = ctx.dma("sync", f"p1c{tau % 2}", cs[:, T1:2 * T1], sintab[:, t0:t0 + T1])
            xbl = split_dma(ctx, "gp", f"p1b{tau % 2}", xb_.ap().rearrange("p (c t) -> p c t", c=NCHK),
                            rearr_cpt(xT[:, t0:t0 + T1]))

            ctx.wait("pe", f"p1b{tau % 2}", xbl)
            for c in range(NCHK):
                ctx.op("pe", lambda e, c=c: e.matmul(
                    ps[SUM][0:1, 0:T1], ones_col_bf[:], xb_[:, c * T1:(c + 1) * T1],
                    start=(c == 0), stop=(c == NCHK - 1)), inc=(c == NCHK - 1))
            ctx.wait("act", f"p1b{tau % 2}", xbl)
            sqmm = {}
            for c in range(NCHK):
                sq = sq_sb[c % 2]
                if c >= 2:
                    ctx.wait("act", "pe", sqmm[c - 2])
                sa = ctx.op("act", lambda e, c=c, sq=sq: e.activation(
                    out=sq[:], in_=xb_[:, c * T1:(c + 1) * T1], func=AF.Square))
                ctx.wait("pe", "act", sa)
                sqmm[c] = ctx.op("pe", lambda e, c=c, sq=sq: e.matmul(
                    ps[SUMSQ][0:1, 0:T1], ones_col_bf[:], sq[:],
                    start=(c == 0), stop=(c == NCHK - 1)))
            stats_pe = ctx.cnt["pe"]
            prev[("xb_pe", tau)] = stats_pe
            prev[("xb_act", tau)] = ctx.cnt["act"]
            if _P1STAGE <= 1:
                prev[("xdone", tau)] = ctx.cnt["dve"]
                prev[("csdone", tau)] = ctx.cnt["dve"]
                return

            ctx.wait("dve", "pe", stats_pe)
            ctx.op("dve", lambda e: e.tensor_scalar(
                out=mrow[:], in0=ps[SUM][0:1, 0:T1], scalar1=1.0 / D, scalar2=0.0,
                op0=ALU.mult, op1=ALU.bypass))
            ctx.op("dve", lambda e: e.tensor_scalar(
                out=vrow[:], in0=ps[SUMSQ][0:1, 0:T1], scalar1=1.0 / D, scalar2=EPS_LN,
                op0=ALU.mult, op1=ALU.add))
            ctx.op("dve", lambda e: e.tensor_tensor(
                out=rrow[:], in0=mrow[:], in1=mrow[:], op=ALU.mult))
            dv = ctx.op("dve", lambda e: e.tensor_tensor(
                out=vrow[:], in0=vrow[:], in1=rrow[:], op=ALU.subtract))
            ctx.wait("act", "dve", dv)
            ctx.op("act", lambda e: e.activation(
                out=rrow[:], in_=vrow[:], func=AF.Ln))
            ar = ctx.op("act", lambda e: e.activation(
                out=rrow[:], in_=rrow[:], func=AF.Exp, scale=-0.5))
            ctx.wait("pe", "act", ar)
            bm = ctx.op("pe", lambda e: e.matmul(ps[MB][:, 0:T1], ones_row_f[:], mrow[:],
                                                 start=True, stop=True))
            br = ctx.op("pe", lambda e: e.matmul(ps[RB][:, 0:T1], ones_row_f[:], rrow[:],
                                                 start=True, stop=True))
            ctx.wait("act", "pe", br)
            ctx.op("act", lambda e: e.activation(out=mb_sb[:], in_=ps[MB][:, 0:T1], func=AF.Copy))
            ab = ctx.op("act", lambda e: e.activation(out=rb_sb[:], in_=ps[RB][:, 0:T1], func=AF.Copy))
            ctx.wait("dve", "act", ab)
            ctx.wait("dve", f"p1x{tau % 2}", xl)
            for c in range(NCHK):
                ctx.op("dve", lambda e, c=c: e.tensor_tensor(
                    out=tmp_f[:], in0=x_[:, c * T1:(c + 1) * T1], in1=mb_sb[:], op=ALU.subtract))
                ctx.op("dve", lambda e, c=c: e.tensor_tensor(
                    out=xn1_sb[:, c * T1:(c + 1) * T1], in0=tmp_f[:], in1=rb_sb[:], op=ALU.mult))
            xn_done = ctx.cnt["dve"]
            prev[("xdone", tau)] = xn_done
            if _P1STAGE <= 2:
                prev[("csdone", tau)] = ctx.cnt["dve"]
                return

            def rope_emit(psrc, dst_even, dst_odd):
                cosA = cs[:, 0:T1]
                sinA = cs[:, T1:2 * T1]
                ctx.op("dve", lambda e: e.tensor_tensor(out=rtm_sb[0][:], in0=psrc[0:64, 0:T1], in1=cosA, op=ALU.mult))
                ctx.op("dve", lambda e: e.tensor_tensor(out=rtm_sb[1][:], in0=psrc[64:128, 0:T1], in1=sinA, op=ALU.mult))
                ctx.op("dve", lambda e: e.tensor_tensor(out=dst_even, in0=rtm_sb[0][:], in1=rtm_sb[1][:], op=ALU.subtract))
                ctx.op("dve", lambda e: e.tensor_tensor(out=rtm_sb[2][:], in0=psrc[0:64, 0:T1], in1=sinA, op=ALU.mult))
                ctx.op("dve", lambda e: e.tensor_tensor(out=rtm_sb[3][:], in0=psrc[64:128, 0:T1], in1=cosA, op=ALU.mult))
                return ctx.op("dve", lambda e: e.tensor_tensor(out=dst_odd, in0=rtm_sb[2][:], in1=rtm_sb[3][:], op=ALU.add))

            ctx.wait("pe", "dve", xn_done)
            if tau == 0:
                ctx.wait("pe", "p1w", wld)
            kmm = {}
            for kv in range(NKV):
                for c in range(NCHK):
                    ctx.op("pe", lambda e, kv=kv, c=c: e.matmul(
                        ps[K0 + kv][:, 0:T1], wk_sl[kv][:, c * 128:(c + 1) * 128],
                        xn1_sb[:, c * T1:(c + 1) * T1],
                        start=(c == 0), stop=(c == NCHK - 1)), inc=(c == NCHK - 1))
                kmm[kv] = ctx.cnt["pe"]
            ctx.wait("dve", f"p1c{tau % 2}", cl2)
            last_nm_red = prev.get(("nmred",), None)
            _sub = _os2.environ.get("KERN_P1SUB", "all")
            for kv in range(NKV):
                ctx.wait("dve", "pe", kmm[kv])
                if _sub in ("all", "rope"):
                    rp = rope_emit(ps[K0 + kv], k_sb[kv][0:64, t0:t0 + T1], k_sb[kv][64:128, t0:t0 + T1])
                else:
                    rp = ctx.op("dve", lambda e, kv=kv: e.tensor_copy(
                        out=k_sb[kv][:, t0:t0 + T1], in_=ps[K0 + kv][:, 0:T1]))
                if _sub in ("all", "norm", "normsq", "normmm"):
                    ctx.wait("act", "dve", rp)
                    sa = ctx.op("act", lambda e, kv=kv, t0=t0: e.activation(
                        out=nsq_sb[:], in_=k_sb[kv][:, t0:t0 + T1], func=AF.Square))
                    if _sub != "normsq":
                        ctx.wait("pe", "act", sa)
                        if last_nm_red is not None:
                            ctx.wait("pe", "dve", last_nm_red)
                        nm = ctx.op("pe", lambda e: e.matmul(ps[SUM][0:1, 0:T1], ones_col_bf[:], nsq_sb[:],
                                                             start=True, stop=True))
                        if _sub != "normmm":
                            ctx.wait("dve", "pe", nm)
                            idx = 128 + knorm_i[0]
                            knorm_i[0] += 1
                            last_nm_red = ctx.op("dve", lambda e, idx=idx: e.reduce_max(
                                out=maxbuf[0:1, idx:idx + 1], in_=ps[SUM][0:1, 0:T1], axis=AXX))
                        ctx.wait("act", "pe", nm)
            prev[("nmred",)] = last_nm_red
            if _P1STAGE <= 3:
                prev[("csdone", tau)] = ctx.cnt["dve"]
                return

            for ts in range(T1 // 128):
                vp = ps[MB + ts]
                for c in range(NCHK):
                    ctx.op("pe", lambda e, ts=ts, c=c, vp=vp: e.matmul(
                        vp[:, 0:NKV * DH],
                        xn1_sb[:, c * T1 + ts * 128: c * T1 + ts * 128 + 128],
                        wv_sl[:, c * NKV * DH:(c + 1) * NKV * DH],
                        start=(c == 0), stop=(c == NCHK - 1)), inc=(c == NCHK - 1))
                vm = ctx.cnt["pe"]
                ctx.wait("act", "pe", vm)
                ctx.op("act", lambda e, ts=ts, vp=vp: e.activation(
                    out=v_sb[(t0 + ts * 128) // 128][:], in_=vp[:, 0:NKV * DH], func=AF.Copy))

            if _P1STAGE <= 4:
                prev[("csdone", tau)] = ctx.cnt["dve"]
                return
            for f in range(NH):
                qi = tau * NH + f
                qp = ps[QA + (f % 2)]
                if qi >= 2:
                    ctx.wait("pe", "dve", prev[("qrope", qi - 2)])
                    ctx.wait("pe", "act", prev[("qsq", qi - 2)])
                for c in range(NCHK):
                    ctx.op("pe", lambda e, f=f, c=c, qp=qp: e.matmul(
                        qp[:, 0:T1], wq_sl[f][:, c * 128:(c + 1) * 128],
                        xn1_sb[:, c * T1:(c + 1) * T1],
                        start=(c == 0), stop=(c == NCHK - 1)), inc=(c == NCHK - 1))
                qmm = ctx.cnt["pe"]
                qd = qrot_sb[f % 2]
                if qi >= 2:
                    ctx.wait("dve", f"p1q{qi % 2}", prev[("qst", qi - 2)])
                ctx.wait("dve", "pe", qmm)
                prev[("qrope", qi)] = rope_emit(qp, qd[0:64, :], qd[64:128, :])
                ctx.wait("act", "dve", prev[("qrope", qi)])
                sa = ctx.op("act", lambda e, qd=qd: e.activation(
                    out=nsq_sb[:], in_=qd[:], func=AF.Square))
                prev[("qsq", qi)] = sa
                ctx.wait("pe", "act", sa)
                ctx.wait("pe", "dve", last_nm_red)
                nm = ctx.op("pe", lambda e: e.matmul(ps[SUM][0:1, 0:T1], ones_col_bf[:], nsq_sb[:],
                                                     start=True, stop=True))
                ctx.wait("dve", "pe", nm)
                last_nm_red = ctx.op("dve", lambda e, qi=qi: e.reduce_max(
                    out=maxbuf[0:1, qi:qi + 1], in_=ps[SUM][0:1, 0:T1], axis=AXX))
                ctx.wait("act", "pe", nm)
                ctx.wait("gp", "dve", ctx.cnt["dve"])
                prev[("qst", qi)] = ctx.dma("gp", f"p1q{qi % 2}", q_sc[f, :, t0:t0 + T1], qd[:])
            prev[("nmred",)] = last_nm_red
            prev[("csdone", tau)] = ctx.cnt["dve"]

        import os as _os
        _nt = int(_os.environ.get("KERN_NT1", str(NT1)))
        for tau in range(_nt):
            tile_p1(tau)

        ctx.op("dve", lambda e: e.reduce_max(out=maxbuf[0:1, 190:191], in_=maxbuf[0:1, 0:128], axis=AXX))
        ctx.op("dve", lambda e: e.reduce_max(out=maxbuf[0:1, 191:192], in_=maxbuf[0:1, 128:160], axis=AXX))
        mm2 = ctx.op("dve", lambda e: e.tensor_tensor(out=maxbuf[0:1, 189:190], in0=maxbuf[0:1, 190:191],
                                                      in1=maxbuf[0:1, 191:192], op=ALU.mult))
        ctx.wait("act", "dve", mm2)
        sq2 = ctx.op("act", lambda e: e.activation(out=maxbuf[0:1, 188:189], in_=maxbuf[0:1, 189:190],
                                                   func=AF.Sqrt))
        ctx.wait("dve", "act", sq2)
        ng = ctx.op("dve", lambda e: e.tensor_scalar(
            out=maxbuf[0:1, 187:188], in0=maxbuf[0:1, 188:189],
            scalar1=-BOUND_INFLATE * LAM, scalar2=0.0, op0=ALU.mult, op1=ALU.bypass))
        ctx.wait("pe", "dve", ng)
        bm2 = ctx.op("pe", lambda e: e.matmul(ps[MB][:, 0:1], ones_row_f[:], maxbuf[0:1, 187:188],
                                              start=True, stop=True))
        ctx.wait("act", "pe", bm2)
        ctx.op("act", lambda e: e.activation(out=negm_sb[:], in_=ps[MB][:, 0:1], func=AF.Copy))

        if debug:
            ctx.wait("gp", "act", ctx.cnt["act"])
            ctx.wait("gp", "dve", ctx.cnt["dve"])
            ctx.wait("gp", "p1q0", ctx.cnt.get("p1q0", 0))
            ctx.wait("gp", "p1q1", ctx.cnt.get("p1q1", 0))
            ctx.dma("gp", "dbg", dbg["negm_dbg"], negm_sb[:])
            for kv in range(NKV):
                ctx.dma("gp", "dbg", dbg["k_dbg"][kv * 128:(kv + 1) * 128, :], k_sb[kv][:])
            for i in range(S // 128):
                ctx.dma("gp", "dbg", dbg["v_dbg"][i], v_sb[i][:])
            for f in range(NH):
                ctx.dma("gp", "dbg", dbg["q_dbg"][f], q_sc[f])
        ctx.barrier()
        ctx.emit_block("P1")
    if upto < 3:
        kvs.close(); p0s.close(); ctx.es.close()
        return nc

    with ExitStack() as p3s:
        q_tl = [p3s.enter_context(nc.sbuf_tensor(f"q_tl{i}", [128, 512], BF16)) for i in range(2)]
        p_tl = [p3s.enter_context(nc.sbuf_tensor(f"p_tl{i}", [128, 1024], BF16)) for i in range(2)]
        inv_row = p3s.enter_context(nc.sbuf_tensor("inv_row", [1, 512], F32))
        invb_sb = [p3s.enter_context(nc.sbuf_tensor(f"invb_sb{i}", [128, 512], F32)) for i in range(2)]
        ao_sb = [p3s.enter_context(nc.sbuf_tensor(f"ao_sb{i}", [128, 512], BF16)) for i in range(2)]
        sps2 = [p3s.enter_context(nc.psum_tensor(f"sps2_{i}", [128, 1024], F32)) for i in range(2)]
        ops_ = [p3s.enter_context(nc.psum_tensor(f"ops{i}", [128, 512], F32)) for i in range(2)]
        rsib = [p3s.enter_context(nc.psum_tensor(f"rsib{i}", [128, 512], F32)) for i in range(2)]

        pm = {}
        ui = 0
        for h in range(NH):
            kv = h // 4
            for j in range(8):
                qt = q_tl[ui % 2]
                if ui >= 2:
                    ctx.wait("sync", "pe", pm[("qtile_pe", ui - 2)])
                qld = ctx.dma("sync", f"p3q{ui % 2}", qt[:], q_sc[h, :, j * 512:(j + 1) * 512])
                op_ = ops_[ui % 2]
                rp_ = rsib[ui % 2]
                if ui >= 2:
                    ctx.wait("pe", "dve", pm[("oev", ui - 2)])
                items = [("pair", pi) for pi in range(2 * j)] + [("diag", di) for di in range(4)]
                nit = len(items)
                exp_done = {}
                last_score = None
                for ci, (kind, idx) in enumerate(items):
                    sp = sps2[ci % 2]
                    if ci == 0:
                        ctx.wait("pe", f"p3q{ui % 2}", qld)
                    if ci >= 2:
                        ctx.wait("pe", "act", exp_done[ci - 2])
                    first = (ci == 0)
                    lastit = (ci == nit - 1)
                    if kind == "pair":
                        k0, k1 = 2 * idx, 2 * idx + 1
                        ctx.op("pe", lambda e, sp=sp, kv=kv, k0=k0, qt=qt: e.matmul(
                            sp[:, 0:512], k_sb[kv][:, k0 * 128:(k0 + 1) * 128], qt[:],
                            start=True, stop=True), inc=False)
                        ctx.op("pe", lambda e, sp=sp, kv=kv, k1=k1, qt=qt: e.matmul(
                            sp[:, 512:1024], k_sb[kv][:, k1 * 128:(k1 + 1) * 128], qt[:],
                            start=True, stop=True), inc=True)
                        smm = ctx.cnt["pe"]
                        pt = p_tl[ci % 2]
                        if ci >= 2:
                            ctx.wait("act", "pe", pm[("pcons", ui, ci - 2)])
                        ctx.wait("act", "pe", smm)
                        ed = ctx.op("act", lambda e, sp=sp, pt=pt: e.activation(
                            out=pt[:], in_=sp[:], func=AF.Exp, bias=negm_sb[:], scale=LAM))
                        exp_done[ci] = ed
                        ctx.wait("pe", "act", ed)
                        ctx.op("pe", lambda e, rp_=rp_, pt=pt, first=first: e.matmul(
                            rp_[0:1, 0:512], ones_col_bf[:], pt[:, 0:512],
                            start=first, stop=False), inc=False)
                        ctx.op("pe", lambda e, rp_=rp_, pt=pt: e.matmul(
                            rp_[0:1, 0:512], ones_col_bf[:], pt[:, 512:1024],
                            start=False, stop=False), inc=False)
                        ctx.op("pe", lambda e, op_=op_, pt=pt, k0=k0, h=h, first=first: e.matmul(
                            op_[:, 0:512], v_sb[k0][:, (h // 4) * DH:(h // 4 + 1) * DH], pt[:, 0:512],
                            start=first, stop=False), inc=False)
                        ctx.op("pe", lambda e, op_=op_, pt=pt, k1=k1, h=h: e.matmul(
                            op_[:, 0:512], v_sb[k1][:, (h // 4) * DH:(h // 4 + 1) * DH], pt[:, 512:1024],
                            start=False, stop=False), inc=True)
                    else:
                        di = idx
                        kc = 4 * j + di
                        qof, N = 128 * di, 512 - 128 * di
                        ctx.op("pe", lambda e, sp=sp, kv=kv, kc=kc, qt=qt, qof=qof, N=N: e.matmul(
                            sp[:, 0:N], k_sb[kv][:, kc * 128:(kc + 1) * 128], qt[:, qof:qof + N],
                            start=True, stop=True), inc=True)
                        smm = ctx.cnt["pe"]
                        pt = p_tl[ci % 2]
                        if ci >= 2:
                            ctx.wait("act", "pe", pm[("pcons", ui, ci - 2)])
                        ctx.wait("act", "pe", smm)
                        ed = ctx.op("act", lambda e, sp=sp, pt=pt, N=N: e.activation(
                            out=pt[:, 0:N], in_=sp[:, 0:N], func=AF.Exp, bias=negm_sb[:], scale=LAM))
                        exp_done[ci] = ed
                        ctx.wait("dve", "act", ed)
                        md = ctx.op("dve", lambda e, pt=pt: e.tensor_tensor(
                            out=pt[:, 0:128], in0=pt[:, 0:128], in1=tri_sb[:], op=ALU.mult))
                        ctx.wait("pe", "dve", md)
                        ctx.op("pe", lambda e, rp_=rp_, pt=pt, qof=qof, N=N, first=first, lastit=lastit: e.matmul(
                            rp_[0:1, qof:qof + N], ones_col_bf[:], pt[:, 0:N],
                            start=first, stop=lastit), inc=False)
                        ctx.op("pe", lambda e, op_=op_, pt=pt, kc=kc, h=h, qof=qof, N=N, first=first, lastit=lastit: e.matmul(
                            op_[:, qof:qof + N], v_sb[kc][:, (h // 4) * DH:(h // 4 + 1) * DH], pt[:, 0:N],
                            start=first, stop=lastit), inc=True)
                    pm[("pcons", ui, ci)] = ctx.cnt["pe"]
                    last_score = ctx.cnt["pe"]
                pm[("qtile_pe", ui)] = last_score
                ctx.wait("dve", "pe", last_score)
                rc = ctx.op("dve", lambda e, rp_=rp_: e.reciprocal(out=inv_row[:], in_=rp_[0:1, 0:512]))
                ctx.wait("pe", "dve", rc)
                bc = ctx.op("pe", lambda e, rp_=rp_: e.matmul(rp_[:, 0:512], ones_row_f[:], inv_row[:],
                                                             start=True, stop=True), inc=True)
                ctx.wait("dve", "pe", bc)
                iv = invb_sb[ui % 2]
                ctx.op("dve", lambda e, rp_=rp_, iv=iv: e.tensor_copy(out=iv[:], in_=rp_[:, 0:512]))
                pm[("ibcons", ui)] = ctx.cnt["dve"]
                ao = ao_sb[ui % 2]
                if ui >= 2:
                    ctx.wait("dve", f"p3s{ui % 2}", pm[("aost", ui - 2)])
                ev = ctx.op("dve", lambda e, op_=op_, iv=iv, ao=ao: e.tensor_tensor(
                    out=ao[:], in0=op_[:, 0:512], in1=iv[:], op=ALU.mult))
                pm[("oev", ui)] = ev
                pm[("rscons", ui)] = pm[("ibcons", ui)]
                if ui >= 1:
                    pass
                ctx.wait("gp", "dve", ev)
                pm[("aost", ui)] = ctx.dma("gp", f"p3s{ui % 2}", cin[h * DH:(h + 1) * DH, j * 512:(j + 1) * 512], ao[:])
                ui += 1

        cc = ctx.sem["cc"]
        ctx.wait("gp", "p3s0", ctx.cnt.get("p3s0", 0))
        ctx.wait("gp", "p3s1", ctx.cnt.get("p3s1", 0))
        ctx.cnt["cc"] += 1
        cctgt = ctx.cnt["cc"]
        ctx.E("gp", lambda e: e.collective_compute(
            "AllGather", ALU.bypass, ins=[cin.opt()], outs=[cout.opt()],
            replica_groups=[[0, 1], [2, 3], [4, 5], [6, 7]]).then_inc(cc, 1))
        ctx.wait("gp", "cc", cctgt)
        if debug:
            ctx.dma("gp", "dbg", dbg["ao_dbg"], cout)
        ctx.barrier()
        ctx.emit_block("P3")
    kvs.close()
    if upto < 4:
        p0s.close(); ctx.es.close()
        return nc

    ao_my = nc.dram_tensor("ao_my", [2 * NH * DH, HALF], BF16, kind="Internal").ap()

    ctx._mksem("p4g")

    def _gather_my_half(e):
        pid = nc.partition_id([mybir.EngineType.Pool])
        off = (pid % 2) * HALF
        h = ctx.sem["p4g"]
        for i in range(4):
            e.dma_start(out=ao_my[i * 512:(i + 1) * 512, :],
                        in_=cout[i * 512:(i + 1) * 512, bass.ds(off, HALF)]).then_inc(h, 16)
    ctx.cnt["p4g"] += 4 * 16
    aomyld = ctx.cnt["p4g"]
    ctx.E("gp", _gather_my_half)
    ctx.wait("gp", "p4g", aomyld)

    with ExitStack() as p4s:
        wo_sl = [p4s.enter_context(nc.sbuf_tensor(f"wo_sl{i}", [128, NCHK * 128], BF16)) for i in range(NCHK)]
        ao4_sb = p4s.enter_context(nc.sbuf_tensor("ao4_sb", [128, 16 * 512], BF16))
        x4_sb = p4s.enter_context(nc.sbuf_tensor("x4_sb", [128, 16 * 512], F32))
        y4_sb = p4s.enter_context(nc.sbuf_tensor("y4_sb", [128, 16 * 512], F32))
        yb_t = p4s.enter_context(nc.sbuf_tensor("yb_t", [128, 512], BF16))
        sq4_t = p4s.enter_context(nc.sbuf_tensor("sq4_t", [128, 512], BF16))
        m4 = p4s.enter_context(nc.sbuf_tensor("m4", [1, 512], F32))
        v4 = p4s.enter_context(nc.sbuf_tensor("v4", [1, 512], F32))
        r4 = p4s.enter_context(nc.sbuf_tensor("r4", [1, 512], F32))
        mb4 = p4s.enter_context(nc.sbuf_tensor("mb4", [128, 512], F32))
        rb4 = p4s.enter_context(nc.sbuf_tensor("rb4", [128, 512], F32))
        t4 = p4s.enter_context(nc.sbuf_tensor("t4", [128, 512], F32))
        xn2_t = p4s.enter_context(nc.sbuf_tensor("xn2_t", [128, 512], BF16))
        yps = [p4s.enter_context(nc.psum_tensor(f"yps{i}", [128, 512], F32)) for i in range(2)]
        s4ps = p4s.enter_context(nc.psum_tensor("s4ps", [1, 512], F32))
        q4ps = p4s.enter_context(nc.psum_tensor("q4ps", [1, 512], F32))
        b4ps = [p4s.enter_context(nc.psum_tensor(f"b4ps{i}", [128, 512], F32)) for i in range(2)]

        for do in range(NCHK):
            wol = split_dma(ctx, "gp", "p4w", wo_sl[do].ap().rearrange("p (c f) -> p c f", c=NCHK),
                            rearr_cpt(woT[:, do * 128:(do + 1) * 128]))

        for b in range(4):
            bc0 = b * 512
            if b >= 1:
                ctx.wait("gp", "pe", pm.get(("aocons", b - 1), 0))
                ctx.wait("sync", "dve", pm.get(("xcons", b - 1), 0))
            aol = None
            for f in range(16):
                aol = ctx.dma("gp", "p4a", ao4_sb[:, f * 512:(f + 1) * 512],
                              ao_my[f * 128:(f + 1) * 128, bc0:bc0 + 512])
            xl4 = split_dma(ctx, "sync", "p4x", x4_sb.ap().rearrange("p (c t) -> p c t", c=NCHK),
                            rearr_cpt(xTown[:, bc0:bc0 + 512]))
            ctx.wait("pe", "p4a", aol)
            if b == 0:
                ctx.wait("pe", "p4w", wol)
            ctx.wait("dve", "p4x", xl4)
            if b >= 1:
                ctx.wait("dve", "p4y", ctx.cnt.get("p4y", 0))
            for do in range(NCHK):
                yp = yps[do % 2]
                if do >= 2:
                    ctx.wait("pe", "dve", pm[("yev", b, do - 2)])
                for f in range(16):
                    ctx.op("pe", lambda e, yp=yp, do=do, f=f: e.matmul(
                        yp[:, 0:512], wo_sl[do][:, f * 128:(f + 1) * 128],
                        ao4_sb[:, f * 512:(f + 1) * 512],
                        start=(f == 0), stop=(f == 15)), inc=(f == 15))
                ym = ctx.cnt["pe"]
                ctx.wait("dve", "pe", ym)
                ev = ctx.op("dve", lambda e, yp=yp, do=do: e.tensor_tensor(
                    out=y4_sb[:, do * 512:(do + 1) * 512], in0=yp[:, 0:512],
                    in1=x4_sb[:, do * 512:(do + 1) * 512], op=ALU.add))
                pm[("yev", b, do)] = ev
                ctx.wait("gp", "dve", ev)
                ctx.dma("gp", "p4y", y_sc[do, :, bc0:bc0 + 512], y4_sb[:, do * 512:(do + 1) * 512])
                ctx.wait("act", "dve", ev)
                if do >= 1:
                    ctx.wait("act", "pe", pm[("stmm", b, do - 1)])
                ya = ctx.op("act", lambda e, do=do: e.activation(
                    out=yb_t[:], in_=y4_sb[:, do * 512:(do + 1) * 512], func=AF.Copy))
                sa = ctx.op("act", lambda e, do=do: e.activation(
                    out=sq4_t[:], in_=y4_sb[:, do * 512:(do + 1) * 512], func=AF.Square))
                ctx.wait("pe", "act", sa)
                ctx.op("pe", lambda e, do=do: e.matmul(
                    s4ps[0:1, 0:512], ones_col_bf[:], yb_t[:],
                    start=(do == 0), stop=(do == 15)), inc=False)
                ctx.op("pe", lambda e, do=do: e.matmul(
                    q4ps[0:1, 0:512], ones_col_bf[:], sq4_t[:],
                    start=(do == 0), stop=(do == 15)), inc=True)
                pm[("stmm", b, do)] = ctx.cnt["pe"]
            pm[("aocons", b)] = ctx.cnt["pe"]
            st4 = pm[("stmm", b, 15)]
            ctx.wait("dve", "pe", st4)
            ctx.op("dve", lambda e: e.tensor_scalar(out=m4[:], in0=s4ps[0:1, 0:512],
                                                    scalar1=1.0 / D, scalar2=0.0, op0=ALU.mult, op1=ALU.bypass))
            ctx.op("dve", lambda e: e.tensor_scalar(out=v4[:], in0=q4ps[0:1, 0:512],
                                                    scalar1=1.0 / D, scalar2=EPS_LN, op0=ALU.mult, op1=ALU.add))
            ctx.op("dve", lambda e: e.tensor_tensor(out=r4[:], in0=m4[:], in1=m4[:], op=ALU.mult))
            dv4 = ctx.op("dve", lambda e: e.tensor_tensor(out=v4[:], in0=v4[:], in1=r4[:], op=ALU.subtract))
            ctx.wait("act", "dve", dv4)
            ctx.op("act", lambda e: e.activation(out=r4[:], in_=v4[:], func=AF.Ln))
            ar4 = ctx.op("act", lambda e: e.activation(out=r4[:], in_=r4[:], func=AF.Exp, scale=-0.5))
            ctx.wait("pe", "act", ar4)
            bm4 = ctx.op("pe", lambda e: e.matmul(b4ps[0][:, 0:512], ones_row_f[:], m4[:], start=True, stop=True))
            br4 = ctx.op("pe", lambda e: e.matmul(b4ps[1][:, 0:512], ones_row_f[:], r4[:], start=True, stop=True))
            ctx.wait("act", "pe", br4)
            ctx.op("act", lambda e: e.activation(out=mb4[:], in_=b4ps[0][:, 0:512], func=AF.Copy))
            ab4 = ctx.op("act", lambda e: e.activation(out=rb4[:], in_=b4ps[1][:, 0:512], func=AF.Copy))
            ctx.wait("dve", "act", ab4)
            for do in range(NCHK):
                ctx.op("dve", lambda e, do=do: e.tensor_tensor(
                    out=t4[:], in0=y4_sb[:, do * 512:(do + 1) * 512], in1=mb4[:], op=ALU.subtract))
                xd = ctx.op("dve", lambda e, do=do: e.tensor_tensor(
                    out=xn2_t[:], in0=t4[:], in1=rb4[:], op=ALU.mult))
                ctx.wait("gp", "dve", xd)
                ctx.dma("gp", "p4n", xn2_sc[do, :, bc0:bc0 + 512], xn2_t[:])
                ctx.wait("dve", "p4n", ctx.cnt["p4n"])
            pm[("xcons", b)] = ctx.cnt["dve"]
        if debug:
            ctx.wait("gp", "dve", ctx.cnt["dve"])
            ctx.wait("gp", "p4y", ctx.cnt.get("p4y", 0))
            ctx.wait("gp", "p4n", ctx.cnt.get("p4n", 0))
            for doo in range(NCHK):
                ctx.dma("gp", "dbg", dbg["y_dbg"][doo], y_sc[doo])
                ctx.dma("gp", "dbg", dbg["xn2_dbg"][doo], xn2_sc[doo])
        ctx.barrier()
        ctx.emit_block("P4")
    p0s.close()
    if upto < 5:
        ctx.es.close()
        return nc

    with ExitStack() as p5s:
        xn5_sb = p5s.enter_context(nc.sbuf_tensor("xn5_sb", [128, NCHK * 1024], BF16))
        hid_sb = p5s.enter_context(nc.sbuf_tensor("hid_sb", [128, NHC_FF * 1024], BF16))
        w15 = [p5s.enter_context(nc.sbuf_tensor(f"w15_{i}", [128, NCHK * 128], BF16)) for i in range(2)]
        w25 = [p5s.enter_context(nc.sbuf_tensor(f"w25_{i}", [128, NCHK * 128], BF16)) for i in range(2)]
        w35 = [p5s.enter_context(nc.sbuf_tensor(f"w35_{i}", [128, NHC_FF * 128], BF16)) for i in range(2)]
        sig_t = [p5s.enter_context(nc.sbuf_tensor(f"sig_t{i}", [128, 512], BF16)) for i in range(2)]
        y5_t = [p5s.enter_context(nc.sbuf_tensor(f"y5_t{i}", [128, 512], F32)) for i in range(2)]
        o5_t = [p5s.enter_context(nc.sbuf_tensor(f"o5_t{i}", [128, 512], F32)) for i in range(2)]
        x1ps = [p5s.enter_context(nc.psum_tensor(f"x1ps{i}", [128, 512], F32)) for i in range(2)]
        x2ps = [p5s.enter_context(nc.psum_tensor(f"x2ps{i}", [128, 512], F32)) for i in range(2)]
        yops = [p5s.enter_context(nc.psum_tensor(f"yops{i}", [128, 512], F32)) for i in range(2)]

        for sti in range(2):
            sc0 = sti * 1024
            if sti >= 1:
                ctx.wait("sync", "pe", pm[("xncons", sti - 1)])
            for c in range(NCHK):
                xnl = ctx.dma("sync", "p5x", xn5_sb[:, c * 1024:(c + 1) * 1024],
                              xn2_sc[c, :, sc0:sc0 + 1024])
            for hc in range(NHC_FF):
                gi = sti * NHC_FF + hc
                w1b, w2b = w15[hc % 2], w25[hc % 2]
                if gi >= 2:
                    ctx.wait("sync", "pe", pm[("wAcons", gi - 2)])
                split_dma(ctx, "sync", f"p5w{hc % 2}", w1b.ap().rearrange("p (c h) -> p c h", c=NCHK),
                          q1_sc[:, :, hc * 128:(hc + 1) * 128].rearrange("c p h -> p c h"))
                wl = split_dma(ctx, "sync", f"p5w{hc % 2}", w2b.ap().rearrange("p (c h) -> p c h", c=NCHK),
                               q2_sc[:, :, hc * 128:(hc + 1) * 128].rearrange("c p h -> p c h"))
                for th in range(2):
                    x1p, x2p = x1ps[th], x2ps[th]
                    if gi >= 1:
                        ctx.wait("pe", "act", pm[("sig", gi - 1, th)])
                        ctx.wait("pe", "dve", pm[("hid", gi - 1, th)])
                    if th == 0:
                        ctx.wait("pe", f"p5w{hc % 2}", wl)
                        if hc == 0:
                            ctx.wait("pe", "p5x", xnl)
                    for c in range(NCHK):
                        ctx.op("pe", lambda e, c=c, th=th, x1p=x1p, w1b=w1b: e.matmul(
                            x1p[:, 0:512], w1b[:, c * 128:(c + 1) * 128],
                            xn5_sb[:, c * 1024 + th * 512: c * 1024 + th * 512 + 512],
                            start=(c == 0), stop=(c == NCHK - 1)), inc=(c == NCHK - 1))
                    x1m = ctx.cnt["pe"]
                    for c in range(NCHK):
                        ctx.op("pe", lambda e, c=c, th=th, x2p=x2p, w2b=w2b: e.matmul(
                            x2p[:, 0:512], w2b[:, c * 128:(c + 1) * 128],
                            xn5_sb[:, c * 1024 + th * 512: c * 1024 + th * 512 + 512],
                            start=(c == 0), stop=(c == NCHK - 1)), inc=(c == NCHK - 1))
                    x2m = ctx.cnt["pe"]
                    sg = sig_t[th]
                    ctx.wait("act", "pe", x1m)
                    if gi >= 1:
                        ctx.wait("act", "dve", pm[("hid", gi - 1, th)])
                    pm[("sig", gi, th)] = ctx.op("act", lambda e, x1p=x1p, sg=sg: e.activation(
                        out=sg[:], in_=x1p[:, 0:512], func=AF.Sigmoid))
                    ctx.wait("dve", "act", pm[("sig", gi, th)])
                    ctx.wait("dve", "pe", x2m)
                    pm[("hid", gi, th)] = ctx.op("dve", lambda e, x2p=x2p, sg=sg, hc=hc, th=th: e.tensor_tensor(
                        out=hid_sb[:, hc * 1024 + th * 512: hc * 1024 + th * 512 + 512],
                        in0=sg[:], in1=x2p[:, 0:512], op=ALU.mult))
                pm[("wAcons", gi)] = ctx.cnt["pe"]
            pm[("xncons", sti)] = ctx.cnt["pe"]
            hid_done = pm[("hid", sti * NHC_FF + NHC_FF - 1, 1)]

            for c in range(NCHK):
                gci = sti * NCHK + c
                w3b = w35[c % 2]
                if gci >= 2:
                    ctx.wait("sync", "pe", pm[("wBcons", gci - 2)])
                w3l = split_dma(ctx, "sync", f"p5v{c % 2}", w3b.ap().rearrange("p (c d) -> p c d", c=NHC_FF),
                                q3_sc[:, :, c * 128:(c + 1) * 128].rearrange("c p d -> p c d"))
                for th in range(2):
                    if gci >= 1:
                        ctx.wait("sync", "dve", pm[("oev5", gci - 1, th)])
                    yl = ctx.dma("sync", f"p5y{th}", y5_t[th][:], y_sc[c, :, sc0 + th * 512: sc0 + th * 512 + 512])
                    yop = yops[th]
                    if gci >= 1:
                        ctx.wait("pe", "dve", pm[("oev5", gci - 1, th)])
                    if th == 0:
                        ctx.wait("pe", f"p5v{c % 2}", w3l)
                        if c == 0:
                            ctx.wait("pe", "dve", hid_done)
                    for hcb in range(NHC_FF):
                        ctx.op("pe", lambda e, hcb=hcb, th=th, yop=yop, w3b=w3b: e.matmul(
                            yop[:, 0:512], w3b[:, hcb * 128:(hcb + 1) * 128],
                            hid_sb[:, hcb * 1024 + th * 512: hcb * 1024 + th * 512 + 512],
                            start=(hcb == 0), stop=(hcb == NHC_FF - 1)), inc=(hcb == NHC_FF - 1))
                    ym5 = ctx.cnt["pe"]
                    ctx.wait("dve", "pe", ym5)
                    ctx.wait("dve", f"p5y{th}", yl)
                    if gci >= 1:
                        ctx.wait("dve", f"p5o{th}", pm[("ost5", gci - 1, th)])
                    ev5 = ctx.op("dve", lambda e, yop=yop, th=th: e.tensor_tensor(
                        out=o5_t[th][:], in0=yop[:, 0:512], in1=y5_t[th][:], op=ALU.add))
                    pm[("oev5", gci, th)] = ev5
                    ctx.wait("gp", "dve", ev5)
                    pm[("ost5", gci, th)] = ctx.dma(
                        "gp", f"p5o{th}", out[c * 128:(c + 1) * 128, sc0 + th * 512: sc0 + th * 512 + 512],
                        o5_t[th][:])
                pm[("wBcons", gci)] = ctx.cnt["pe"]

        ctx.barrier()
        ctx.emit_block("P5")

    ctx.es.close()
    return nc


_PERM128 = np.concatenate([np.arange(0, 128, 2), np.arange(1, 128, 2)])


def _permute_heads_rows(W):
    n = W.shape[0] // 128
    idx = np.concatenate([h * 128 + _PERM128 for h in range(n)])
    return W[idx]


def _rope_tables():
    freqs = (1.0 / (10000.0 ** (np.arange(0, DH, 2, dtype=np.float32) /
                                np.float32(DH)))).astype(np.float32)
    ang = np.arange(S, dtype=np.float32)[None, :] * freqs[:, None]
    return np.cos(ang).astype(np.float32), np.sin(ang).astype(np.float32)


_NC_CACHE = {}


def kernel(L0_attn_x, L0_attn_Wq, L0_attn_Wk, L0_attn_Wv, L0_attn_Wo,
           L0_ffn_w1, L0_ffn_w2, L0_ffn_w3, _debug=False, _results=None):
    x = np.asarray(L0_attn_x, dtype=np.float32)
    Wq = np.asarray(L0_attn_Wq, dtype=np.float32)
    Wk = np.asarray(L0_attn_Wk, dtype=np.float32)
    Wv = np.asarray(L0_attn_Wv, dtype=np.float32)
    Wo = np.asarray(L0_attn_Wo, dtype=np.float32)
    w1 = np.asarray(L0_ffn_w1, dtype=np.float32)
    w2 = np.asarray(L0_ffn_w2, dtype=np.float32)
    w3 = np.asarray(L0_ffn_w3, dtype=np.float32)

    WqP = _permute_heads_rows(Wq)
    WkP = _permute_heads_rows(Wk)
    cost, sint = _rope_tables()
    trim = np.triu(np.ones((128, 128), dtype=np.float32)).astype(ml_dtypes.bfloat16)
    woT = np.ascontiguousarray(Wo.T)
    w1T = np.ascontiguousarray(w1.T)
    w2T = np.ascontiguousarray(w2.T)
    w3T = np.ascontiguousarray(w3.T)

    in_maps = []
    for c in range(8):
        b, g = c // 2, c % 2
        xTb = np.ascontiguousarray(x[b].T)
        in_maps.append({
            "xT": xTb,
            "xTown": np.ascontiguousarray(xTb[:, g * HALF:(g + 1) * HALF]),
            "wqT": np.ascontiguousarray(WqP[g * NH * DH:(g + 1) * NH * DH, :].T),
            "wkT": np.ascontiguousarray(WkP[g * NKV * DH:(g + 1) * NKV * DH, :].T),
            "wvT": np.ascontiguousarray(Wv[g * NKV * DH:(g + 1) * NKV * DH, :].T),
            "woT": woT,
            "w1T": w1T,
            "w2T": w2T,
            "w3T": w3T,
            "costab": cost,
            "sintab": sint,
            "tri": trim,
        })

    import os
    upto = int(os.environ.get("KERN_UPTO", "5"))
    key = ("prog", bool(_debug), upto)
    if key not in _NC_CACHE:
        _NC_CACHE[key] = build(debug=_debug, upto=upto)
    nc = _NC_CACHE[key]
    res = run_bass_kernel_spmd(nc, in_maps, core_ids=list(range(8)))
    if _results is not None:
        _results.extend(res.results)
    outp = np.empty((4, S, D), dtype=np.float32)
    for c in range(8):
        b, g = c // 2, c % 2
        outp[b, g * HALF:(g + 1) * HALF, :] = res.results[c]["out"].T
    return outp


# version 03
# speedup vs baseline: 1.2786x; 1.0716x over previous
import numpy as np
import ml_dtypes
import concourse.bass as bass
import concourse.mybir as mybir
from concourse.bass_utils import run_bass_kernel_spmd
from concourse.alu_op_type import AluOpType as ALU
from contextlib import ExitStack

F32 = mybir.dt.float32
BF16 = mybir.dt.bfloat16
AF = mybir.ActivationFunctionType
AXX = mybir.AxisListType.X

D = 2048
S = 4096
NH = 8
NKV = 2
DH = 128
HFF = 5632
NCHK = D // 128
T1 = 256
NT1 = S // T1
HALF = S // 2
LAM = float(1.0 / np.sqrt(128.0))
EPS_LN = 1e-5
EPS_Q = 1e-8
BOUND_INFLATE = 1.02
NHC_FF = HFF // 128

ENGS = ("sync", "gp", "pe", "act", "dve")
QSEM = {"sync": "ld", "gp": "gp"}
ESEM = {"pe": "pe", "act": "act", "dve": "dve"}


class Ctx:
    def __init__(self, nc, debug=False):
        self.nc = nc
        self.debug = debug
        self.streams = {e: [] for e in ENGS}
        self.es = ExitStack()
        self.sem = {}
        self.cnt = {}
        for name in ("pe", "act", "dve", "cc"):
            self._mksem(name)

    def _mksem(self, name):
        if name not in self.sem:
            self.sem[name] = self.es.enter_context(self.nc.semaphore(name))
            self.cnt[name] = 0
        return self.sem[name]

    def E(self, eng, fn):
        self.streams[eng].append(fn)

    def wait(self, eng, sem, val):
        if val is None or val <= 0:
            return
        h = self.sem[sem]
        self.E(eng, lambda e, h=h, v=val: e.wait_ge(h, v))

    def dma(self, qeng, sem, out, in_):
        h = self._mksem(sem)
        self.cnt[sem] += 16
        tgt = self.cnt[sem]
        self.E(qeng, lambda e, o=out, i=in_, h=h: e.dma_start(out=o, in_=i).then_inc(h, 16))
        return tgt

    def op(self, eng, fn, inc=True, serial=True):
        semname = ESEM[eng]
        if serial and eng in ("act", "dve") and self.cnt[semname] > 0:
            h0 = self.sem[semname]
            v0 = self.cnt[semname]
            self.E(eng, lambda e, h=h0, v=v0: e.wait_ge(h, v))
        if inc:
            self.cnt[semname] += 1
            tgt = self.cnt[semname]
            h = self.sem[semname]
            self.E(eng, lambda e, fn=fn, h=h: fn(e).then_inc(h, 1))
            return tgt
        self.E(eng, lambda e, fn=fn: fn(e))
        return self.cnt[semname]

    def barrier(self):
        counts = dict(self.cnt)
        own_engine_sem = {"pe": "pe", "act": "act", "dve": "dve"}
        for eng in ENGS:
            skip = own_engine_sem.get(eng)
            for s, v in counts.items():
                if s == skip or s == "cc":
                    continue
                self.wait(eng, s, v)

    def emit_block(self, name="phase"):
        nc = self.nc
        streams = self.streams
        import contextlib
        with nc.named_scope(name), nc.Block() as block:
            @block.sync
            def _(sync):
                for fn in streams["sync"]:
                    fn(sync)

            @block.gpsimd
            def _(gpsimd):
                for fn in streams["gp"]:
                    fn(gpsimd)

            @block.tensor
            def _(tensor):
                for fn in streams["pe"]:
                    fn(tensor)

            @block.scalar
            def _(scalar):
                for fn in streams["act"]:
                    fn(scalar)

            @block.vector
            def _(vector):
                for fn in streams["dve"]:
                    fn(vector)
        self.streams = {e: [] for e in ENGS}


def rearr_cpt(ap2d):
    return ap2d.rearrange("(c p) t -> p c t", p=128)


def split_dma(ctx, qeng, sem, dst3, src3, nsplit=4):
    C = dst3.shape[1]
    step = C // nsplit
    tgt = None
    for i in range(nsplit):
        tgt = ctx.dma(qeng, sem, dst3[:, i * step:(i + 1) * step, :],
                      src3[:, i * step:(i + 1) * step, :])
    return tgt


def build(debug=False, upto=5):
    nc = bass.Bass("TRN2", target_bir_lowering=False)

    xT = nc.dram_tensor("xT", [D, S], F32, kind="ExternalInput").ap()
    xTown = nc.dram_tensor("xTown", [D, HALF], F32, kind="ExternalInput").ap()
    wqT = nc.dram_tensor("wqT", [D, NH * DH], F32, kind="ExternalInput").ap()
    wkT = nc.dram_tensor("wkT", [D, NKV * DH], F32, kind="ExternalInput").ap()
    wvT = nc.dram_tensor("wvT", [D, NKV * DH], F32, kind="ExternalInput").ap()
    woT = nc.dram_tensor("woT", [D, D], F32, kind="ExternalInput").ap()
    w1T = nc.dram_tensor("w1T", [D, HFF], F32, kind="ExternalInput").ap()
    w2T = nc.dram_tensor("w2T", [D, HFF], F32, kind="ExternalInput").ap()
    w3T = nc.dram_tensor("w3T", [HFF, D], F32, kind="ExternalInput").ap()
    costab = nc.dram_tensor("costab", [64, S], F32, kind="ExternalInput").ap()
    sintab = nc.dram_tensor("sintab", [64, S], F32, kind="ExternalInput").ap()
    tri = nc.dram_tensor("tri", [128, 128], BF16, kind="ExternalInput").ap()
    out = nc.dram_tensor("out", [D, HALF], F32, kind="ExternalOutput").ap()

    q_sc = nc.dram_tensor("q_sc", [NH, 128, S], BF16, kind="Internal").ap()
    q1_sc = nc.dram_tensor("q1_sc", [NCHK, 128, HFF], BF16, kind="Internal").ap()
    q2_sc = nc.dram_tensor("q2_sc", [NCHK, 128, HFF], BF16, kind="Internal").ap()
    q3_sc = nc.dram_tensor("q3_sc", [NHC_FF, 128, D], BF16, kind="Internal").ap()
    cin = nc.dram_tensor("cin", [NH * DH, S], BF16, kind="Internal").ap()
    cout_t = nc.dram_tensor("cout", [2 * NH * DH, S], BF16, kind="Internal")
    cout = cout_t.ap()
    y_sc = nc.dram_tensor("y_sc", [NCHK, 128, HALF], F32, kind="Internal").ap()
    xn2_sc = nc.dram_tensor("xn2_sc", [NCHK, 128, HALF], BF16, kind="Internal").ap()

    dbg = {}
    if debug:
        dbg["k_dbg"] = nc.dram_tensor("k_dbg", [NKV * 128, S], BF16, kind="ExternalOutput").ap()
        dbg["v_dbg"] = nc.dram_tensor("v_dbg", [S // 128, 128, NKV * DH], BF16, kind="ExternalOutput").ap()
        dbg["q_dbg"] = nc.dram_tensor("q_dbg", [NH, 128, S], BF16, kind="ExternalOutput").ap()
        dbg["negm_dbg"] = nc.dram_tensor("negm_dbg", [128, 1], F32, kind="ExternalOutput").ap()
        dbg["thr_dbg"] = nc.dram_tensor("thr_dbg", [128, 3], F32, kind="ExternalOutput").ap()
        dbg["ao_dbg"] = nc.dram_tensor("ao_dbg", [2 * NH * DH, S], BF16, kind="ExternalOutput").ap()
        dbg["y_dbg"] = nc.dram_tensor("y_dbg", [NCHK, 128, HALF], F32, kind="ExternalOutput").ap()
        dbg["xn2_dbg"] = nc.dram_tensor("xn2_dbg", [NCHK, 128, HALF], BF16, kind="ExternalOutput").ap()

    ctx = Ctx(nc, debug)
    es = ctx.es

    ones_col_bf = es.enter_context(nc.sbuf_tensor("ones_col_bf", [128, 1], BF16))
    ones_col_f = es.enter_context(nc.sbuf_tensor("ones_col_f", [128, 1], F32))
    ones_row_f = es.enter_context(nc.sbuf_tensor("ones_row_f", [1, 128], F32))
    negm_sb = es.enter_context(nc.sbuf_tensor("negm_sb", [128, 1], F32))
    maxbuf = es.enter_context(nc.sbuf_tensor("maxbuf", [1, 192], F32))
    p0s = ExitStack()
    wt_sb = [p0s.enter_context(nc.sbuf_tensor(f"wt_sb{i}", [128, 2048], F32)) for i in range(3)]
    dump_sb = p0s.enter_context(nc.sbuf_tensor("dump_sb", [128, 2048], BF16))
    gacc = p0s.enter_context(nc.sbuf_tensor("gacc", [128, 64], F32))
    gcol = p0s.enter_context(nc.sbuf_tensor("gcol", [128, 3], F32))
    thr_sb = p0s.enter_context(nc.sbuf_tensor("thr_sb", [128, 3], F32))
    nthr_sb = p0s.enter_context(nc.sbuf_tensor("nthr_sb", [128, 3], F32))
    bq_sb = p0s.enter_context(nc.sbuf_tensor("bq_sb", [128, 2048], BF16))
    qq_sb = [p0s.enter_context(nc.sbuf_tensor(f"qq_sb{i}", [128, 2048], BF16)) for i in range(2)]
    kvs = ExitStack()
    k_sb = [kvs.enter_context(nc.sbuf_tensor(f"k_sb{i}", [128, S], BF16)) for i in range(NKV)]
    v_sb = [kvs.enter_context(nc.sbuf_tensor(f"v_sb{i}", [128, NKV * DH], BF16)) for i in range(S // 128)]
    tri_sb = kvs.enter_context(nc.sbuf_tensor("tri_sb", [128, 128], BF16))

    ctx.op("dve", lambda e: e.memset(ones_col_bf[:], 1.0))
    ctx.op("dve", lambda e: e.memset(ones_col_f[:], 1.0))
    ctx.op("dve", lambda e: e.memset(ones_row_f[:], 1.0))
    ctx.op("dve", lambda e: e.memset(maxbuf[:], 0.0))
    ctx.dma("sync", "ldonce", tri_sb[:], tri)

    WMATS = [(w1T, D, HFF, q1_sc, 1408), (w2T, D, HFF, q2_sc, 1408), (w3T, HFF, D, q3_sc, 2048)]
    wt_guard = {}
    qq_guard = {}

    def p0_gamma_one(widx):
        w, R, C, _, TW = WMATS[widx]
        numel = R * C
        nrt, nct = R // 128, C // TW
        ntile = nrt * nct
        last = 0
        ti = 0
        ctx.wait("act", "dve", ctx.cnt["dve"])
        for r in range(nrt):
            for c in range(nct):
                buf = wt_sb[ti % 3]
                if (ti % 3) in wt_guard:
                    ctx.wait("sync", *wt_guard[ti % 3])
                ld = ctx.dma("sync", f"p0w{ti % 3}", buf[:, 0:TW], w[r * 128:(r + 1) * 128, c * TW:(c + 1) * TW])
                ctx.wait("act", f"p0w{ti % 3}", ld)
                last = ctx.op("act", lambda e, b=buf, col=ti, TW=TW: e.activation(
                    out=dump_sb[:, 0:TW], in_=b[:, 0:TW], func=AF.Abs,
                    accum_out=gacc[:, col:col + 1]))
                wt_guard[ti % 3] = ("act", last)
                ti += 1
        ctx.wait("dve", "act", last)
        rt = ctx.op("dve", lambda e, n=ntile, wi=widx: e.tensor_reduce(
            out=gcol[:, wi:wi + 1], in_=gacc[:, 0:n], axis=AXX, op=ALU.add))
        ctx.wait("pe", "dve", rt)
        with nc.psum_tensor(f"g_ps{widx}", [128, 16], F32) as g_ps:
            pm = ctx.op("pe", lambda e, wi=widx: e.matmul(
                g_ps[0:1, 0:1], ones_col_f[:], gcol[:, wi:wi + 1], start=True, stop=True))
            ctx.wait("dve", "pe", pm)
            t1 = ctx.op("dve", lambda e, wi=widx, n=numel: e.tensor_scalar(
                out=thr_sb[0:1, wi:wi + 1], in0=g_ps[0:1, 0:1],
                scalar1=0.5 / n, scalar2=0.5 * EPS_Q, op0=ALU.mult, op1=ALU.add))
            ctx.wait("pe", "dve", t1)
            pb = ctx.op("pe", lambda e, wi=widx: e.matmul(
                g_ps[:, 1:2], ones_row_f[:], thr_sb[0:1, wi:wi + 1], start=True, stop=True))
            ctx.wait("dve", "pe", pb)
            t2 = ctx.op("dve", lambda e, wi=widx: e.tensor_copy(
                out=thr_sb[:, wi:wi + 1], in_=g_ps[:, 1:2]))
            t3 = ctx.op("dve", lambda e, wi=widx: e.tensor_scalar(
                out=nthr_sb[:, wi:wi + 1], in0=g_ps[:, 1:2],
                scalar1=-1.0, scalar2=0.0, op0=ALU.mult, op1=ALU.bypass))
            ctx.wait("pe", "dve", t3)

    def p0_quant_one(widx):
        w, R, C, tgt, TW = WMATS[widx]
        nrt, nct = R // 128, C // TW
        dve_done = {}
        st_done = {}
        ti = 0
        for r in range(nrt):
            for c in range(nct):
                buf = wt_sb[ti % 3]
                if (ti % 3) in wt_guard:
                    ctx.wait("sync", *wt_guard[ti % 3])
                ld = ctx.dma("sync", f"p0w{ti % 3}", buf[:, 0:TW], w[r * 128:(r + 1) * 128, c * TW:(c + 1) * TW])
                ctx.wait("dve", f"p0w{ti % 3}", ld)
                qb = qq_sb[ti % 2]
                if (ti % 2) in qq_guard:
                    ctx.wait("dve", f"p0s{ti % 2}", qq_guard[ti % 2])
                ctx.op("dve", lambda e, b=buf, wi=widx, TW=TW: e.tensor_scalar(
                    out=bq_sb[:, 0:TW], in0=b[:, 0:TW], scalar1=nthr_sb[:, wi:wi + 1], scalar2=-1.0,
                    op0=ALU.is_gt, op1=ALU.add))
                dve_done[ti] = ctx.op("dve", lambda e, b=buf, wi=widx, qb=qb, TW=TW: e.scalar_tensor_tensor(
                    out=qb[:, 0:TW], in0=b[:, 0:TW], scalar=thr_sb[:, wi:wi + 1], in1=bq_sb[:, 0:TW],
                    op0=ALU.is_ge, op1=ALU.add))
                wt_guard[ti % 3] = ("dve", dve_done[ti])
                ctx.wait("gp", "dve", dve_done[ti])
                qq_guard[ti % 2] = ctx.dma("gp", f"p0s{ti % 2}", tgt[r, :, c * TW:(c + 1) * TW], qb[:, 0:TW])
                ti += 1

    for widx in range(3):
        p0_gamma_one(widx)
    for widx in range(3):
        p0_quant_one(widx)
    if debug:
        ctx.wait("gp", "dve", ctx.cnt["dve"])
        ctx.dma("gp", "dbg", dbg["thr_dbg"], thr_sb[:, 0:3])
    ctx.barrier()
    ctx.emit_block("P0")
    if upto < 1:
        kvs.close(); p0s.close(); ctx.es.close()
        return nc

    with ExitStack() as p1s:
        x_sb = [p1s.enter_context(nc.sbuf_tensor(f"x_sb{i}", [128, NCHK * T1], F32)) for i in range(2)]
        xb_sb = [p1s.enter_context(nc.sbuf_tensor(f"xb_sb{i}", [128, NCHK * T1], BF16)) for i in range(2)]
        xn1_sb = p1s.enter_context(nc.sbuf_tensor("xn1_sb", [128, NCHK * T1], BF16))
        sq_sb = [p1s.enter_context(nc.sbuf_tensor(f"sq_sb{i}", [128, T1], BF16)) for i in range(2)]
        wq_sl = [p1s.enter_context(nc.sbuf_tensor(f"wq_sl{i}", [128, NCHK * 128], BF16)) for i in range(NH)]
        wk_sl = [p1s.enter_context(nc.sbuf_tensor(f"wk_sl{i}", [128, NCHK * 128], BF16)) for i in range(NKV)]
        wv_sl = p1s.enter_context(nc.sbuf_tensor("wv_sl", [128, NCHK * NKV * DH], BF16))
        mrow = p1s.enter_context(nc.sbuf_tensor("mrow", [1, T1], F32))
        vrow = p1s.enter_context(nc.sbuf_tensor("vrow", [1, T1], F32))
        rrow = p1s.enter_context(nc.sbuf_tensor("rrow", [1, T1], F32))
        mb_sb = p1s.enter_context(nc.sbuf_tensor("mb_sb", [128, T1], F32))
        rb_sb = p1s.enter_context(nc.sbuf_tensor("rb_sb", [128, T1], F32))
        tmp_f = p1s.enter_context(nc.sbuf_tensor("tmp_f", [128, T1], F32))
        cs_sb = [p1s.enter_context(nc.sbuf_tensor(f"cs_sb{i}", [64, 2 * T1], F32)) for i in range(2)]
        rtm_sb = [p1s.enter_context(nc.sbuf_tensor(f"rtm_sb{i}", [64, T1], F32)) for i in range(4)]
        qrot_sb = [p1s.enter_context(nc.sbuf_tensor(f"qrot_sb{i}", [128, T1], BF16)) for i in range(2)]
        nsq_sb = p1s.enter_context(nc.sbuf_tensor("nsq_sb", [128, T1], BF16))
        ps = [p1s.enter_context(nc.psum_tensor(f"p1ps{i}", [128, 512], F32)) for i in range(8)]
        SUM, SUMSQ, MB, RB, K0, K1, QA, QB = range(8)

        for f in range(NH):
            split_dma(ctx, "gp", "p1w", wq_sl[f].ap().rearrange("p (c f) -> p c f", c=NCHK),
                      rearr_cpt(wqT[:, f * 128:(f + 1) * 128]))
        for kv in range(NKV):
            split_dma(ctx, "gp", "p1w", wk_sl[kv].ap().rearrange("p (c f) -> p c f", c=NCHK),
                      rearr_cpt(wkT[:, kv * 128:(kv + 1) * 128]))
        wld = split_dma(ctx, "gp", "p1w", wv_sl.ap().rearrange("p (c f) -> p c f", c=NCHK),
                        rearr_cpt(wvT[:, :]))

        prev = {}
        knorm_i = [0]

        import os as _os2
        _P1STAGE = int(_os2.environ.get("KERN_P1STAGE", "9"))

        def tile_p1(tau):
            t0 = tau * T1
            x_ = x_sb[tau % 2]
            xb_ = xb_sb[tau % 2]
            cs = cs_sb[tau % 2]
            if tau >= 2:
                ctx.wait("sync", "dve", prev[("xdone", tau - 2)])
                ctx.wait("sync", "dve", prev[("csdone", tau - 2)])
                ctx.wait("gp", "pe", prev[("xb_pe", tau - 2)])
                ctx.wait("gp", "act", prev[("xb_act", tau - 2)])
            if ("nmred",) in prev:
                ctx.wait("pe", "dve", prev[("nmred",)])
            xl = split_dma(ctx, "sync", f"p1x{tau % 2}", x_.ap().rearrange("p (c t) -> p c t", c=NCHK),
                           rearr_cpt(xT[:, t0:t0 + T1]))
            cl1 = ctx.dma("sync", f"p1c{tau % 2}", cs[:, 0:T1], costab[:, t0:t0 + T1])
            cl2 = ctx.dma("sync", f"p1c{tau % 2}", cs[:, T1:2 * T1], sintab[:, t0:t0 + T1])
            xbl = split_dma(ctx, "gp", f"p1b{tau % 2}", xb_.ap().rearrange("p (c t) -> p c t", c=NCHK),
                            rearr_cpt(xT[:, t0:t0 + T1]))

            ctx.wait("pe", f"p1b{tau % 2}", xbl)
            for c in range(NCHK):
                ctx.op("pe", lambda e, c=c: e.matmul(
                    ps[SUM][0:1, 0:T1], ones_col_bf[:], xb_[:, c * T1:(c + 1) * T1],
                    start=(c == 0), stop=(c == NCHK - 1)), inc=(c == NCHK - 1))
            ctx.wait("act", f"p1b{tau % 2}", xbl)
            sqmm = {}
            for c in range(NCHK):
                sq = sq_sb[c % 2]
                if c >= 2:
                    ctx.wait("act", "pe", sqmm[c - 2])
                sa = ctx.op("act", lambda e, c=c, sq=sq: e.activation(
                    out=sq[:], in_=xb_[:, c * T1:(c + 1) * T1], func=AF.Square))
                ctx.wait("pe", "act", sa)
                sqmm[c] = ctx.op("pe", lambda e, c=c, sq=sq: e.matmul(
                    ps[SUMSQ][0:1, 0:T1], ones_col_bf[:], sq[:],
                    start=(c == 0), stop=(c == NCHK - 1)))
            stats_pe = ctx.cnt["pe"]
            prev[("xb_pe", tau)] = stats_pe
            prev[("xb_act", tau)] = ctx.cnt["act"]
            if _P1STAGE <= 1:
                prev[("xdone", tau)] = ctx.cnt["dve"]
                prev[("csdone", tau)] = ctx.cnt["dve"]
                return

            ctx.wait("dve", "pe", stats_pe)
            ctx.op("dve", lambda e: e.tensor_scalar(
                out=mrow[:], in0=ps[SUM][0:1, 0:T1], scalar1=1.0 / D, scalar2=0.0,
                op0=ALU.mult, op1=ALU.bypass))
            ctx.op("dve", lambda e: e.tensor_scalar(
                out=vrow[:], in0=ps[SUMSQ][0:1, 0:T1], scalar1=1.0 / D, scalar2=EPS_LN,
                op0=ALU.mult, op1=ALU.add))
            ctx.op("dve", lambda e: e.tensor_tensor(
                out=rrow[:], in0=mrow[:], in1=mrow[:], op=ALU.mult))
            dv = ctx.op("dve", lambda e: e.tensor_tensor(
                out=vrow[:], in0=vrow[:], in1=rrow[:], op=ALU.subtract))
            ctx.wait("act", "dve", dv)
            ctx.op("act", lambda e: e.activation(
                out=rrow[:], in_=vrow[:], func=AF.Ln))
            ar = ctx.op("act", lambda e: e.activation(
                out=rrow[:], in_=rrow[:], func=AF.Exp, scale=-0.5))
            ctx.wait("pe", "act", ar)
            bm = ctx.op("pe", lambda e: e.matmul(ps[MB][:, 0:T1], ones_row_f[:], mrow[:],
                                                 start=True, stop=True))
            br = ctx.op("pe", lambda e: e.matmul(ps[RB][:, 0:T1], ones_row_f[:], rrow[:],
                                                 start=True, stop=True))
            ctx.wait("act", "pe", br)
            ctx.op("act", lambda e: e.activation(out=mb_sb[:], in_=ps[MB][:, 0:T1], func=AF.Copy))
            ab = ctx.op("act", lambda e: e.activation(out=rb_sb[:], in_=ps[RB][:, 0:T1], func=AF.Copy))
            ctx.wait("dve", "act", ab)
            ctx.wait("dve", f"p1x{tau % 2}", xl)
            for c in range(NCHK):
                ctx.op("dve", lambda e, c=c: e.tensor_tensor(
                    out=tmp_f[:], in0=x_[:, c * T1:(c + 1) * T1], in1=mb_sb[:], op=ALU.subtract))
                ctx.op("dve", lambda e, c=c: e.tensor_tensor(
                    out=xn1_sb[:, c * T1:(c + 1) * T1], in0=tmp_f[:], in1=rb_sb[:], op=ALU.mult))
            xn_done = ctx.cnt["dve"]
            prev[("xdone", tau)] = xn_done
            if _P1STAGE <= 2:
                prev[("csdone", tau)] = ctx.cnt["dve"]
                return

            def rope_emit(psrc, dst_even, dst_odd):
                cosA = cs[:, 0:T1]
                sinA = cs[:, T1:2 * T1]
                ctx.op("dve", lambda e: e.tensor_tensor(out=rtm_sb[0][:], in0=psrc[0:64, 0:T1], in1=cosA, op=ALU.mult))
                ctx.op("dve", lambda e: e.tensor_tensor(out=rtm_sb[1][:], in0=psrc[64:128, 0:T1], in1=sinA, op=ALU.mult))
                ctx.op("dve", lambda e: e.tensor_tensor(out=dst_even, in0=rtm_sb[0][:], in1=rtm_sb[1][:], op=ALU.subtract))
                ctx.op("dve", lambda e: e.tensor_tensor(out=rtm_sb[2][:], in0=psrc[0:64, 0:T1], in1=sinA, op=ALU.mult))
                ctx.op("dve", lambda e: e.tensor_tensor(out=rtm_sb[3][:], in0=psrc[64:128, 0:T1], in1=cosA, op=ALU.mult))
                return ctx.op("dve", lambda e: e.tensor_tensor(out=dst_odd, in0=rtm_sb[2][:], in1=rtm_sb[3][:], op=ALU.add))

            ctx.wait("pe", "dve", xn_done)
            if tau == 0:
                ctx.wait("pe", "p1w", wld)
            kmm = {}
            for kv in range(NKV):
                for c in range(NCHK):
                    ctx.op("pe", lambda e, kv=kv, c=c: e.matmul(
                        ps[K0 + kv][:, 0:T1], wk_sl[kv][:, c * 128:(c + 1) * 128],
                        xn1_sb[:, c * T1:(c + 1) * T1],
                        start=(c == 0), stop=(c == NCHK - 1)), inc=(c == NCHK - 1))
                kmm[kv] = ctx.cnt["pe"]
            ctx.wait("dve", f"p1c{tau % 2}", cl2)
            last_nm_red = prev.get(("nmred",), None)
            _sub = _os2.environ.get("KERN_P1SUB", "all")
            for kv in range(NKV):
                ctx.wait("dve", "pe", kmm[kv])
                if _sub in ("all", "rope"):
                    rp = rope_emit(ps[K0 + kv], k_sb[kv][0:64, t0:t0 + T1], k_sb[kv][64:128, t0:t0 + T1])
                else:
                    rp = ctx.op("dve", lambda e, kv=kv: e.tensor_copy(
                        out=k_sb[kv][:, t0:t0 + T1], in_=ps[K0 + kv][:, 0:T1]))
                if _sub in ("all", "norm", "normsq", "normmm"):
                    ctx.wait("act", "dve", rp)
                    sa = ctx.op("act", lambda e, kv=kv, t0=t0: e.activation(
                        out=nsq_sb[:], in_=k_sb[kv][:, t0:t0 + T1], func=AF.Square))
                    if _sub != "normsq":
                        ctx.wait("pe", "act", sa)
                        if last_nm_red is not None:
                            ctx.wait("pe", "dve", last_nm_red)
                        nm = ctx.op("pe", lambda e: e.matmul(ps[SUM][0:1, 0:T1], ones_col_bf[:], nsq_sb[:],
                                                             start=True, stop=True))
                        if _sub != "normmm":
                            ctx.wait("dve", "pe", nm)
                            idx = 128 + knorm_i[0]
                            knorm_i[0] += 1
                            last_nm_red = ctx.op("dve", lambda e, idx=idx: e.reduce_max(
                                out=maxbuf[0:1, idx:idx + 1], in_=ps[SUM][0:1, 0:T1], axis=AXX))
                        ctx.wait("act", "pe", nm)
            prev[("nmred",)] = last_nm_red
            if _P1STAGE <= 3:
                prev[("csdone", tau)] = ctx.cnt["dve"]
                return

            for ts in range(T1 // 128):
                vp = ps[MB + ts]
                for c in range(NCHK):
                    ctx.op("pe", lambda e, ts=ts, c=c, vp=vp: e.matmul(
                        vp[:, 0:NKV * DH],
                        xn1_sb[:, c * T1 + ts * 128: c * T1 + ts * 128 + 128],
                        wv_sl[:, c * NKV * DH:(c + 1) * NKV * DH],
                        start=(c == 0), stop=(c == NCHK - 1)), inc=(c == NCHK - 1))
                vm = ctx.cnt["pe"]
                ctx.wait("act", "pe", vm)
                ctx.op("act", lambda e, ts=ts, vp=vp: e.activation(
                    out=v_sb[(t0 + ts * 128) // 128][:], in_=vp[:, 0:NKV * DH], func=AF.Copy))

            if _P1STAGE <= 4:
                prev[("csdone", tau)] = ctx.cnt["dve"]
                return
            for f in range(NH):
                qi = tau * NH + f
                qp = ps[QA + (f % 2)]
                if qi >= 2:
                    ctx.wait("pe", "dve", prev[("qrope", qi - 2)])
                    ctx.wait("pe", "act", prev[("qsq", qi - 2)])
                for c in range(NCHK):
                    ctx.op("pe", lambda e, f=f, c=c, qp=qp: e.matmul(
                        qp[:, 0:T1], wq_sl[f][:, c * 128:(c + 1) * 128],
                        xn1_sb[:, c * T1:(c + 1) * T1],
                        start=(c == 0), stop=(c == NCHK - 1)), inc=(c == NCHK - 1))
                qmm = ctx.cnt["pe"]
                qd = qrot_sb[f % 2]
                if qi >= 2:
                    ctx.wait("dve", f"p1q{qi % 2}", prev[("qst", qi - 2)])
                ctx.wait("dve", "pe", qmm)
                prev[("qrope", qi)] = rope_emit(qp, qd[0:64, :], qd[64:128, :])
                ctx.wait("act", "dve", prev[("qrope", qi)])
                sa = ctx.op("act", lambda e, qd=qd: e.activation(
                    out=nsq_sb[:], in_=qd[:], func=AF.Square))
                prev[("qsq", qi)] = sa
                ctx.wait("pe", "act", sa)
                ctx.wait("pe", "dve", last_nm_red)
                nm = ctx.op("pe", lambda e: e.matmul(ps[SUM][0:1, 0:T1], ones_col_bf[:], nsq_sb[:],
                                                     start=True, stop=True))
                ctx.wait("dve", "pe", nm)
                last_nm_red = ctx.op("dve", lambda e, qi=qi: e.reduce_max(
                    out=maxbuf[0:1, qi:qi + 1], in_=ps[SUM][0:1, 0:T1], axis=AXX))
                ctx.wait("act", "pe", nm)
                ctx.wait("gp", "dve", ctx.cnt["dve"])
                prev[("qst", qi)] = ctx.dma("gp", f"p1q{qi % 2}", q_sc[f, :, t0:t0 + T1], qd[:])
            prev[("nmred",)] = last_nm_red
            prev[("csdone", tau)] = ctx.cnt["dve"]

        import os as _os
        _nt = int(_os.environ.get("KERN_NT1", str(NT1)))
        for tau in range(_nt):
            tile_p1(tau)

        ctx.op("dve", lambda e: e.reduce_max(out=maxbuf[0:1, 190:191], in_=maxbuf[0:1, 0:128], axis=AXX))
        ctx.op("dve", lambda e: e.reduce_max(out=maxbuf[0:1, 191:192], in_=maxbuf[0:1, 128:160], axis=AXX))
        mm2 = ctx.op("dve", lambda e: e.tensor_tensor(out=maxbuf[0:1, 189:190], in0=maxbuf[0:1, 190:191],
                                                      in1=maxbuf[0:1, 191:192], op=ALU.mult))
        ctx.wait("act", "dve", mm2)
        sq2 = ctx.op("act", lambda e: e.activation(out=maxbuf[0:1, 188:189], in_=maxbuf[0:1, 189:190],
                                                   func=AF.Sqrt))
        ctx.wait("dve", "act", sq2)
        ng = ctx.op("dve", lambda e: e.tensor_scalar(
            out=maxbuf[0:1, 187:188], in0=maxbuf[0:1, 188:189],
            scalar1=-BOUND_INFLATE * LAM, scalar2=0.0, op0=ALU.mult, op1=ALU.bypass))
        ctx.wait("pe", "dve", ng)
        bm2 = ctx.op("pe", lambda e: e.matmul(ps[MB][:, 0:1], ones_row_f[:], maxbuf[0:1, 187:188],
                                              start=True, stop=True))
        ctx.wait("act", "pe", bm2)
        ctx.op("act", lambda e: e.activation(out=negm_sb[:], in_=ps[MB][:, 0:1], func=AF.Copy))

        if debug:
            ctx.wait("gp", "act", ctx.cnt["act"])
            ctx.wait("gp", "dve", ctx.cnt["dve"])
            ctx.wait("gp", "p1q0", ctx.cnt.get("p1q0", 0))
            ctx.wait("gp", "p1q1", ctx.cnt.get("p1q1", 0))
            ctx.dma("gp", "dbg", dbg["negm_dbg"], negm_sb[:])
            for kv in range(NKV):
                ctx.dma("gp", "dbg", dbg["k_dbg"][kv * 128:(kv + 1) * 128, :], k_sb[kv][:])
            for i in range(S // 128):
                ctx.dma("gp", "dbg", dbg["v_dbg"][i], v_sb[i][:])
            for f in range(NH):
                ctx.dma("gp", "dbg", dbg["q_dbg"][f], q_sc[f])
        ctx.barrier()
        ctx.emit_block("P1")
    if upto < 3:
        kvs.close(); p0s.close(); ctx.es.close()
        return nc

    with ExitStack() as p3s:
        q_tl = [p3s.enter_context(nc.sbuf_tensor(f"q_tl{i}", [128, 512], BF16)) for i in range(2)]
        p_tl = [p3s.enter_context(nc.sbuf_tensor(f"p_tl{i}", [128, 1024], BF16)) for i in range(2)]
        inv_row = p3s.enter_context(nc.sbuf_tensor("inv_row", [1, 512], F32))
        invb_sb = [p3s.enter_context(nc.sbuf_tensor(f"invb_sb{i}", [128, 512], F32)) for i in range(2)]
        ao_sb = [p3s.enter_context(nc.sbuf_tensor(f"ao_sb{i}", [128, 512], BF16)) for i in range(2)]
        sps2 = [p3s.enter_context(nc.psum_tensor(f"sps2_{i}", [128, 1024], F32)) for i in range(2)]
        ops_ = [p3s.enter_context(nc.psum_tensor(f"ops{i}", [128, 512], F32)) for i in range(2)]
        rsib = [p3s.enter_context(nc.psum_tensor(f"rsib{i}", [128, 512], F32)) for i in range(2)]

        pm = {}
        ui = 0
        for h in range(NH):
            kv = h // 4
            for j in range(8):
                qt = q_tl[ui % 2]
                if ui >= 2:
                    ctx.wait("sync", "pe", pm[("qtile_pe", ui - 2)])
                qld = ctx.dma("sync", f"p3q{ui % 2}", qt[:], q_sc[h, :, j * 512:(j + 1) * 512])
                op_ = ops_[ui % 2]
                rp_ = rsib[ui % 2]
                if ui >= 2:
                    ctx.wait("pe", "dve", pm[("oev", ui - 2)])
                items = [("pair", pi) for pi in range(2 * j)] + [("diag", di) for di in range(4)]
                nit = len(items)
                exp_done = {}
                pend = None

                def consume(p):
                    ctx.wait("pe", p[0], p[1])
                    p[2]()
                    return ctx.cnt["pe"]

                for ci, (kind, idx) in enumerate(items):
                    sp = sps2[ci % 2]
                    if ci == 0:
                        ctx.wait("pe", f"p3q{ui % 2}", qld)
                    if ci >= 2:
                        ctx.wait("pe", "act", exp_done[ci - 2])
                    first = (ci == 0)
                    lastit = (ci == nit - 1)
                    pt = p_tl[ci % 2]
                    if kind == "pair":
                        k0, k1 = 2 * idx, 2 * idx + 1
                        ctx.op("pe", lambda e, sp=sp, kv=kv, k0=k0, qt=qt: e.matmul(
                            sp[:, 0:512], k_sb[kv][:, k0 * 128:(k0 + 1) * 128], qt[:],
                            start=True, stop=True), inc=False)
                        ctx.op("pe", lambda e, sp=sp, kv=kv, k1=k1, qt=qt: e.matmul(
                            sp[:, 512:1024], k_sb[kv][:, k1 * 128:(k1 + 1) * 128], qt[:],
                            start=True, stop=True), inc=True)
                        smm = ctx.cnt["pe"]
                        if ci >= 2:
                            ctx.wait("act", "pe", pm[("pcons", ui, ci - 2)])
                        ctx.wait("act", "pe", smm)
                        ed = ctx.op("act", lambda e, sp=sp, pt=pt: e.activation(
                            out=pt[:], in_=sp[:], func=AF.Exp, bias=negm_sb[:], scale=LAM),
                            serial=False)
                        exp_done[ci] = ed

                        def emit_cons(pt=pt, k0=k0, k1=k1, first=first, lastit=lastit,
                                      rp_=rp_, op_=op_, h=h):
                            ctx.op("pe", lambda e: e.matmul(
                                rp_[0:1, 0:512], ones_col_bf[:], pt[:, 0:512],
                                start=first, stop=False), inc=False)
                            ctx.op("pe", lambda e: e.matmul(
                                rp_[0:1, 0:512], ones_col_bf[:], pt[:, 512:1024],
                                start=False, stop=False), inc=False)
                            ctx.op("pe", lambda e: e.matmul(
                                op_[:, 0:512], v_sb[k0][:, (h // 4) * DH:(h // 4 + 1) * DH], pt[:, 0:512],
                                start=first, stop=False), inc=False)
                            ctx.op("pe", lambda e: e.matmul(
                                op_[:, 0:512], v_sb[k1][:, (h // 4) * DH:(h // 4 + 1) * DH], pt[:, 512:1024],
                                start=False, stop=lastit), inc=True)
                        ready = ("act", ed)
                    else:
                        di = idx
                        kc = 4 * j + di
                        qof, N = 128 * di, 512 - 128 * di
                        ctx.op("pe", lambda e, sp=sp, kv=kv, kc=kc, qt=qt, qof=qof, N=N: e.matmul(
                            sp[:, 0:N], k_sb[kv][:, kc * 128:(kc + 1) * 128], qt[:, qof:qof + N],
                            start=True, stop=True), inc=True)
                        smm = ctx.cnt["pe"]
                        if ci >= 2:
                            ctx.wait("act", "pe", pm[("pcons", ui, ci - 2)])
                        ctx.wait("act", "pe", smm)
                        ed = ctx.op("act", lambda e, sp=sp, pt=pt, N=N: e.activation(
                            out=pt[:, 0:N], in_=sp[:, 0:N], func=AF.Exp, bias=negm_sb[:], scale=LAM),
                            serial=False)
                        exp_done[ci] = ed
                        ctx.wait("dve", "act", ed)
                        md = ctx.op("dve", lambda e, pt=pt: e.tensor_tensor(
                            out=pt[:, 0:128], in0=pt[:, 0:128], in1=tri_sb[:], op=ALU.mult))

                        def emit_cons(pt=pt, kc=kc, qof=qof, N=N, first=first, lastit=lastit,
                                      rp_=rp_, op_=op_, h=h):
                            ctx.op("pe", lambda e: e.matmul(
                                rp_[0:1, qof:qof + N], ones_col_bf[:], pt[:, 0:N],
                                start=first, stop=lastit), inc=False)
                            ctx.op("pe", lambda e: e.matmul(
                                op_[:, qof:qof + N], v_sb[kc][:, (h // 4) * DH:(h // 4 + 1) * DH], pt[:, 0:N],
                                start=first, stop=lastit), inc=True)
                        ready = ("dve", md)
                    if pend is not None:
                        pm[("pcons", ui, ci - 1)] = consume(pend)
                    pend = (ready[0], ready[1], emit_cons)
                pm[("pcons", ui, nit - 1)] = consume(pend)
                last_score = ctx.cnt["pe"]
                pm[("qtile_pe", ui)] = last_score
                ctx.wait("dve", "pe", last_score)
                rc = ctx.op("dve", lambda e, rp_=rp_: e.reciprocal(out=inv_row[:], in_=rp_[0:1, 0:512]))
                ctx.wait("pe", "dve", rc)
                bc = ctx.op("pe", lambda e, rp_=rp_: e.matmul(rp_[:, 0:512], ones_row_f[:], inv_row[:],
                                                             start=True, stop=True), inc=True)
                ctx.wait("dve", "pe", bc)
                iv = invb_sb[ui % 2]
                ctx.op("dve", lambda e, rp_=rp_, iv=iv: e.tensor_copy(out=iv[:], in_=rp_[:, 0:512]))
                pm[("ibcons", ui)] = ctx.cnt["dve"]
                ao = ao_sb[ui % 2]
                if ui >= 2:
                    ctx.wait("dve", f"p3s{ui % 2}", pm[("aost", ui - 2)])
                ev = ctx.op("dve", lambda e, op_=op_, iv=iv, ao=ao: e.tensor_tensor(
                    out=ao[:], in0=op_[:, 0:512], in1=iv[:], op=ALU.mult))
                pm[("oev", ui)] = ev
                pm[("rscons", ui)] = pm[("ibcons", ui)]
                if ui >= 1:
                    pass
                ctx.wait("gp", "dve", ev)
                pm[("aost", ui)] = ctx.dma("gp", f"p3s{ui % 2}", cin[h * DH:(h + 1) * DH, j * 512:(j + 1) * 512], ao[:])
                ui += 1

        cc = ctx.sem["cc"]
        ctx.wait("gp", "p3s0", ctx.cnt.get("p3s0", 0))
        ctx.wait("gp", "p3s1", ctx.cnt.get("p3s1", 0))
        ctx.cnt["cc"] += 1
        cctgt = ctx.cnt["cc"]
        ctx.E("gp", lambda e: e.collective_compute(
            "AllGather", ALU.bypass, ins=[cin.opt()], outs=[cout.opt()],
            replica_groups=[[0, 1], [2, 3], [4, 5], [6, 7]]).then_inc(cc, 1))
        ctx.wait("gp", "cc", cctgt)
        if debug:
            ctx.dma("gp", "dbg", dbg["ao_dbg"], cout)
        ctx.barrier()
        ctx.emit_block("P3")
    kvs.close()
    if upto < 4:
        p0s.close(); ctx.es.close()
        return nc

    ao_my = nc.dram_tensor("ao_my", [2 * NH * DH, HALF], BF16, kind="Internal").ap()

    ctx._mksem("p4g")

    def _gather_my_half(e):
        pid = nc.partition_id([mybir.EngineType.Pool])
        off = (pid % 2) * HALF
        h = ctx.sem["p4g"]
        for i in range(4):
            e.dma_start(out=ao_my[i * 512:(i + 1) * 512, :],
                        in_=cout[i * 512:(i + 1) * 512, bass.ds(off, HALF)]).then_inc(h, 16)
    ctx.cnt["p4g"] += 4 * 16
    aomyld = ctx.cnt["p4g"]
    ctx.E("gp", _gather_my_half)
    ctx.wait("gp", "p4g", aomyld)

    with ExitStack() as p4s:
        wo_sl = [p4s.enter_context(nc.sbuf_tensor(f"wo_sl{i}", [128, NCHK * 128], BF16)) for i in range(NCHK)]
        ao4_sb = p4s.enter_context(nc.sbuf_tensor("ao4_sb", [128, 16 * 512], BF16))
        x4_sb = p4s.enter_context(nc.sbuf_tensor("x4_sb", [128, 16 * 512], F32))
        y4_sb = p4s.enter_context(nc.sbuf_tensor("y4_sb", [128, 16 * 512], F32))
        yb_t = p4s.enter_context(nc.sbuf_tensor("yb_t", [128, 512], BF16))
        sq4_t = p4s.enter_context(nc.sbuf_tensor("sq4_t", [128, 512], BF16))
        m4 = p4s.enter_context(nc.sbuf_tensor("m4", [1, 512], F32))
        v4 = p4s.enter_context(nc.sbuf_tensor("v4", [1, 512], F32))
        r4 = p4s.enter_context(nc.sbuf_tensor("r4", [1, 512], F32))
        mb4 = p4s.enter_context(nc.sbuf_tensor("mb4", [128, 512], F32))
        rb4 = p4s.enter_context(nc.sbuf_tensor("rb4", [128, 512], F32))
        t4 = p4s.enter_context(nc.sbuf_tensor("t4", [128, 512], F32))
        xn2_t = p4s.enter_context(nc.sbuf_tensor("xn2_t", [128, 512], BF16))
        yps = [p4s.enter_context(nc.psum_tensor(f"yps{i}", [128, 512], F32)) for i in range(2)]
        s4ps = p4s.enter_context(nc.psum_tensor("s4ps", [1, 512], F32))
        q4ps = p4s.enter_context(nc.psum_tensor("q4ps", [1, 512], F32))
        b4ps = [p4s.enter_context(nc.psum_tensor(f"b4ps{i}", [128, 512], F32)) for i in range(2)]

        for do in range(NCHK):
            wol = split_dma(ctx, "gp", "p4w", wo_sl[do].ap().rearrange("p (c f) -> p c f", c=NCHK),
                            rearr_cpt(woT[:, do * 128:(do + 1) * 128]))

        for b in range(4):
            bc0 = b * 512
            if b >= 1:
                ctx.wait("gp", "pe", pm.get(("aocons", b - 1), 0))
                ctx.wait("sync", "dve", pm.get(("xcons", b - 1), 0))
            aol = None
            for f in range(16):
                aol = ctx.dma("gp", "p4a", ao4_sb[:, f * 512:(f + 1) * 512],
                              ao_my[f * 128:(f + 1) * 128, bc0:bc0 + 512])
            xl4 = split_dma(ctx, "sync", "p4x", x4_sb.ap().rearrange("p (c t) -> p c t", c=NCHK),
                            rearr_cpt(xTown[:, bc0:bc0 + 512]))
            ctx.wait("pe", "p4a", aol)
            if b == 0:
                ctx.wait("pe", "p4w", wol)
            ctx.wait("dve", "p4x", xl4)
            if b >= 1:
                ctx.wait("dve", "p4y", ctx.cnt.get("p4y", 0))
            for do in range(NCHK):
                yp = yps[do % 2]
                if do >= 2:
                    ctx.wait("pe", "dve", pm[("yev", b, do - 2)])
                for f in range(16):
                    ctx.op("pe", lambda e, yp=yp, do=do, f=f: e.matmul(
                        yp[:, 0:512], wo_sl[do][:, f * 128:(f + 1) * 128],
                        ao4_sb[:, f * 512:(f + 1) * 512],
                        start=(f == 0), stop=(f == 15)), inc=(f == 15))
                ym = ctx.cnt["pe"]
                ctx.wait("dve", "pe", ym)
                ev = ctx.op("dve", lambda e, yp=yp, do=do: e.tensor_tensor(
                    out=y4_sb[:, do * 512:(do + 1) * 512], in0=yp[:, 0:512],
                    in1=x4_sb[:, do * 512:(do + 1) * 512], op=ALU.add))
                pm[("yev", b, do)] = ev
                ctx.wait("gp", "dve", ev)
                ctx.dma("gp", "p4y", y_sc[do, :, bc0:bc0 + 512], y4_sb[:, do * 512:(do + 1) * 512])
                ctx.wait("act", "dve", ev)
                if do >= 1:
                    ctx.wait("act", "pe", pm[("stmm", b, do - 1)])
                ya = ctx.op("act", lambda e, do=do: e.activation(
                    out=yb_t[:], in_=y4_sb[:, do * 512:(do + 1) * 512], func=AF.Copy))
                sa = ctx.op("act", lambda e, do=do: e.activation(
                    out=sq4_t[:], in_=y4_sb[:, do * 512:(do + 1) * 512], func=AF.Square))
                ctx.wait("pe", "act", sa)
                ctx.op("pe", lambda e, do=do: e.matmul(
                    s4ps[0:1, 0:512], ones_col_bf[:], yb_t[:],
                    start=(do == 0), stop=(do == 15)), inc=False)
                ctx.op("pe", lambda e, do=do: e.matmul(
                    q4ps[0:1, 0:512], ones_col_bf[:], sq4_t[:],
                    start=(do == 0), stop=(do == 15)), inc=True)
                pm[("stmm", b, do)] = ctx.cnt["pe"]
            pm[("aocons", b)] = ctx.cnt["pe"]
            st4 = pm[("stmm", b, 15)]
            ctx.wait("dve", "pe", st4)
            ctx.op("dve", lambda e: e.tensor_scalar(out=m4[:], in0=s4ps[0:1, 0:512],
                                                    scalar1=1.0 / D, scalar2=0.0, op0=ALU.mult, op1=ALU.bypass))
            ctx.op("dve", lambda e: e.tensor_scalar(out=v4[:], in0=q4ps[0:1, 0:512],
                                                    scalar1=1.0 / D, scalar2=EPS_LN, op0=ALU.mult, op1=ALU.add))
            ctx.op("dve", lambda e: e.tensor_tensor(out=r4[:], in0=m4[:], in1=m4[:], op=ALU.mult))
            dv4 = ctx.op("dve", lambda e: e.tensor_tensor(out=v4[:], in0=v4[:], in1=r4[:], op=ALU.subtract))
            ctx.wait("act", "dve", dv4)
            ctx.op("act", lambda e: e.activation(out=r4[:], in_=v4[:], func=AF.Ln))
            ar4 = ctx.op("act", lambda e: e.activation(out=r4[:], in_=r4[:], func=AF.Exp, scale=-0.5))
            ctx.wait("pe", "act", ar4)
            bm4 = ctx.op("pe", lambda e: e.matmul(b4ps[0][:, 0:512], ones_row_f[:], m4[:], start=True, stop=True))
            br4 = ctx.op("pe", lambda e: e.matmul(b4ps[1][:, 0:512], ones_row_f[:], r4[:], start=True, stop=True))
            ctx.wait("act", "pe", br4)
            ctx.op("act", lambda e: e.activation(out=mb4[:], in_=b4ps[0][:, 0:512], func=AF.Copy))
            ab4 = ctx.op("act", lambda e: e.activation(out=rb4[:], in_=b4ps[1][:, 0:512], func=AF.Copy))
            ctx.wait("dve", "act", ab4)
            for do in range(NCHK):
                ctx.op("dve", lambda e, do=do: e.tensor_tensor(
                    out=t4[:], in0=y4_sb[:, do * 512:(do + 1) * 512], in1=mb4[:], op=ALU.subtract))
                xd = ctx.op("dve", lambda e, do=do: e.tensor_tensor(
                    out=xn2_t[:], in0=t4[:], in1=rb4[:], op=ALU.mult))
                ctx.wait("gp", "dve", xd)
                ctx.dma("gp", "p4n", xn2_sc[do, :, bc0:bc0 + 512], xn2_t[:])
                ctx.wait("dve", "p4n", ctx.cnt["p4n"])
            pm[("xcons", b)] = ctx.cnt["dve"]
        if debug:
            ctx.wait("gp", "dve", ctx.cnt["dve"])
            ctx.wait("gp", "p4y", ctx.cnt.get("p4y", 0))
            ctx.wait("gp", "p4n", ctx.cnt.get("p4n", 0))
            for doo in range(NCHK):
                ctx.dma("gp", "dbg", dbg["y_dbg"][doo], y_sc[doo])
                ctx.dma("gp", "dbg", dbg["xn2_dbg"][doo], xn2_sc[doo])
        ctx.barrier()
        ctx.emit_block("P4")
    p0s.close()
    if upto < 5:
        ctx.es.close()
        return nc

    with ExitStack() as p5s:
        xn5_sb = p5s.enter_context(nc.sbuf_tensor("xn5_sb", [128, NCHK * 1024], BF16))
        hid_sb = p5s.enter_context(nc.sbuf_tensor("hid_sb", [128, NHC_FF * 1024], BF16))
        w15 = [p5s.enter_context(nc.sbuf_tensor(f"w15_{i}", [128, NCHK * 128], BF16)) for i in range(2)]
        w25 = [p5s.enter_context(nc.sbuf_tensor(f"w25_{i}", [128, NCHK * 128], BF16)) for i in range(2)]
        w35 = [p5s.enter_context(nc.sbuf_tensor(f"w35_{i}", [128, NHC_FF * 128], BF16)) for i in range(2)]
        sig_t = [p5s.enter_context(nc.sbuf_tensor(f"sig_t{i}", [128, 512], BF16)) for i in range(2)]
        y5_t = [p5s.enter_context(nc.sbuf_tensor(f"y5_t{i}", [128, 512], F32)) for i in range(2)]
        o5_t = [p5s.enter_context(nc.sbuf_tensor(f"o5_t{i}", [128, 512], F32)) for i in range(2)]
        x1ps = [p5s.enter_context(nc.psum_tensor(f"x1ps{i}", [128, 512], F32)) for i in range(2)]
        x2ps = [p5s.enter_context(nc.psum_tensor(f"x2ps{i}", [128, 512], F32)) for i in range(2)]
        yops = [p5s.enter_context(nc.psum_tensor(f"yops{i}", [128, 512], F32)) for i in range(2)]

        for sti in range(2):
            sc0 = sti * 1024
            if sti >= 1:
                ctx.wait("sync", "pe", pm[("xncons", sti - 1)])
            for c in range(NCHK):
                xnl = ctx.dma("sync", "p5x", xn5_sb[:, c * 1024:(c + 1) * 1024],
                              xn2_sc[c, :, sc0:sc0 + 1024])
            for hc in range(NHC_FF):
                gi = sti * NHC_FF + hc
                w1b, w2b = w15[hc % 2], w25[hc % 2]
                if gi >= 2:
                    ctx.wait("sync", "pe", pm[("wAcons", gi - 2)])
                split_dma(ctx, "sync", f"p5w{hc % 2}", w1b.ap().rearrange("p (c h) -> p c h", c=NCHK),
                          q1_sc[:, :, hc * 128:(hc + 1) * 128].rearrange("c p h -> p c h"))
                wl = split_dma(ctx, "sync", f"p5w{hc % 2}", w2b.ap().rearrange("p (c h) -> p c h", c=NCHK),
                               q2_sc[:, :, hc * 128:(hc + 1) * 128].rearrange("c p h -> p c h"))
                for th in range(2):
                    if gi >= 1:
                        ctx.wait("pe", "act", pm[("sig", gi - 1, th)])
                        ctx.wait("pe", "dve", pm[("hid", gi - 1, th)])
                ctx.wait("pe", f"p5w{hc % 2}", wl)
                if hc == 0:
                    ctx.wait("pe", "p5x", xnl)
                for c in range(NCHK):
                    for th in range(2):
                        ctx.op("pe", lambda e, c=c, th=th, w1b=w1b: e.matmul(
                            x1ps[th][:, 0:512], w1b[:, c * 128:(c + 1) * 128],
                            xn5_sb[:, c * 1024 + th * 512: c * 1024 + th * 512 + 512],
                            start=(c == 0), stop=(c == NCHK - 1)),
                            inc=(c == NCHK - 1 and th == 1))
                x1m = ctx.cnt["pe"]
                for c in range(NCHK):
                    for th in range(2):
                        ctx.op("pe", lambda e, c=c, th=th, w2b=w2b: e.matmul(
                            x2ps[th][:, 0:512], w2b[:, c * 128:(c + 1) * 128],
                            xn5_sb[:, c * 1024 + th * 512: c * 1024 + th * 512 + 512],
                            start=(c == 0), stop=(c == NCHK - 1)),
                            inc=(c == NCHK - 1 and th == 1))
                x2m = ctx.cnt["pe"]
                for th in range(2):
                    sg = sig_t[th]
                    ctx.wait("act", "pe", x1m)
                    if gi >= 1:
                        ctx.wait("act", "dve", pm[("hid", gi - 1, th)])
                    pm[("sig", gi, th)] = ctx.op("act", lambda e, th=th, sg=sg: e.activation(
                        out=sg[:], in_=x1ps[th][:, 0:512], func=AF.Sigmoid), serial=False)
                    ctx.wait("dve", "act", pm[("sig", gi, th)])
                    ctx.wait("dve", "pe", x2m)
                    pm[("hid", gi, th)] = ctx.op("dve", lambda e, th=th, sg=sg, hc=hc: e.tensor_tensor(
                        out=hid_sb[:, hc * 1024 + th * 512: hc * 1024 + th * 512 + 512],
                        in0=sg[:], in1=x2ps[th][:, 0:512], op=ALU.mult))
                pm[("wAcons", gi)] = ctx.cnt["pe"]
            pm[("xncons", sti)] = ctx.cnt["pe"]
            hid_done = pm[("hid", sti * NHC_FF + NHC_FF - 1, 1)]

            for c in range(NCHK):
                gci = sti * NCHK + c
                w3b = w35[c % 2]
                if gci >= 2:
                    ctx.wait("sync", "pe", pm[("wBcons", gci - 2)])
                w3l = split_dma(ctx, "sync", f"p5v{c % 2}", w3b.ap().rearrange("p (c d) -> p c d", c=NHC_FF),
                                q3_sc[:, :, c * 128:(c + 1) * 128].rearrange("c p d -> p c d"))
                for th in range(2):
                    if gci >= 1:
                        ctx.wait("sync", "dve", pm[("oev5", gci - 1, th)])
                    yl = ctx.dma("sync", f"p5y{th}", y5_t[th][:], y_sc[c, :, sc0 + th * 512: sc0 + th * 512 + 512])
                    pm[("yl5", gci, th)] = yl
                    if gci >= 1:
                        ctx.wait("pe", "dve", pm[("oev5", gci - 1, th)])
                ctx.wait("pe", f"p5v{c % 2}", w3l)
                if c == 0 and sti == sti:
                    ctx.wait("pe", "dve", hid_done)
                for hcb in range(NHC_FF):
                    for th in range(2):
                        ctx.op("pe", lambda e, hcb=hcb, th=th, w3b=w3b: e.matmul(
                            yops[th][:, 0:512], w3b[:, hcb * 128:(hcb + 1) * 128],
                            hid_sb[:, hcb * 1024 + th * 512: hcb * 1024 + th * 512 + 512],
                            start=(hcb == 0), stop=(hcb == NHC_FF - 1)),
                            inc=(hcb == NHC_FF - 1 and th == 1))
                ym5 = ctx.cnt["pe"]
                for th in range(2):
                    ctx.wait("dve", "pe", ym5)
                    ctx.wait("dve", f"p5y{th}", pm[("yl5", gci, th)])
                    if gci >= 1:
                        ctx.wait("dve", f"p5o{th}", pm[("ost5", gci - 1, th)])
                    ev5 = ctx.op("dve", lambda e, th=th: e.tensor_tensor(
                        out=o5_t[th][:], in0=yops[th][:, 0:512], in1=y5_t[th][:], op=ALU.add))
                    pm[("oev5", gci, th)] = ev5
                    ctx.wait("gp", "dve", ev5)
                    pm[("ost5", gci, th)] = ctx.dma(
                        "gp", f"p5o{th}", out[c * 128:(c + 1) * 128, sc0 + th * 512: sc0 + th * 512 + 512],
                        o5_t[th][:])
                pm[("wBcons", gci)] = ctx.cnt["pe"]

        ctx.barrier()
        ctx.emit_block("P5")

    ctx.es.close()
    return nc


_PERM128 = np.concatenate([np.arange(0, 128, 2), np.arange(1, 128, 2)])


def _permute_heads_rows(W):
    n = W.shape[0] // 128
    idx = np.concatenate([h * 128 + _PERM128 for h in range(n)])
    return W[idx]


def _rope_tables():
    freqs = (1.0 / (10000.0 ** (np.arange(0, DH, 2, dtype=np.float32) /
                                np.float32(DH)))).astype(np.float32)
    ang = np.arange(S, dtype=np.float32)[None, :] * freqs[:, None]
    return np.cos(ang).astype(np.float32), np.sin(ang).astype(np.float32)


_NC_CACHE = {}


def kernel(L0_attn_x, L0_attn_Wq, L0_attn_Wk, L0_attn_Wv, L0_attn_Wo,
           L0_ffn_w1, L0_ffn_w2, L0_ffn_w3, _debug=False, _results=None):
    x = np.asarray(L0_attn_x, dtype=np.float32)
    Wq = np.asarray(L0_attn_Wq, dtype=np.float32)
    Wk = np.asarray(L0_attn_Wk, dtype=np.float32)
    Wv = np.asarray(L0_attn_Wv, dtype=np.float32)
    Wo = np.asarray(L0_attn_Wo, dtype=np.float32)
    w1 = np.asarray(L0_ffn_w1, dtype=np.float32)
    w2 = np.asarray(L0_ffn_w2, dtype=np.float32)
    w3 = np.asarray(L0_ffn_w3, dtype=np.float32)

    WqP = _permute_heads_rows(Wq)
    WkP = _permute_heads_rows(Wk)
    cost, sint = _rope_tables()
    trim = np.triu(np.ones((128, 128), dtype=np.float32)).astype(ml_dtypes.bfloat16)
    woT = np.ascontiguousarray(Wo.T)
    w1T = np.ascontiguousarray(w1.T)
    w2T = np.ascontiguousarray(w2.T)
    w3T = np.ascontiguousarray(w3.T)

    in_maps = []
    for c in range(8):
        b, g = c // 2, c % 2
        xTb = np.ascontiguousarray(x[b].T)
        in_maps.append({
            "xT": xTb,
            "xTown": np.ascontiguousarray(xTb[:, g * HALF:(g + 1) * HALF]),
            "wqT": np.ascontiguousarray(WqP[g * NH * DH:(g + 1) * NH * DH, :].T),
            "wkT": np.ascontiguousarray(WkP[g * NKV * DH:(g + 1) * NKV * DH, :].T),
            "wvT": np.ascontiguousarray(Wv[g * NKV * DH:(g + 1) * NKV * DH, :].T),
            "woT": woT,
            "w1T": w1T,
            "w2T": w2T,
            "w3T": w3T,
            "costab": cost,
            "sintab": sint,
            "tri": trim,
        })

    import os
    upto = int(os.environ.get("KERN_UPTO", "5"))
    key = ("prog", bool(_debug), upto)
    if key not in _NC_CACHE:
        _NC_CACHE[key] = build(debug=_debug, upto=upto)
    nc = _NC_CACHE[key]
    res = run_bass_kernel_spmd(nc, in_maps, core_ids=list(range(8)))
    if _results is not None:
        _results.extend(res.results)
    outp = np.empty((4, S, D), dtype=np.float32)
    for c in range(8):
        b, g = c // 2, c % 2
        outp[b, g * HALF:(g + 1) * HALF, :] = res.results[c]["out"].T
    return outp


# version 04 (final)
# speedup vs baseline: 1.3758x; 1.0760x over previous
import numpy as np
import ml_dtypes
import concourse.bass as bass
import concourse.mybir as mybir
from concourse.bass_utils import run_bass_kernel_spmd
from concourse.alu_op_type import AluOpType as ALU
from contextlib import ExitStack

F32 = mybir.dt.float32
BF16 = mybir.dt.bfloat16
AF = mybir.ActivationFunctionType
AXX = mybir.AxisListType.X

D = 2048
S = 4096
NH = 8
NKV = 2
DH = 128
HFF = 5632
NCHK = D // 128
T1 = 256
NT1 = S // T1
HALF = S // 2
LAM = float(1.0 / np.sqrt(128.0))
EPS_LN = 1e-5
EPS_Q = 1e-8
BOUND_INFLATE = 1.02
NHC_FF = HFF // 128

ENGS = ("sync", "gp", "pe", "act", "dve")
QSEM = {"sync": "ld", "gp": "gp"}
ESEM = {"pe": "pe", "act": "act", "dve": "dve"}


class Ctx:
    def __init__(self, nc, debug=False):
        self.nc = nc
        self.debug = debug
        self.streams = {e: [] for e in ENGS}
        self.es = ExitStack()
        self.sem = {}
        self.cnt = {}
        for name in ("pe", "act", "dve", "cc"):
            self._mksem(name)

    def _mksem(self, name):
        if name not in self.sem:
            self.sem[name] = self.es.enter_context(self.nc.semaphore(name))
            self.cnt[name] = 0
        return self.sem[name]

    def E(self, eng, fn):
        self.streams[eng].append(fn)

    def wait(self, eng, sem, val):
        if val is None or val <= 0:
            return
        h = self.sem[sem]
        self.E(eng, lambda e, h=h, v=val: e.wait_ge(h, v))

    def dma(self, qeng, sem, out, in_):
        h = self._mksem(sem)
        self.cnt[sem] += 16
        tgt = self.cnt[sem]
        self.E(qeng, lambda e, o=out, i=in_, h=h: e.dma_start(out=o, in_=i).then_inc(h, 16))
        return tgt

    def op(self, eng, fn, inc=True, serial=True):
        semname = ESEM[eng]
        if serial and eng in ("act", "dve") and self.cnt[semname] > 0:
            h0 = self.sem[semname]
            v0 = self.cnt[semname]
            self.E(eng, lambda e, h=h0, v=v0: e.wait_ge(h, v))
        if inc:
            self.cnt[semname] += 1
            tgt = self.cnt[semname]
            h = self.sem[semname]
            self.E(eng, lambda e, fn=fn, h=h: fn(e).then_inc(h, 1))
            return tgt
        self.E(eng, lambda e, fn=fn: fn(e))
        return self.cnt[semname]

    def barrier(self):
        counts = dict(self.cnt)
        own_engine_sem = {"pe": "pe", "act": "act", "dve": "dve"}
        for eng in ENGS:
            skip = own_engine_sem.get(eng)
            for s, v in counts.items():
                if s == skip or s == "cc":
                    continue
                self.wait(eng, s, v)

    def emit_block(self, name="phase"):
        nc = self.nc
        streams = self.streams
        import contextlib
        with nc.named_scope(name), nc.Block() as block:
            @block.sync
            def _(sync):
                for fn in streams["sync"]:
                    fn(sync)

            @block.gpsimd
            def _(gpsimd):
                for fn in streams["gp"]:
                    fn(gpsimd)

            @block.tensor
            def _(tensor):
                for fn in streams["pe"]:
                    fn(tensor)

            @block.scalar
            def _(scalar):
                for fn in streams["act"]:
                    fn(scalar)

            @block.vector
            def _(vector):
                for fn in streams["dve"]:
                    fn(vector)
        self.streams = {e: [] for e in ENGS}


def rearr_cpt(ap2d):
    return ap2d.rearrange("(c p) t -> p c t", p=128)


def split_dma(ctx, qeng, sem, dst3, src3, nsplit=4):
    C = dst3.shape[1]
    step = C // nsplit
    tgt = None
    for i in range(nsplit):
        tgt = ctx.dma(qeng, sem, dst3[:, i * step:(i + 1) * step, :],
                      src3[:, i * step:(i + 1) * step, :])
    return tgt


def build(debug=False, upto=5):
    nc = bass.Bass("TRN2", target_bir_lowering=False)

    xT = nc.dram_tensor("xT", [D, S], F32, kind="ExternalInput").ap()
    xTown = nc.dram_tensor("xTown", [D, HALF], F32, kind="ExternalInput").ap()
    wqT = nc.dram_tensor("wqT", [D, NH * DH], F32, kind="ExternalInput").ap()
    wkT = nc.dram_tensor("wkT", [D, NKV * DH], F32, kind="ExternalInput").ap()
    wvT = nc.dram_tensor("wvT", [D, NKV * DH], F32, kind="ExternalInput").ap()
    woT = nc.dram_tensor("woT", [D, D], F32, kind="ExternalInput").ap()
    w1T = nc.dram_tensor("w1T", [D, HFF], F32, kind="ExternalInput").ap()
    w2T = nc.dram_tensor("w2T", [D, HFF], F32, kind="ExternalInput").ap()
    w3T = nc.dram_tensor("w3T", [HFF, D], F32, kind="ExternalInput").ap()
    costab = nc.dram_tensor("costab", [64, S], F32, kind="ExternalInput").ap()
    sintab = nc.dram_tensor("sintab", [64, S], F32, kind="ExternalInput").ap()
    tri = nc.dram_tensor("tri", [128, 128], BF16, kind="ExternalInput").ap()
    out = nc.dram_tensor("out", [D, HALF], F32, kind="ExternalOutput").ap()

    q_sc = nc.dram_tensor("q_sc", [NH, 128, S], BF16, kind="Internal").ap()
    q1_sc = nc.dram_tensor("q1_sc", [NCHK, 128, HFF], BF16, kind="Internal").ap()
    q2_sc = nc.dram_tensor("q2_sc", [NCHK, 128, HFF], BF16, kind="Internal").ap()
    q3_sc = nc.dram_tensor("q3_sc", [NHC_FF, 128, D], BF16, kind="Internal").ap()
    cin = nc.dram_tensor("cin", [NH * DH, S], BF16, kind="Internal").ap()
    cout_t = nc.dram_tensor("cout", [2 * NH * DH, S], BF16, kind="Internal")
    cout = cout_t.ap()
    y_sc = nc.dram_tensor("y_sc", [NCHK, 128, HALF], F32, kind="Internal").ap()
    g_scr = nc.dram_tensor("g_scr", [128, 1], F32, kind="Internal").ap()
    g_scr2 = nc.dram_tensor("g_scr2", [1, 1], F32, kind="Internal").ap()
    xn2_sc = nc.dram_tensor("xn2_sc", [NCHK, 128, HALF], BF16, kind="Internal").ap()

    dbg = {}
    if debug:
        dbg["k_dbg"] = nc.dram_tensor("k_dbg", [NKV * 128, S], BF16, kind="ExternalOutput").ap()
        dbg["v_dbg"] = nc.dram_tensor("v_dbg", [S // 128, 128, NKV * DH], BF16, kind="ExternalOutput").ap()
        dbg["q_dbg"] = nc.dram_tensor("q_dbg", [NH, 128, S], BF16, kind="ExternalOutput").ap()
        dbg["negm_dbg"] = nc.dram_tensor("negm_dbg", [128, 1], F32, kind="ExternalOutput").ap()
        dbg["thr_dbg"] = nc.dram_tensor("thr_dbg", [128, 3], F32, kind="ExternalOutput").ap()
        dbg["ao_dbg"] = nc.dram_tensor("ao_dbg", [2 * NH * DH, S], BF16, kind="ExternalOutput").ap()
        dbg["y_dbg"] = nc.dram_tensor("y_dbg", [NCHK, 128, HALF], F32, kind="ExternalOutput").ap()
        dbg["xn2_dbg"] = nc.dram_tensor("xn2_dbg", [NCHK, 128, HALF], BF16, kind="ExternalOutput").ap()

    ctx = Ctx(nc, debug)
    es = ctx.es

    ones_col_bf = es.enter_context(nc.sbuf_tensor("ones_col_bf", [128, 1], BF16))
    ones_col_f = es.enter_context(nc.sbuf_tensor("ones_col_f", [128, 1], F32))
    ones_row_f = es.enter_context(nc.sbuf_tensor("ones_row_f", [1, 128], F32))
    negm_sb = es.enter_context(nc.sbuf_tensor("negm_sb", [128, 1], F32))
    maxbuf = es.enter_context(nc.sbuf_tensor("maxbuf", [1, 192], F32))
    p0s = ExitStack()
    wt_sb = [p0s.enter_context(nc.sbuf_tensor(f"wt_sb{i}", [128, 2048], F32)) for i in range(3)]
    dump_sb = p0s.enter_context(nc.sbuf_tensor("dump_sb", [128, 2048], BF16))
    gacc = p0s.enter_context(nc.sbuf_tensor("gacc", [128, 64], F32))
    gcol = p0s.enter_context(nc.sbuf_tensor("gcol", [128, 3], F32))
    grow = p0s.enter_context(nc.sbuf_tensor("grow", [1, 128], F32))
    gtot = p0s.enter_context(nc.sbuf_tensor("gtot", [1, 4], F32))
    thr_sb = p0s.enter_context(nc.sbuf_tensor("thr_sb", [128, 3], F32))
    nthr_sb = p0s.enter_context(nc.sbuf_tensor("nthr_sb", [128, 3], F32))
    bq_sb = p0s.enter_context(nc.sbuf_tensor("bq_sb", [128, 2048], BF16))
    qq_sb = [p0s.enter_context(nc.sbuf_tensor(f"qq_sb{i}", [128, 2048], BF16)) for i in range(2)]
    kvs = ExitStack()
    k_sb = [kvs.enter_context(nc.sbuf_tensor(f"k_sb{i}", [128, S], BF16)) for i in range(NKV)]
    v_sb = [kvs.enter_context(nc.sbuf_tensor(f"v_sb{i}", [128, NKV * DH], BF16)) for i in range(S // 128)]
    tri_sb = kvs.enter_context(nc.sbuf_tensor("tri_sb", [128, 128], BF16))

    ctx.op("dve", lambda e: e.memset(ones_col_bf[:], 1.0))
    ctx.op("dve", lambda e: e.memset(ones_col_f[:], 1.0))
    ctx.op("dve", lambda e: e.memset(ones_row_f[:], 1.0))
    ctx.op("dve", lambda e: e.memset(maxbuf[:], 0.0))
    ctx.dma("sync", "ldonce", tri_sb[:], tri)

    WMATS = [(w1T, D, HFF, q1_sc, 1408), (w2T, D, HFF, q2_sc, 1408), (w3T, HFF, D, q3_sc, 2048)]
    wt_guard = {}
    qq_guard = {}

    def p0_gamma_gen():
      for widx in range(3):
        w, R, C, _, TW = WMATS[widx]
        numel = R * C
        nrt, nct = R // 128, C // TW
        ntile = nrt * nct
        last = 0
        ti = 0
        ctx.wait("act", "dve", ctx.cnt["dve"])
        for r in range(nrt):
            for c in range(nct):
                buf = wt_sb[ti % 3]
                if (ti % 3) in wt_guard:
                    ctx.wait("sync", *wt_guard[ti % 3])
                ld = ctx.dma("sync", f"p0w{ti % 3}", buf[:, 0:TW], w[r * 128:(r + 1) * 128, c * TW:(c + 1) * TW])
                ctx.wait("act", f"p0w{ti % 3}", ld)
                last = ctx.op("act", lambda e, b=buf, col=ti, TW=TW: e.activation(
                    out=dump_sb[:, 0:TW], in_=b[:, 0:TW], func=AF.Abs,
                    accum_out=gacc[:, col:col + 1]))
                wt_guard[ti % 3] = ("act", last)
                ti += 1
                yield
        ctx.wait("dve", "act", last)
        rt = ctx.op("dve", lambda e, n=ntile, wi=widx: e.tensor_reduce(
            out=gcol[:, wi:wi + 1], in_=gacc[:, 0:n], axis=AXX, op=ALU.add))
        ctx.wait("gp", "dve", rt)
        s1 = ctx.dma("gp", "p0f", g_scr, gcol[:, widx:widx + 1])
        ctx.wait("gp", "p0f", s1)
        s2 = ctx.dma("gp", "p0f", grow[:], g_scr.rearrange("p one -> one p"))
        ctx.wait("dve", "p0f", s2)
        t2 = ctx.op("dve", lambda e, wi=widx: e.tensor_reduce(
            out=gtot[0:1, wi:wi + 1], in_=grow[:], axis=AXX, op=ALU.add))
        t3 = ctx.op("dve", lambda e, wi=widx, n=numel: e.tensor_scalar(
            out=gtot[0:1, wi:wi + 1], in0=gtot[0:1, wi:wi + 1],
            scalar1=0.5 / n, scalar2=0.5 * EPS_Q, op0=ALU.mult, op1=ALU.add))
        ctx.wait("gp", "dve", t3)
        s3 = ctx.dma("gp", "p0f", g_scr2, gtot[0:1, widx:widx + 1])
        ctx.wait("gp", "p0f", s3)
        s4 = ctx.dma("gp", "p0f", thr_sb[:, widx:widx + 1],
                     g_scr2.partition_broadcast(128))
        ctx.wait("dve", "p0f", s4)
        ctx.op("dve", lambda e, wi=widx: e.tensor_scalar(
            out=nthr_sb[:, wi:wi + 1], in0=thr_sb[:, wi:wi + 1],
            scalar1=-1.0, scalar2=0.0, op0=ALU.mult, op1=ALU.bypass))
        yield

    def p0_quant_gen():
      for widx in range(3):
        w, R, C, tgt, TW = WMATS[widx]
        nrt, nct = R // 128, C // TW
        dve_done = {}
        st_done = {}
        ti = 0
        for r in range(nrt):
            for c in range(nct):
                buf = wt_sb[ti % 3]
                if (ti % 3) in wt_guard:
                    ctx.wait("sync", *wt_guard[ti % 3])
                ld = ctx.dma("sync", f"p0w{ti % 3}", buf[:, 0:TW], w[r * 128:(r + 1) * 128, c * TW:(c + 1) * TW])
                ctx.wait("dve", f"p0w{ti % 3}", ld)
                qb = qq_sb[ti % 2]
                if (ti % 2) in qq_guard:
                    ctx.wait("dve", f"p0s{ti % 2}", qq_guard[ti % 2])
                ctx.op("dve", lambda e, b=buf, wi=widx, TW=TW: e.tensor_scalar(
                    out=bq_sb[:, 0:TW], in0=b[:, 0:TW], scalar1=nthr_sb[:, wi:wi + 1], scalar2=-1.0,
                    op0=ALU.is_gt, op1=ALU.add))
                dve_done[ti] = ctx.op("dve", lambda e, b=buf, wi=widx, qb=qb, TW=TW: e.scalar_tensor_tensor(
                    out=qb[:, 0:TW], in0=b[:, 0:TW], scalar=thr_sb[:, wi:wi + 1], in1=bq_sb[:, 0:TW],
                    op0=ALU.is_ge, op1=ALU.add))
                wt_guard[ti % 3] = ("dve", dve_done[ti])
                ctx.wait("gp", "dve", dve_done[ti])
                qq_guard[ti % 2] = ctx.dma("gp", f"p0s{ti % 2}", tgt[r, :, c * TW:(c + 1) * TW], qb[:, 0:TW])
                ti += 1
                yield

    gamma_gen = p0_gamma_gen()
    quant_gen = p0_quant_gen()

    def pump(gen, n):
        for _ in range(n):
            if next(gen, "DONE") == "DONE":
                return False
        return True
    ctx.barrier()
    ctx.emit_block("P0")
    if upto < 1:
        kvs.close(); p0s.close(); ctx.es.close()
        return nc

    with ExitStack() as p1s:
        x_sb = [p1s.enter_context(nc.sbuf_tensor(f"x_sb{i}", [128, NCHK * T1], F32)) for i in range(2)]
        xb_sb = [p1s.enter_context(nc.sbuf_tensor(f"xb_sb{i}", [128, NCHK * T1], BF16)) for i in range(2)]
        xn1_sb = p1s.enter_context(nc.sbuf_tensor("xn1_sb", [128, NCHK * T1], BF16))
        sq_sb = [p1s.enter_context(nc.sbuf_tensor(f"sq_sb{i}", [128, T1], BF16)) for i in range(2)]
        wq_sl = [p1s.enter_context(nc.sbuf_tensor(f"wq_sl{i}", [128, NCHK * 128], BF16)) for i in range(NH)]
        wk_sl = [p1s.enter_context(nc.sbuf_tensor(f"wk_sl{i}", [128, NCHK * 128], BF16)) for i in range(NKV)]
        wv_sl = p1s.enter_context(nc.sbuf_tensor("wv_sl", [128, NCHK * NKV * DH], BF16))
        mrow = p1s.enter_context(nc.sbuf_tensor("mrow", [1, T1], F32))
        vrow = p1s.enter_context(nc.sbuf_tensor("vrow", [1, T1], F32))
        rrow = p1s.enter_context(nc.sbuf_tensor("rrow", [1, T1], F32))
        mb_sb = p1s.enter_context(nc.sbuf_tensor("mb_sb", [128, T1], F32))
        rb_sb = p1s.enter_context(nc.sbuf_tensor("rb_sb", [128, T1], F32))
        tmp_f = p1s.enter_context(nc.sbuf_tensor("tmp_f", [128, T1], F32))
        cs_sb = [p1s.enter_context(nc.sbuf_tensor(f"cs_sb{i}", [64, 2 * T1], F32)) for i in range(2)]
        rtm_sb = [p1s.enter_context(nc.sbuf_tensor(f"rtm_sb{i}", [64, T1], F32)) for i in range(4)]
        qrot_sb = [p1s.enter_context(nc.sbuf_tensor(f"qrot_sb{i}", [128, T1], BF16)) for i in range(2)]
        nsq_sb = p1s.enter_context(nc.sbuf_tensor("nsq_sb", [128, T1], BF16))
        ps = [p1s.enter_context(nc.psum_tensor(f"p1ps{i}", [128, 512], F32)) for i in range(8)]
        SUM, SUMSQ, MB, RB, K0, K1, QA, QB = range(8)

        for f in range(NH):
            split_dma(ctx, "gp", "p1w", wq_sl[f].ap().rearrange("p (c f) -> p c f", c=NCHK),
                      rearr_cpt(wqT[:, f * 128:(f + 1) * 128]))
        for kv in range(NKV):
            split_dma(ctx, "gp", "p1w", wk_sl[kv].ap().rearrange("p (c f) -> p c f", c=NCHK),
                      rearr_cpt(wkT[:, kv * 128:(kv + 1) * 128]))
        wld = split_dma(ctx, "gp", "p1w", wv_sl.ap().rearrange("p (c f) -> p c f", c=NCHK),
                        rearr_cpt(wvT[:, :]))

        prev = {}
        knorm_i = [0]

        import os as _os2
        _P1STAGE = int(_os2.environ.get("KERN_P1STAGE", "9"))

        def tile_p1(tau):
            t0 = tau * T1
            x_ = x_sb[tau % 2]
            xb_ = xb_sb[tau % 2]
            cs = cs_sb[tau % 2]
            if tau >= 2:
                ctx.wait("sync", "dve", prev[("xdone", tau - 2)])
                ctx.wait("sync", "dve", prev[("csdone", tau - 2)])
                ctx.wait("gp", "pe", prev[("xb_pe", tau - 2)])
                ctx.wait("gp", "act", prev[("xb_act", tau - 2)])
            if ("nmred",) in prev:
                ctx.wait("pe", "dve", prev[("nmred",)])
            xl = split_dma(ctx, "sync", f"p1x{tau % 2}", x_.ap().rearrange("p (c t) -> p c t", c=NCHK),
                           rearr_cpt(xT[:, t0:t0 + T1]))
            cl1 = ctx.dma("sync", f"p1c{tau % 2}", cs[:, 0:T1], costab[:, t0:t0 + T1])
            cl2 = ctx.dma("sync", f"p1c{tau % 2}", cs[:, T1:2 * T1], sintab[:, t0:t0 + T1])
            xbl = split_dma(ctx, "gp", f"p1b{tau % 2}", xb_.ap().rearrange("p (c t) -> p c t", c=NCHK),
                            rearr_cpt(xT[:, t0:t0 + T1]))

            ctx.wait("pe", f"p1b{tau % 2}", xbl)
            for c in range(NCHK):
                ctx.op("pe", lambda e, c=c: e.matmul(
                    ps[SUM][0:1, 0:T1], ones_col_bf[:], xb_[:, c * T1:(c + 1) * T1],
                    start=(c == 0), stop=(c == NCHK - 1)), inc=(c == NCHK - 1))
            ctx.wait("act", f"p1b{tau % 2}", xbl)
            sqmm = {}
            for c in range(NCHK):
                sq = sq_sb[c % 2]
                if c >= 2:
                    ctx.wait("act", "pe", sqmm[c - 2])
                sa = ctx.op("act", lambda e, c=c, sq=sq: e.activation(
                    out=sq[:], in_=xb_[:, c * T1:(c + 1) * T1], func=AF.Square))
                ctx.wait("pe", "act", sa)
                sqmm[c] = ctx.op("pe", lambda e, c=c, sq=sq: e.matmul(
                    ps[SUMSQ][0:1, 0:T1], ones_col_bf[:], sq[:],
                    start=(c == 0), stop=(c == NCHK - 1)))
            stats_pe = ctx.cnt["pe"]
            prev[("xb_pe", tau)] = stats_pe
            prev[("xb_act", tau)] = ctx.cnt["act"]
            if _P1STAGE <= 1:
                prev[("xdone", tau)] = ctx.cnt["dve"]
                prev[("csdone", tau)] = ctx.cnt["dve"]
                return

            ctx.wait("dve", "pe", stats_pe)
            ctx.op("dve", lambda e: e.tensor_scalar(
                out=mrow[:], in0=ps[SUM][0:1, 0:T1], scalar1=1.0 / D, scalar2=0.0,
                op0=ALU.mult, op1=ALU.bypass))
            ctx.op("dve", lambda e: e.tensor_scalar(
                out=vrow[:], in0=ps[SUMSQ][0:1, 0:T1], scalar1=1.0 / D, scalar2=EPS_LN,
                op0=ALU.mult, op1=ALU.add))
            ctx.op("dve", lambda e: e.tensor_tensor(
                out=rrow[:], in0=mrow[:], in1=mrow[:], op=ALU.mult))
            dv = ctx.op("dve", lambda e: e.tensor_tensor(
                out=vrow[:], in0=vrow[:], in1=rrow[:], op=ALU.subtract))
            ctx.wait("act", "dve", dv)
            ctx.op("act", lambda e: e.activation(
                out=rrow[:], in_=vrow[:], func=AF.Ln))
            ar = ctx.op("act", lambda e: e.activation(
                out=rrow[:], in_=rrow[:], func=AF.Exp, scale=-0.5))
            ctx.wait("pe", "act", ar)
            bm = ctx.op("pe", lambda e: e.matmul(ps[MB][:, 0:T1], ones_row_f[:], mrow[:],
                                                 start=True, stop=True))
            br = ctx.op("pe", lambda e: e.matmul(ps[RB][:, 0:T1], ones_row_f[:], rrow[:],
                                                 start=True, stop=True))
            ctx.wait("act", "pe", br)
            ctx.op("act", lambda e: e.activation(out=mb_sb[:], in_=ps[MB][:, 0:T1], func=AF.Copy))
            ab = ctx.op("act", lambda e: e.activation(out=rb_sb[:], in_=ps[RB][:, 0:T1], func=AF.Copy))
            ctx.wait("dve", "act", ab)
            ctx.wait("dve", f"p1x{tau % 2}", xl)
            for c in range(NCHK):
                ctx.op("dve", lambda e, c=c: e.tensor_tensor(
                    out=tmp_f[:], in0=x_[:, c * T1:(c + 1) * T1], in1=mb_sb[:], op=ALU.subtract))
                ctx.op("dve", lambda e, c=c: e.tensor_tensor(
                    out=xn1_sb[:, c * T1:(c + 1) * T1], in0=tmp_f[:], in1=rb_sb[:], op=ALU.mult))
            xn_done = ctx.cnt["dve"]
            prev[("xdone", tau)] = xn_done
            if _P1STAGE <= 2:
                prev[("csdone", tau)] = ctx.cnt["dve"]
                return

            def rope_emit(psrc, dst_even, dst_odd):
                cosA = cs[:, 0:T1]
                sinA = cs[:, T1:2 * T1]
                ctx.op("dve", lambda e: e.tensor_tensor(out=rtm_sb[0][:], in0=psrc[0:64, 0:T1], in1=cosA, op=ALU.mult))
                ctx.op("dve", lambda e: e.tensor_tensor(out=rtm_sb[1][:], in0=psrc[64:128, 0:T1], in1=sinA, op=ALU.mult))
                ctx.op("dve", lambda e: e.tensor_tensor(out=dst_even, in0=rtm_sb[0][:], in1=rtm_sb[1][:], op=ALU.subtract))
                ctx.op("dve", lambda e: e.tensor_tensor(out=rtm_sb[2][:], in0=psrc[0:64, 0:T1], in1=sinA, op=ALU.mult))
                ctx.op("dve", lambda e: e.tensor_tensor(out=rtm_sb[3][:], in0=psrc[64:128, 0:T1], in1=cosA, op=ALU.mult))
                return ctx.op("dve", lambda e: e.tensor_tensor(out=dst_odd, in0=rtm_sb[2][:], in1=rtm_sb[3][:], op=ALU.add))

            ctx.wait("pe", "dve", xn_done)
            if tau == 0:
                ctx.wait("pe", "p1w", wld)
            kmm = {}
            for kv in range(NKV):
                for c in range(NCHK):
                    ctx.op("pe", lambda e, kv=kv, c=c: e.matmul(
                        ps[K0 + kv][:, 0:T1], wk_sl[kv][:, c * 128:(c + 1) * 128],
                        xn1_sb[:, c * T1:(c + 1) * T1],
                        start=(c == 0), stop=(c == NCHK - 1)), inc=(c == NCHK - 1))
                kmm[kv] = ctx.cnt["pe"]
            ctx.wait("dve", f"p1c{tau % 2}", cl2)
            last_nm_red = prev.get(("nmred",), None)
            _sub = _os2.environ.get("KERN_P1SUB", "all")
            for kv in range(NKV):
                ctx.wait("dve", "pe", kmm[kv])
                if _sub in ("all", "rope"):
                    rp = rope_emit(ps[K0 + kv], k_sb[kv][0:64, t0:t0 + T1], k_sb[kv][64:128, t0:t0 + T1])
                else:
                    rp = ctx.op("dve", lambda e, kv=kv: e.tensor_copy(
                        out=k_sb[kv][:, t0:t0 + T1], in_=ps[K0 + kv][:, 0:T1]))
                if _sub in ("all", "norm", "normsq", "normmm"):
                    ctx.wait("act", "dve", rp)
                    sa = ctx.op("act", lambda e, kv=kv, t0=t0: e.activation(
                        out=nsq_sb[:], in_=k_sb[kv][:, t0:t0 + T1], func=AF.Square))
                    if _sub != "normsq":
                        ctx.wait("pe", "act", sa)
                        if last_nm_red is not None:
                            ctx.wait("pe", "dve", last_nm_red)
                        nm = ctx.op("pe", lambda e: e.matmul(ps[SUM][0:1, 0:T1], ones_col_bf[:], nsq_sb[:],
                                                             start=True, stop=True))
                        if _sub != "normmm":
                            ctx.wait("dve", "pe", nm)
                            idx = 128 + knorm_i[0]
                            knorm_i[0] += 1
                            last_nm_red = ctx.op("dve", lambda e, idx=idx: e.reduce_max(
                                out=maxbuf[0:1, idx:idx + 1], in_=ps[SUM][0:1, 0:T1], axis=AXX))
                        ctx.wait("act", "pe", nm)
            prev[("nmred",)] = last_nm_red
            if _P1STAGE <= 3:
                prev[("csdone", tau)] = ctx.cnt["dve"]
                return

            for ts in range(T1 // 128):
                vp = ps[MB + ts]
                for c in range(NCHK):
                    ctx.op("pe", lambda e, ts=ts, c=c, vp=vp: e.matmul(
                        vp[:, 0:NKV * DH],
                        xn1_sb[:, c * T1 + ts * 128: c * T1 + ts * 128 + 128],
                        wv_sl[:, c * NKV * DH:(c + 1) * NKV * DH],
                        start=(c == 0), stop=(c == NCHK - 1)), inc=(c == NCHK - 1))
                vm = ctx.cnt["pe"]
                ctx.wait("act", "pe", vm)
                ctx.op("act", lambda e, ts=ts, vp=vp: e.activation(
                    out=v_sb[(t0 + ts * 128) // 128][:], in_=vp[:, 0:NKV * DH], func=AF.Copy))

            if _P1STAGE <= 4:
                prev[("csdone", tau)] = ctx.cnt["dve"]
                return
            for f in range(NH):
                qi = tau * NH + f
                qp = ps[QA + (f % 2)]
                if qi >= 2:
                    ctx.wait("pe", "dve", prev[("qrope", qi - 2)])
                    ctx.wait("pe", "act", prev[("qsq", qi - 2)])
                for c in range(NCHK):
                    ctx.op("pe", lambda e, f=f, c=c, qp=qp: e.matmul(
                        qp[:, 0:T1], wq_sl[f][:, c * 128:(c + 1) * 128],
                        xn1_sb[:, c * T1:(c + 1) * T1],
                        start=(c == 0), stop=(c == NCHK - 1)), inc=(c == NCHK - 1))
                qmm = ctx.cnt["pe"]
                qd = qrot_sb[f % 2]
                if qi >= 2:
                    ctx.wait("dve", f"p1q{qi % 2}", prev[("qst", qi - 2)])
                ctx.wait("dve", "pe", qmm)
                prev[("qrope", qi)] = rope_emit(qp, qd[0:64, :], qd[64:128, :])
                ctx.wait("act", "dve", prev[("qrope", qi)])
                sa = ctx.op("act", lambda e, qd=qd: e.activation(
                    out=nsq_sb[:], in_=qd[:], func=AF.Square))
                prev[("qsq", qi)] = sa
                ctx.wait("pe", "act", sa)
                ctx.wait("pe", "dve", last_nm_red)
                nm = ctx.op("pe", lambda e: e.matmul(ps[SUM][0:1, 0:T1], ones_col_bf[:], nsq_sb[:],
                                                     start=True, stop=True))
                ctx.wait("dve", "pe", nm)
                last_nm_red = ctx.op("dve", lambda e, qi=qi: e.reduce_max(
                    out=maxbuf[0:1, qi:qi + 1], in_=ps[SUM][0:1, 0:T1], axis=AXX))
                ctx.wait("act", "pe", nm)
                ctx.wait("gp", "dve", ctx.cnt["dve"])
                prev[("qst", qi)] = ctx.dma("gp", f"p1q{qi % 2}", q_sc[f, :, t0:t0 + T1], qd[:])
            prev[("nmred",)] = last_nm_red
            prev[("csdone", tau)] = ctx.cnt["dve"]

        for tau in range(NT1):
            tile_p1(tau)
            pump(gamma_gen, 12)
        while pump(gamma_gen, 50):
            pass

        ctx.op("dve", lambda e: e.reduce_max(out=maxbuf[0:1, 190:191], in_=maxbuf[0:1, 0:128], axis=AXX))
        ctx.op("dve", lambda e: e.reduce_max(out=maxbuf[0:1, 191:192], in_=maxbuf[0:1, 128:160], axis=AXX))
        mm2 = ctx.op("dve", lambda e: e.tensor_tensor(out=maxbuf[0:1, 189:190], in0=maxbuf[0:1, 190:191],
                                                      in1=maxbuf[0:1, 191:192], op=ALU.mult))
        ctx.wait("act", "dve", mm2)
        sq2 = ctx.op("act", lambda e: e.activation(out=maxbuf[0:1, 188:189], in_=maxbuf[0:1, 189:190],
                                                   func=AF.Sqrt))
        ctx.wait("dve", "act", sq2)
        ng = ctx.op("dve", lambda e: e.tensor_scalar(
            out=maxbuf[0:1, 187:188], in0=maxbuf[0:1, 188:189],
            scalar1=-BOUND_INFLATE * LAM, scalar2=0.0, op0=ALU.mult, op1=ALU.bypass))
        ctx.wait("pe", "dve", ng)
        bm2 = ctx.op("pe", lambda e: e.matmul(ps[MB][:, 0:1], ones_row_f[:], maxbuf[0:1, 187:188],
                                              start=True, stop=True))
        ctx.wait("act", "pe", bm2)
        ctx.op("act", lambda e: e.activation(out=negm_sb[:], in_=ps[MB][:, 0:1], func=AF.Copy))

        if debug:
            ctx.wait("gp", "act", ctx.cnt["act"])
            ctx.wait("gp", "dve", ctx.cnt["dve"])
            ctx.wait("gp", "p1q0", ctx.cnt.get("p1q0", 0))
            ctx.wait("gp", "p1q1", ctx.cnt.get("p1q1", 0))
            ctx.dma("gp", "dbg", dbg["thr_dbg"], thr_sb[:, 0:3])
            ctx.dma("gp", "dbg", dbg["negm_dbg"], negm_sb[:])
            for kv in range(NKV):
                ctx.dma("gp", "dbg", dbg["k_dbg"][kv * 128:(kv + 1) * 128, :], k_sb[kv][:])
            for i in range(S // 128):
                ctx.dma("gp", "dbg", dbg["v_dbg"][i], v_sb[i][:])
            for f in range(NH):
                ctx.dma("gp", "dbg", dbg["q_dbg"][f], q_sc[f])
        ctx.barrier()
        ctx.emit_block("P1")
    if upto < 3:
        kvs.close(); p0s.close(); ctx.es.close()
        return nc

    with ExitStack() as p3s:
        q_tl = [p3s.enter_context(nc.sbuf_tensor(f"q_tl{i}", [128, 512], BF16)) for i in range(2)]
        p_tl = [p3s.enter_context(nc.sbuf_tensor(f"p_tl{i}", [128, 1024], BF16)) for i in range(2)]
        inv_row = p3s.enter_context(nc.sbuf_tensor("inv_row", [1, 512], F32))
        invb_sb = [p3s.enter_context(nc.sbuf_tensor(f"invb_sb{i}", [128, 512], F32)) for i in range(2)]
        ao_sb = [p3s.enter_context(nc.sbuf_tensor(f"ao_sb{i}", [128, 512], BF16)) for i in range(2)]
        sps2 = [p3s.enter_context(nc.psum_tensor(f"sps2_{i}", [128, 1024], F32)) for i in range(2)]
        ops_ = [p3s.enter_context(nc.psum_tensor(f"ops{i}", [128, 512], F32)) for i in range(2)]
        rsib = [p3s.enter_context(nc.psum_tensor(f"rsib{i}", [128, 512], F32)) for i in range(2)]

        pm = {}
        ui = 0
        for h in range(NH):
            kv = h // 4
            for j in range(8):
                qt = q_tl[ui % 2]
                if ui >= 2:
                    ctx.wait("sync", "pe", pm[("qtile_pe", ui - 2)])
                qld = ctx.dma("sync", f"p3q{ui % 2}", qt[:], q_sc[h, :, j * 512:(j + 1) * 512])
                op_ = ops_[ui % 2]
                rp_ = rsib[ui % 2]
                if ui >= 2:
                    ctx.wait("pe", "dve", pm[("oev", ui - 2)])
                items = [("pair", pi) for pi in range(2 * j)] + [("diag", di) for di in range(4)]
                nit = len(items)
                exp_done = {}
                pend = None

                def consume(p):
                    ctx.wait("pe", p[0], p[1])
                    p[2]()
                    return ctx.cnt["pe"]

                for ci, (kind, idx) in enumerate(items):
                    sp = sps2[ci % 2]
                    if ci == 0:
                        ctx.wait("pe", f"p3q{ui % 2}", qld)
                    if ci >= 2:
                        ctx.wait("pe", "act", exp_done[ci - 2])
                    first = (ci == 0)
                    lastit = (ci == nit - 1)
                    pt = p_tl[ci % 2]
                    if kind == "pair":
                        k0, k1 = 2 * idx, 2 * idx + 1
                        ctx.op("pe", lambda e, sp=sp, kv=kv, k0=k0, qt=qt: e.matmul(
                            sp[:, 0:512], k_sb[kv][:, k0 * 128:(k0 + 1) * 128], qt[:],
                            start=True, stop=True), inc=False)
                        ctx.op("pe", lambda e, sp=sp, kv=kv, k1=k1, qt=qt: e.matmul(
                            sp[:, 512:1024], k_sb[kv][:, k1 * 128:(k1 + 1) * 128], qt[:],
                            start=True, stop=True), inc=True)
                        smm = ctx.cnt["pe"]
                        if ci >= 2:
                            ctx.wait("act", "pe", pm[("pcons", ui, ci - 2)])
                        ctx.wait("act", "pe", smm)
                        ed = ctx.op("act", lambda e, sp=sp, pt=pt: e.activation(
                            out=pt[:], in_=sp[:], func=AF.Exp, bias=negm_sb[:], scale=LAM),
                            serial=False)
                        exp_done[ci] = ed

                        def emit_cons(pt=pt, k0=k0, k1=k1, first=first, lastit=lastit,
                                      rp_=rp_, op_=op_, h=h):
                            ctx.op("pe", lambda e: e.matmul(
                                rp_[0:1, 0:512], ones_col_bf[:], pt[:, 0:512],
                                start=first, stop=False), inc=False)
                            ctx.op("pe", lambda e: e.matmul(
                                rp_[0:1, 0:512], ones_col_bf[:], pt[:, 512:1024],
                                start=False, stop=False), inc=False)
                            ctx.op("pe", lambda e: e.matmul(
                                op_[:, 0:512], v_sb[k0][:, (h // 4) * DH:(h // 4 + 1) * DH], pt[:, 0:512],
                                start=first, stop=False), inc=False)
                            ctx.op("pe", lambda e: e.matmul(
                                op_[:, 0:512], v_sb[k1][:, (h // 4) * DH:(h // 4 + 1) * DH], pt[:, 512:1024],
                                start=False, stop=lastit), inc=True)
                        ready = ("act", ed)
                    else:
                        di = idx
                        kc = 4 * j + di
                        qof, N = 128 * di, 512 - 128 * di
                        ctx.op("pe", lambda e, sp=sp, kv=kv, kc=kc, qt=qt, qof=qof, N=N: e.matmul(
                            sp[:, 0:N], k_sb[kv][:, kc * 128:(kc + 1) * 128], qt[:, qof:qof + N],
                            start=True, stop=True), inc=True)
                        smm = ctx.cnt["pe"]
                        if ci >= 2:
                            ctx.wait("act", "pe", pm[("pcons", ui, ci - 2)])
                        ctx.wait("act", "pe", smm)
                        ed = ctx.op("act", lambda e, sp=sp, pt=pt, N=N: e.activation(
                            out=pt[:, 0:N], in_=sp[:, 0:N], func=AF.Exp, bias=negm_sb[:], scale=LAM),
                            serial=False)
                        exp_done[ci] = ed
                        ctx.wait("dve", "act", ed)
                        md = ctx.op("dve", lambda e, pt=pt: e.tensor_tensor(
                            out=pt[:, 0:128], in0=pt[:, 0:128], in1=tri_sb[:], op=ALU.mult))

                        def emit_cons(pt=pt, kc=kc, qof=qof, N=N, first=first, lastit=lastit,
                                      rp_=rp_, op_=op_, h=h):
                            ctx.op("pe", lambda e: e.matmul(
                                rp_[0:1, qof:qof + N], ones_col_bf[:], pt[:, 0:N],
                                start=first, stop=lastit), inc=False)
                            ctx.op("pe", lambda e: e.matmul(
                                op_[:, qof:qof + N], v_sb[kc][:, (h // 4) * DH:(h // 4 + 1) * DH], pt[:, 0:N],
                                start=first, stop=lastit), inc=True)
                        ready = ("dve", md)
                    if pend is not None:
                        pm[("pcons", ui, ci - 1)] = consume(pend)
                    pend = (ready[0], ready[1], emit_cons)
                pm[("pcons", ui, nit - 1)] = consume(pend)
                last_score = ctx.cnt["pe"]
                pm[("qtile_pe", ui)] = last_score
                ctx.wait("dve", "pe", last_score)
                rc = ctx.op("dve", lambda e, rp_=rp_: e.reciprocal(out=inv_row[:], in_=rp_[0:1, 0:512]))
                ctx.wait("pe", "dve", rc)
                bc = ctx.op("pe", lambda e, rp_=rp_: e.matmul(rp_[:, 0:512], ones_row_f[:], inv_row[:],
                                                             start=True, stop=True), inc=True)
                ctx.wait("dve", "pe", bc)
                iv = invb_sb[ui % 2]
                ctx.op("dve", lambda e, rp_=rp_, iv=iv: e.tensor_copy(out=iv[:], in_=rp_[:, 0:512]))
                pm[("ibcons", ui)] = ctx.cnt["dve"]
                ao = ao_sb[ui % 2]
                if ui >= 2:
                    ctx.wait("dve", f"p3s{ui % 2}", pm[("aost", ui - 2)])
                ev = ctx.op("dve", lambda e, op_=op_, iv=iv, ao=ao: e.tensor_tensor(
                    out=ao[:], in0=op_[:, 0:512], in1=iv[:], op=ALU.mult))
                pm[("oev", ui)] = ev
                pm[("rscons", ui)] = pm[("ibcons", ui)]
                if ui >= 1:
                    pass
                ctx.wait("gp", "dve", ev)
                pm[("aost", ui)] = ctx.dma("gp", f"p3s{ui % 2}", cin[h * DH:(h + 1) * DH, j * 512:(j + 1) * 512], ao[:])
                ui += 1
                pump(quant_gen, 4)

        while pump(quant_gen, 50):
            pass
        cc = ctx.sem["cc"]
        ctx.wait("gp", "p3s0", ctx.cnt.get("p3s0", 0))
        ctx.wait("gp", "p3s1", ctx.cnt.get("p3s1", 0))
        ctx.cnt["cc"] += 1
        cctgt = ctx.cnt["cc"]
        ctx.E("gp", lambda e: e.collective_compute(
            "AllGather", ALU.bypass, ins=[cin.opt()], outs=[cout.opt()],
            replica_groups=[[0, 1], [2, 3], [4, 5], [6, 7]]).then_inc(cc, 1))
        ctx.wait("gp", "cc", cctgt)
        if debug:
            ctx.dma("gp", "dbg", dbg["ao_dbg"], cout)
        ctx.barrier()
        ctx.emit_block("P3")
    kvs.close()
    if upto < 4:
        p0s.close(); ctx.es.close()
        return nc

    ao_my = nc.dram_tensor("ao_my", [2 * NH * DH, HALF], BF16, kind="Internal").ap()

    ctx._mksem("p4g")

    def _gather_my_half(e):
        pid = nc.partition_id([mybir.EngineType.Pool])
        off = (pid % 2) * HALF
        h = ctx.sem["p4g"]
        for i in range(4):
            e.dma_start(out=ao_my[i * 512:(i + 1) * 512, :],
                        in_=cout[i * 512:(i + 1) * 512, bass.ds(off, HALF)]).then_inc(h, 16)
    ctx.cnt["p4g"] += 4 * 16
    aomyld = ctx.cnt["p4g"]
    ctx.E("gp", _gather_my_half)
    ctx.wait("gp", "p4g", aomyld)

    with ExitStack() as p4s:
        wo_sl = [p4s.enter_context(nc.sbuf_tensor(f"wo_sl{i}", [128, NCHK * 128], BF16)) for i in range(NCHK)]
        ao4_sb = p4s.enter_context(nc.sbuf_tensor("ao4_sb", [128, 16 * 512], BF16))
        x4_sb = p4s.enter_context(nc.sbuf_tensor("x4_sb", [128, 16 * 512], F32))
        y4_sb = p4s.enter_context(nc.sbuf_tensor("y4_sb", [128, 16 * 512], F32))
        yb_t = p4s.enter_context(nc.sbuf_tensor("yb_t", [128, 512], BF16))
        sq4_t = p4s.enter_context(nc.sbuf_tensor("sq4_t", [128, 512], BF16))
        m4 = p4s.enter_context(nc.sbuf_tensor("m4", [1, 512], F32))
        v4 = p4s.enter_context(nc.sbuf_tensor("v4", [1, 512], F32))
        r4 = p4s.enter_context(nc.sbuf_tensor("r4", [1, 512], F32))
        mb4 = p4s.enter_context(nc.sbuf_tensor("mb4", [128, 512], F32))
        rb4 = p4s.enter_context(nc.sbuf_tensor("rb4", [128, 512], F32))
        t4 = p4s.enter_context(nc.sbuf_tensor("t4", [128, 512], F32))
        xn2_t = p4s.enter_context(nc.sbuf_tensor("xn2_t", [128, 512], BF16))
        yps = [p4s.enter_context(nc.psum_tensor(f"yps{i}", [128, 512], F32)) for i in range(2)]
        s4ps = p4s.enter_context(nc.psum_tensor("s4ps", [1, 512], F32))
        q4ps = p4s.enter_context(nc.psum_tensor("q4ps", [1, 512], F32))
        b4ps = [p4s.enter_context(nc.psum_tensor(f"b4ps{i}", [128, 512], F32)) for i in range(2)]

        for do in range(NCHK):
            wol = split_dma(ctx, "gp", "p4w", wo_sl[do].ap().rearrange("p (c f) -> p c f", c=NCHK),
                            rearr_cpt(woT[:, do * 128:(do + 1) * 128]))

        for b in range(4):
            bc0 = b * 512
            if b >= 1:
                ctx.wait("gp", "pe", pm.get(("aocons", b - 1), 0))
                ctx.wait("sync", "dve", pm.get(("xcons", b - 1), 0))
            aol = None
            for f in range(16):
                aol = ctx.dma("gp", "p4a", ao4_sb[:, f * 512:(f + 1) * 512],
                              ao_my[f * 128:(f + 1) * 128, bc0:bc0 + 512])
            xl4 = split_dma(ctx, "sync", "p4x", x4_sb.ap().rearrange("p (c t) -> p c t", c=NCHK),
                            rearr_cpt(xTown[:, bc0:bc0 + 512]))
            ctx.wait("pe", "p4a", aol)
            if b == 0:
                ctx.wait("pe", "p4w", wol)
            ctx.wait("dve", "p4x", xl4)
            if b >= 1:
                ctx.wait("dve", "p4y", ctx.cnt.get("p4y", 0))
            for do in range(NCHK):
                yp = yps[do % 2]
                if do >= 2:
                    ctx.wait("pe", "dve", pm[("yev", b, do - 2)])
                for f in range(16):
                    ctx.op("pe", lambda e, yp=yp, do=do, f=f: e.matmul(
                        yp[:, 0:512], wo_sl[do][:, f * 128:(f + 1) * 128],
                        ao4_sb[:, f * 512:(f + 1) * 512],
                        start=(f == 0), stop=(f == 15)), inc=(f == 15))
                ym = ctx.cnt["pe"]
                ctx.wait("dve", "pe", ym)
                ev = ctx.op("dve", lambda e, yp=yp, do=do: e.tensor_tensor(
                    out=y4_sb[:, do * 512:(do + 1) * 512], in0=yp[:, 0:512],
                    in1=x4_sb[:, do * 512:(do + 1) * 512], op=ALU.add))
                pm[("yev", b, do)] = ev
                ctx.wait("gp", "dve", ev)
                ctx.dma("gp", "p4y", y_sc[do, :, bc0:bc0 + 512], y4_sb[:, do * 512:(do + 1) * 512])
                ctx.wait("act", "dve", ev)
                if do >= 1:
                    ctx.wait("act", "pe", pm[("stmm", b, do - 1)])
                ya = ctx.op("act", lambda e, do=do: e.activation(
                    out=yb_t[:], in_=y4_sb[:, do * 512:(do + 1) * 512], func=AF.Copy))
                sa = ctx.op("act", lambda e, do=do: e.activation(
                    out=sq4_t[:], in_=y4_sb[:, do * 512:(do + 1) * 512], func=AF.Square))
                ctx.wait("pe", "act", sa)
                ctx.op("pe", lambda e, do=do: e.matmul(
                    s4ps[0:1, 0:512], ones_col_bf[:], yb_t[:],
                    start=(do == 0), stop=(do == 15)), inc=False)
                ctx.op("pe", lambda e, do=do: e.matmul(
                    q4ps[0:1, 0:512], ones_col_bf[:], sq4_t[:],
                    start=(do == 0), stop=(do == 15)), inc=True)
                pm[("stmm", b, do)] = ctx.cnt["pe"]
            pm[("aocons", b)] = ctx.cnt["pe"]
            st4 = pm[("stmm", b, 15)]
            ctx.wait("dve", "pe", st4)
            ctx.op("dve", lambda e: e.tensor_scalar(out=m4[:], in0=s4ps[0:1, 0:512],
                                                    scalar1=1.0 / D, scalar2=0.0, op0=ALU.mult, op1=ALU.bypass))
            ctx.op("dve", lambda e: e.tensor_scalar(out=v4[:], in0=q4ps[0:1, 0:512],
                                                    scalar1=1.0 / D, scalar2=EPS_LN, op0=ALU.mult, op1=ALU.add))
            ctx.op("dve", lambda e: e.tensor_tensor(out=r4[:], in0=m4[:], in1=m4[:], op=ALU.mult))
            dv4 = ctx.op("dve", lambda e: e.tensor_tensor(out=v4[:], in0=v4[:], in1=r4[:], op=ALU.subtract))
            ctx.wait("act", "dve", dv4)
            ctx.op("act", lambda e: e.activation(out=r4[:], in_=v4[:], func=AF.Ln))
            ar4 = ctx.op("act", lambda e: e.activation(out=r4[:], in_=r4[:], func=AF.Exp, scale=-0.5))
            ctx.wait("pe", "act", ar4)
            bm4 = ctx.op("pe", lambda e: e.matmul(b4ps[0][:, 0:512], ones_row_f[:], m4[:], start=True, stop=True))
            br4 = ctx.op("pe", lambda e: e.matmul(b4ps[1][:, 0:512], ones_row_f[:], r4[:], start=True, stop=True))
            ctx.wait("act", "pe", br4)
            ctx.op("act", lambda e: e.activation(out=mb4[:], in_=b4ps[0][:, 0:512], func=AF.Copy))
            ab4 = ctx.op("act", lambda e: e.activation(out=rb4[:], in_=b4ps[1][:, 0:512], func=AF.Copy))
            ctx.wait("dve", "act", ab4)
            for do in range(NCHK):
                ctx.op("dve", lambda e, do=do: e.tensor_tensor(
                    out=t4[:], in0=y4_sb[:, do * 512:(do + 1) * 512], in1=mb4[:], op=ALU.subtract))
                xd = ctx.op("dve", lambda e, do=do: e.tensor_tensor(
                    out=xn2_t[:], in0=t4[:], in1=rb4[:], op=ALU.mult))
                ctx.wait("gp", "dve", xd)
                ctx.dma("gp", "p4n", xn2_sc[do, :, bc0:bc0 + 512], xn2_t[:])
                ctx.wait("dve", "p4n", ctx.cnt["p4n"])
            pm[("xcons", b)] = ctx.cnt["dve"]
        if debug:
            ctx.wait("gp", "dve", ctx.cnt["dve"])
            ctx.wait("gp", "p4y", ctx.cnt.get("p4y", 0))
            ctx.wait("gp", "p4n", ctx.cnt.get("p4n", 0))
            for doo in range(NCHK):
                ctx.dma("gp", "dbg", dbg["y_dbg"][doo], y_sc[doo])
                ctx.dma("gp", "dbg", dbg["xn2_dbg"][doo], xn2_sc[doo])
        ctx.barrier()
        ctx.emit_block("P4")
    p0s.close()
    if upto < 5:
        ctx.es.close()
        return nc

    with ExitStack() as p5s:
        xn5_sb = p5s.enter_context(nc.sbuf_tensor("xn5_sb", [128, NCHK * 1024], BF16))
        hid_sb = p5s.enter_context(nc.sbuf_tensor("hid_sb", [128, NHC_FF * 1024], BF16))
        w15 = [p5s.enter_context(nc.sbuf_tensor(f"w15_{i}", [128, NCHK * 128], BF16)) for i in range(2)]
        w25 = [p5s.enter_context(nc.sbuf_tensor(f"w25_{i}", [128, NCHK * 128], BF16)) for i in range(2)]
        w35 = [p5s.enter_context(nc.sbuf_tensor(f"w35_{i}", [128, NHC_FF * 128], BF16)) for i in range(2)]
        sig_t = [p5s.enter_context(nc.sbuf_tensor(f"sig_t{i}", [128, 512], BF16)) for i in range(2)]
        y5_t = [p5s.enter_context(nc.sbuf_tensor(f"y5_t{i}", [128, 512], F32)) for i in range(2)]
        o5_t = [p5s.enter_context(nc.sbuf_tensor(f"o5_t{i}", [128, 512], F32)) for i in range(2)]
        x1ps = [p5s.enter_context(nc.psum_tensor(f"x1ps{i}", [128, 512], F32)) for i in range(2)]
        x2ps = [p5s.enter_context(nc.psum_tensor(f"x2ps{i}", [128, 512], F32)) for i in range(2)]
        yops = [p5s.enter_context(nc.psum_tensor(f"yops{i}", [128, 512], F32)) for i in range(2)]

        for sti in range(2):
            sc0 = sti * 1024
            if sti >= 1:
                ctx.wait("sync", "pe", pm[("xncons", sti - 1)])
            for c in range(NCHK):
                xnl = ctx.dma("sync", "p5x", xn5_sb[:, c * 1024:(c + 1) * 1024],
                              xn2_sc[c, :, sc0:sc0 + 1024])
            for hc in range(NHC_FF):
                gi = sti * NHC_FF + hc
                w1b, w2b = w15[hc % 2], w25[hc % 2]
                if gi >= 2:
                    ctx.wait("sync", "pe", pm[("wAcons", gi - 2)])
                split_dma(ctx, "sync", f"p5w{hc % 2}", w1b.ap().rearrange("p (c h) -> p c h", c=NCHK),
                          q1_sc[:, :, hc * 128:(hc + 1) * 128].rearrange("c p h -> p c h"))
                wl = split_dma(ctx, "sync", f"p5w{hc % 2}", w2b.ap().rearrange("p (c h) -> p c h", c=NCHK),
                               q2_sc[:, :, hc * 128:(hc + 1) * 128].rearrange("c p h -> p c h"))
                for th in range(2):
                    if gi >= 1:
                        ctx.wait("pe", "act", pm[("sig", gi - 1, th)])
                        ctx.wait("pe", "dve", pm[("hid", gi - 1, th)])
                ctx.wait("pe", f"p5w{hc % 2}", wl)
                if hc == 0:
                    ctx.wait("pe", "p5x", xnl)
                for c in range(NCHK):
                    for th in range(2):
                        ctx.op("pe", lambda e, c=c, th=th, w1b=w1b: e.matmul(
                            x1ps[th][:, 0:512], w1b[:, c * 128:(c + 1) * 128],
                            xn5_sb[:, c * 1024 + th * 512: c * 1024 + th * 512 + 512],
                            start=(c == 0), stop=(c == NCHK - 1)),
                            inc=(c == NCHK - 1 and th == 1))
                x1m = ctx.cnt["pe"]
                for c in range(NCHK):
                    for th in range(2):
                        ctx.op("pe", lambda e, c=c, th=th, w2b=w2b: e.matmul(
                            x2ps[th][:, 0:512], w2b[:, c * 128:(c + 1) * 128],
                            xn5_sb[:, c * 1024 + th * 512: c * 1024 + th * 512 + 512],
                            start=(c == 0), stop=(c == NCHK - 1)),
                            inc=(c == NCHK - 1 and th == 1))
                x2m = ctx.cnt["pe"]
                for th in range(2):
                    sg = sig_t[th]
                    ctx.wait("act", "pe", x1m)
                    if gi >= 1:
                        ctx.wait("act", "dve", pm[("hid", gi - 1, th)])
                    pm[("sig", gi, th)] = ctx.op("act", lambda e, th=th, sg=sg: e.activation(
                        out=sg[:], in_=x1ps[th][:, 0:512], func=AF.Sigmoid), serial=False)
                    ctx.wait("dve", "act", pm[("sig", gi, th)])
                    ctx.wait("dve", "pe", x2m)
                    pm[("hid", gi, th)] = ctx.op("dve", lambda e, th=th, sg=sg, hc=hc: e.tensor_tensor(
                        out=hid_sb[:, hc * 1024 + th * 512: hc * 1024 + th * 512 + 512],
                        in0=sg[:], in1=x2ps[th][:, 0:512], op=ALU.mult))
                pm[("wAcons", gi)] = ctx.cnt["pe"]
            pm[("xncons", sti)] = ctx.cnt["pe"]
            hid_done = pm[("hid", sti * NHC_FF + NHC_FF - 1, 1)]

            for c in range(NCHK):
                gci = sti * NCHK + c
                w3b = w35[c % 2]
                if gci >= 2:
                    ctx.wait("sync", "pe", pm[("wBcons", gci - 2)])
                w3l = split_dma(ctx, "sync", f"p5v{c % 2}", w3b.ap().rearrange("p (c d) -> p c d", c=NHC_FF),
                                q3_sc[:, :, c * 128:(c + 1) * 128].rearrange("c p d -> p c d"))
                for th in range(2):
                    if gci >= 1:
                        ctx.wait("sync", "dve", pm[("oev5", gci - 1, th)])
                    yl = ctx.dma("sync", f"p5y{th}", y5_t[th][:], y_sc[c, :, sc0 + th * 512: sc0 + th * 512 + 512])
                    pm[("yl5", gci, th)] = yl
                    if gci >= 1:
                        ctx.wait("pe", "dve", pm[("oev5", gci - 1, th)])
                ctx.wait("pe", f"p5v{c % 2}", w3l)
                if c == 0 and sti == sti:
                    ctx.wait("pe", "dve", hid_done)
                for hcb in range(NHC_FF):
                    for th in range(2):
                        ctx.op("pe", lambda e, hcb=hcb, th=th, w3b=w3b: e.matmul(
                            yops[th][:, 0:512], w3b[:, hcb * 128:(hcb + 1) * 128],
                            hid_sb[:, hcb * 1024 + th * 512: hcb * 1024 + th * 512 + 512],
                            start=(hcb == 0), stop=(hcb == NHC_FF - 1)),
                            inc=(hcb == NHC_FF - 1 and th == 1))
                ym5 = ctx.cnt["pe"]
                for th in range(2):
                    ctx.wait("dve", "pe", ym5)
                    ctx.wait("dve", f"p5y{th}", pm[("yl5", gci, th)])
                    if gci >= 1:
                        ctx.wait("dve", f"p5o{th}", pm[("ost5", gci - 1, th)])
                    ev5 = ctx.op("dve", lambda e, th=th: e.tensor_tensor(
                        out=o5_t[th][:], in0=yops[th][:, 0:512], in1=y5_t[th][:], op=ALU.add))
                    pm[("oev5", gci, th)] = ev5
                    ctx.wait("gp", "dve", ev5)
                    pm[("ost5", gci, th)] = ctx.dma(
                        "gp", f"p5o{th}", out[c * 128:(c + 1) * 128, sc0 + th * 512: sc0 + th * 512 + 512],
                        o5_t[th][:])
                pm[("wBcons", gci)] = ctx.cnt["pe"]

        ctx.barrier()
        ctx.emit_block("P5")

    ctx.es.close()
    return nc


_PERM128 = np.concatenate([np.arange(0, 128, 2), np.arange(1, 128, 2)])


def _permute_heads_rows(W):
    n = W.shape[0] // 128
    idx = np.concatenate([h * 128 + _PERM128 for h in range(n)])
    return W[idx]


def _rope_tables():
    freqs = (1.0 / (10000.0 ** (np.arange(0, DH, 2, dtype=np.float32) /
                                np.float32(DH)))).astype(np.float32)
    ang = np.arange(S, dtype=np.float32)[None, :] * freqs[:, None]
    return np.cos(ang).astype(np.float32), np.sin(ang).astype(np.float32)


_NC_CACHE = {}


def kernel(L0_attn_x, L0_attn_Wq, L0_attn_Wk, L0_attn_Wv, L0_attn_Wo,
           L0_ffn_w1, L0_ffn_w2, L0_ffn_w3, _debug=False, _results=None):
    x = np.asarray(L0_attn_x, dtype=np.float32)
    Wq = np.asarray(L0_attn_Wq, dtype=np.float32)
    Wk = np.asarray(L0_attn_Wk, dtype=np.float32)
    Wv = np.asarray(L0_attn_Wv, dtype=np.float32)
    Wo = np.asarray(L0_attn_Wo, dtype=np.float32)
    w1 = np.asarray(L0_ffn_w1, dtype=np.float32)
    w2 = np.asarray(L0_ffn_w2, dtype=np.float32)
    w3 = np.asarray(L0_ffn_w3, dtype=np.float32)

    WqP = _permute_heads_rows(Wq)
    WkP = _permute_heads_rows(Wk)
    cost, sint = _rope_tables()
    trim = np.triu(np.ones((128, 128), dtype=np.float32)).astype(ml_dtypes.bfloat16)
    woT = np.ascontiguousarray(Wo.T)
    w1T = np.ascontiguousarray(w1.T)
    w2T = np.ascontiguousarray(w2.T)
    w3T = np.ascontiguousarray(w3.T)

    in_maps = []
    for c in range(8):
        b, g = c // 2, c % 2
        xTb = np.ascontiguousarray(x[b].T)
        in_maps.append({
            "xT": xTb,
            "xTown": np.ascontiguousarray(xTb[:, g * HALF:(g + 1) * HALF]),
            "wqT": np.ascontiguousarray(WqP[g * NH * DH:(g + 1) * NH * DH, :].T),
            "wkT": np.ascontiguousarray(WkP[g * NKV * DH:(g + 1) * NKV * DH, :].T),
            "wvT": np.ascontiguousarray(Wv[g * NKV * DH:(g + 1) * NKV * DH, :].T),
            "woT": woT,
            "w1T": w1T,
            "w2T": w2T,
            "w3T": w3T,
            "costab": cost,
            "sintab": sint,
            "tri": trim,
        })

    import os
    upto = int(os.environ.get("KERN_UPTO", "5"))
    key = ("prog", bool(_debug), upto)
    if key not in _NC_CACHE:
        _NC_CACHE[key] = build(debug=_debug, upto=upto)
    nc = _NC_CACHE[key]
    res = run_bass_kernel_spmd(nc, in_maps, core_ids=list(range(8)))
    if _results is not None:
        _results.extend(res.results)
    outp = np.empty((4, S, D), dtype=np.float32)
    for c in range(8):
        b, g = c // 2, c % 2
        outp[b, g * HALF:(g + 1) * HALF, :] = res.results[c]["out"].T
    return outp
